# version 07
# baseline (speedup 1.0000x reference)
import numpy as np
import ml_dtypes
from contextlib import ExitStack

import os
import concourse.bass as bass
import concourse.mybir as mybir
from concourse.bass_utils import run_bass_kernel_spmd

F32 = mybir.dt.float32
BF16 = mybir.dt.bfloat16
U8 = mybir.dt.uint8
AF = mybir.ActivationFunctionType
ALU = mybir.AluOpType
AX = mybir.AxisListType

B, S, D = 4, 4096, 1024
H, HD = 16, 64
T, NT, NBLK, NKT = 2048, 16, 8, 32
BW = 640
DI = 2816
NC22 = DI // 128
VC = float(2 ** 22 + 1)
MG = float(3 * 2 ** 22)
ENGS = ("tensor", "vector", "scalar", "gpsimd", "sync")


def band_kbase(t):
    gq0 = 4 * (t >> 1) + (t & 1)
    return min(max(gq0 * 128 - 128, 0), S - BW)


def build():
    nc = bass.Bass("TRN2", target_bir_lowering=False)

    def inp(name, shape, dt=F32):
        return nc.dram_tensor(name, shape, dt, kind="ExternalInput").ap()

    x0_own = inp("x0_own", [T, D])
    x0_full = inp("x0_full", [S, D])
    w_qkv = inp("w_qkv", [D, 3 * D]); b_qkv = inp("b_qkv", [3 * D])
    w_out = inp("w_out", [D, D]); b_out = inp("b_out", [D])
    cn_w1 = inp("cn_w1", [D, D]); cn_b1 = inp("cn_b1", [D])
    cn_w2 = inp("cn_w2", [D, 512]); cn_b2 = inp("cn_b2", [512])
    cn_w3 = inp("cn_w3", [512, 256]); cn_b3 = inp("cn_b3", [256])
    cn_w4 = inp("cn_w4", [256, 1]); cn_b4 = inp("cn_b4", [1])
    gc_w1 = inp("gc_w1", [D, 512]); gc_b1 = inp("gc_b1", [512])
    gc_w2 = inp("gc_w2", [512, 256]); gc_b2 = inp("gc_b2", [256])
    gc_w3 = inp("gc_w3", [256, 1]); gc_b3 = inp("gc_b3", [1])
    ar_in = inp("ar", [1])
    Wq = inp("Wq", [D, D]); Wk = inp("Wk", [D, D])
    Wv = inp("Wv", [D, D]); Wo = inp("Wo", [D, D])
    wg_sl = inp("wg_sl", [352, D]); wu_sl = inp("wu_sl", [352, D])
    wd_sl = inp("wd_sl", [128, DI])
    cmask = inp("cmask", [NBLK * 4, 128 * 256], BF16)
    bmask = inp("bmask", [NT, 128 * BW], BF16)

    out_t = nc.dram_tensor("out", [T, D], F32, kind="ExternalOutput").ap()
    DBG = int(os.environ.get("DBG", "0"))
    if DBG:
        o_x1 = nc.dram_tensor("o_x1", [128, NT * D], F32, kind="ExternalOutput").ap()
        o_x2 = nc.dram_tensor("o_x2", [128, NT * D], F32, kind="ExternalOutput").ap()
        o_x3 = nc.dram_tensor("o_x3", [128, NT * D], F32, kind="ExternalOutput").ap()
        o_ctx = nc.dram_tensor("o_ctx", [D, T], BF16, kind="ExternalOutput").ap()
        o_p5 = nc.dram_tensor("o_p5", [128, 64], F32, kind="ExternalOutput").ap()
        o_p5b = nc.dram_tensor("o_p5b", [128, 8], BF16, kind="ExternalOutput").ap()
        o_p5c = nc.dram_tensor("o_p5c", [128, 24], BF16, kind="ExternalOutput").ap()

    def dram(name, shape, dt=BF16):
        return nc.dram_tensor(name, shape, dt).ap()

    d_v = dram("d_v", [128, NKT * 1040])
    d_x1 = dram("d_x1", [128, NT * D], F32)
    d_x2 = dram("d_x2", [128, NT * D], F32)
    d_ctx = dram("d_ctx", [D, T])
    d_h = dram("d_h", [128, NC22 * T])
    EXK = 8 * T; EXV = 16 * 1040; EXW = EXK + EXV + 8
    d_exi = dram("d_exi", [128, EXW]); d_exo = dram("d_exo", [256, EXW])
    d_gci = dram("d_gci", [128, 8], F32); d_gco = dram("d_gco", [128, 8], F32)
    d_wgi = dram("d_wgi", [352, D]); d_wgo = dram("d_wgo", [DI, D])
    d_wui = dram("d_wui", [352, D]); d_wuo = dram("d_wuo", [DI, D])
    d_wdi = dram("d_wdi", [128, DI]); d_wdo = dram("d_wdo", [D, DI])

    pairs = [[0, 1], [2, 3], [4, 5], [6, 7]]
    allg = [list(range(8))]

    sems, cnts = {}, {}
    stk = ExitStack()
    with stk:
        def SEM(n):
            if n not in sems:
                sems[n] = stk.enter_context(nc.semaphore(n))
                cnts[n] = 0
            return sems[n]

        def I(n, k=1):
            s = SEM(n)
            cnts[n] += k
            return s, cnts[n]

        def C(n):
            SEM(n)
            return cnts[n]

        sb = lambda n, s, d=F32: stk.enter_context(nc.sbuf_tensor(n, s, d))

        consts = sb("consts", [128, 8])
        ones1b = sb("ones1b", [1, 128], BF16)
        ones1f = sb("ones1f", [1, 128], F32)
        onescb = sb("onescb", [128, 1], BF16)
        bq_sb = sb("bq_sb", [128, 16])
        bv_row = sb("bv_row", [1, D], BF16)
        bo_row = sb("bo_row", [1, D], BF16)
        cnb1 = sb("cnb1", [128, 8]); cnb2 = sb("cnb2", [128, 4]); cnb3 = sb("cnb3", [128, 2])
        gcb1 = sb("gcb1", [128, 4]); gcb2 = sb("gcb2", [128, 2])
        b4r = sb("b4r", [128, 1]); gb3r = sb("gb3r", [128, 1]); arr = sb("arr", [128, 1])
        mbuf = sb("mbuf", [128, NT * H])
        nm8 = sb("nm8", [128, NT * H])
        vtsb = sb("vtsb", [128, NKT * H])
        vsumb = sb("vsumb", [128, H])
        cbuf = sb("cbuf", [128, NT * H])
        tfc = sb("tfc", [128, NT]); cplb = sb("cplb", [128, NT]); ncplb = sb("ncplb", [128, NT])
        gfr = sb("gfr", [128, 1]); a6r = sb("a6r", [128, 1]); bvr = sb("bvr", [128, 1])
        lnst = sb("lnst", [128, 12])
        gsc = sb("gsc", [128, 32])
        scr = sb("scr", [128, 2048])
        scb = sb("scb", [128, 4096], BF16)
        zrow = sb("zrow", [1, D], BF16)
        QT = {}

        def alloc_quant(es, tagn):
            QT["xn"] = es.enter_context(nc.sbuf_tensor("qxn" + tagn, [128, DI], F32))
            QT["A"] = es.enter_context(nc.sbuf_tensor("qA" + tagn, [128, DI], F32))
            QT["B"] = es.enter_context(nc.sbuf_tensor("qB" + tagn, [128, DI], F32))
            QT["am"] = es.enter_context(nc.sbuf_tensor("qam" + tagn, [128, DI // 16], F32))
            QT["sc"] = es.enter_context(nc.sbuf_tensor("qsc" + tagn, [128, DI // 16], F32))
            QT["rc"] = es.enter_context(nc.sbuf_tensor("qrc" + tagn, [128, DI // 16], F32))
            QT["mk"] = es.enter_context(nc.sbuf_tensor("qmk" + tagn, [128, DI], U8))

        qs = SEM("qs"); qg = SEM("qg"); BAR = SEM("bar")
        nbar = [0]

        class Ph:
            def __init__(self):
                self.ops = {e: [] for e in ENGS}

            def add(self, e, fn):
                self.ops[e].append(fn)

            def emit(self):
                base = nbar[0] * len(ENGS)
                qsv, qgv = cnts["qs"], cnts["qg"]
                with nc.Block() as blk:
                    for ename in ENGS:
                        def mk(ename=ename):
                            def body(eng):
                                eng.wait_ge(BAR, base)
                                for fn in self.ops[ename]:
                                    fn(eng)
                                if ename == "sync":
                                    eng.wait_ge(qs, qsv)
                                if ename == "gpsimd":
                                    eng.wait_ge(qg, qgv)
                                eng.drain().then_inc(BAR, 1)
                            return body
                        getattr(blk, ename)(mk())
                cnts["bar"] += len(ENGS)
                nbar[0] += 1

        KPH = int(os.environ.get("KPH", "99"))
        SEC2 = int(os.environ.get("SEC2", "9"))
        SKIPW = int(os.environ.get("SKIPW", "0"))
        KMO = int(os.environ.get("KMO", "8"))
        KEV = int(os.environ.get("KEV", "1"))
        phct = [0]
        es_list = []

        def ES():
            e = ExitStack()
            es_list.append(e)
            return e

        class StopBuild(Exception):
            pass

        def CKPT():
            phct[0] += 1
            if phct[0] >= KPH:
                raise StopBuild()

        def sched_quant(ph, src_fn, dst_fn, n, rows, gate, tag, pre_wait=None):
            r = slice(0, rows)
            nb = n // 16
            s1, t1 = I(tag + "1")

            def fv1(v):
                src = src_fn()
                qxn, qA, qB = QT["xn"], QT["A"], QT["B"]
                qam, qsc, qrc = QT["am"], QT["sc"], QT["rc"]
                if gate is not None:
                    v.wait_ge(*gate)
                v.tensor_reduce(qam[r, 0:nb], src.rearrange("p (g e) -> p g e", e=16),
                                AX.X, ALU.max, apply_absolute_value=True)
                v.drain()
                v.tensor_scalar(qsc[r, 0:nb], qam[r, 0:nb], 1.0 / 6.0, 1e-12, ALU.mult, ALU.max)
                v.reciprocal(qrc[r, 0:nb], qsc[r, 0:nb])
                v.tensor_tensor(qxn.ap()[r, 0:n].rearrange("p (g e) -> p g e", e=16),
                                src.rearrange("p (g e) -> p g e", e=16),
                                qrc.ap()[r, 0:nb].broadcast_to([rows, nb, 16]),
                                ALU.mult)
                v.scalar_tensor_tensor(qB[r, 0:n], qxn[r, 0:n], VC, qxn[r, 0:n], ALU.mult, ALU.subtract)
                v.scalar_tensor_tensor(qA[r, 0:n], qxn[r, 0:n], VC, qB[r, 0:n],
                                       ALU.mult, ALU.subtract).then_inc(s1, 1)
            ph.add("vector", fv1)
            s2, t2 = I(tag + "2")

            def fs(s_):
                qxn, qB = QT["xn"], QT["B"]
                s_.wait_ge(s1, t1)
                s_.activation(qB[r, 0:n], qxn[r, 0:n], AF.Copy, bias=MG, scale=2.0)
                s_.activation(qB[r, 0:n], qB[r, 0:n], AF.Copy, bias=-MG / 2.0,
                              scale=0.5).then_inc(s2, 1)
            ph.add("scalar", fs)
            s3, t3 = I(tag + "3")

            def fv2(v):
                dst = dst_fn()
                qxn, qA, qB, qsc, msku = QT["xn"], QT["A"], QT["B"], QT["sc"], QT["mk"]
                v.wait_ge(s2, t2)
                if pre_wait is not None:
                    v.wait_ge(*pre_wait)
                v.scalar_tensor_tensor(qxn[r, 0:n], qA[r, 0:n], 1.0, qA[r, 0:n], ALU.mult, ALU.mult)
                v.tensor_single_scalar(msku[r, 0:n], qxn[r, 0:n], 1.0, ALU.is_ge)
                v.select(qxn[r, 0:n], msku[r, 0:n], qA[r, 0:n], qB[r, 0:n])
                v.tensor_tensor(dst.rearrange("p (g e) -> p g e", e=16),
                                qxn.ap()[r, 0:n].rearrange("p (g e) -> p g e", e=16),
                                qsc.ap()[r, 0:nb].broadcast_to([rows, nb, 16]),
                                ALU.mult).then_inc(s3, 1)
            ph.add("vector", fv2)
            return (s3, t3), (s1, t1)

        def sched_ln(ph, psum_fn, resid_fn, out32, outbf, spill, tag, extra_wait=None,
                     outbf_wait=None):
            sa, ta = I(tag + "a")

            def fva(v):
                ps = psum_fn(); res = resid_fn()
                if extra_wait is not None:
                    v.wait_ge(*extra_wait())
                v.scalar_tensor_tensor(out32, ps, 1.0, res, ALU.mult, ALU.add,
                                       accum_out=lnst[:, 0:1])
                v.scalar_tensor_tensor(scr[:, 0:1024], out32, 1.0, out32,
                                       ALU.mult, ALU.mult, accum_out=lnst[:, 1:2])
                v.tensor_scalar_mul(lnst[:, 2:3], lnst[:, 0:1], -1.0 / 1024.0)
                v.tensor_scalar_mul(lnst[:, 3:4], lnst[:, 1:2], 1.0 / 1024.0)
                v.scalar_tensor_tensor(lnst[:, 3:4], lnst[:, 2:3], lnst[:, 2:3],
                                       lnst[:, 3:4], ALU.mult, ALU.subtract).then_inc(sa, 1)
            ph.add("vector", fva)
            sbv, tb = I(tag + "b")

            def fsc(s_):
                s_.wait_ge(sa, ta)
                s_.activation(lnst[:, 5:6], lnst[:, 3:4], AF.Sqrt,
                              bias=consts[:, 1:2], scale=-1.0).then_inc(sbv, 1)
            ph.add("scalar", fsc)
            sc_, tc = I(tag + "c")

            def fvc(v):
                v.wait_ge(sbv, tb)
                if outbf_wait is not None and outbf_wait[1]:
                    v.wait_ge(*outbf_wait)
                v.reciprocal(lnst[:, 6:7], lnst[:, 5:6])
                v.tensor_scalar(out32, out32, lnst[:, 2:3], lnst[:, 6:7], ALU.add, ALU.mult)
                if outbf is not None:
                    v.tensor_copy(outbf, out32).then_inc(sc_, 1)
                else:
                    v.engine_nop().then_inc(sc_, 1)
            ph.add("vector", fvc)
            if spill is not None:
                _, tq = I("qs", 16)

                def fsp(sy):
                    sy.wait_ge(sc_, tc)
                    sy.dma_start(out=spill, in_=out32).then_inc(qs, 16)
                ph.add("sync", fsp)
            return sc_, tc

        try:
            es_p1 = ES()
            wsl = es_p1.enter_context(nc.sbuf_tensor("wsl", [128, DI], F32))
            wsq = es_p1.enter_context(nc.sbuf_tensor("wsq", [128, DI], BF16))
            alloc_quant(es_p1, "w")

            ph = Ph()

            def f_pre(g):
                ncd = nc.allow_non_contiguous_dma(reason="tiny bias/const loads")
                ncd.__enter__()
                g.memset(consts[:, 0:1], -4.0)
                g.memset(consts[:, 1:2], 1e-5)
                g.memset(consts[:, 2:3], -1.0)
                g.memset(consts[:, 3:4], 0.0)
                g.memset(consts[:, 4:5], 1.0)
                g.memset(ones1b[:], 1.0)
                g.memset(ones1f[:], 1.0)
                g.memset(onescb[:], 1.0)
                g.memset(zrow[:], 0.0)
                for src, dst in [(b_qkv[0:1024], bq_sb[:, 0:8]), (b_qkv[1024:2048], bq_sb[:, 8:16]),
                                 (cn_b1, cnb1[:]), (cn_b2, cnb2[:]), (cn_b3, cnb3[:]),
                                 (gc_b1, gcb1[:]), (gc_b2, gcb2[:])]:
                    g.dma_start(out=dst, in_=src.rearrange("(c p) -> p c", p=128)).then_inc(qg, 16)
                g.dma_start(out=bv_row[:], in_=b_qkv[2048:3072].rearrange("(a f) -> a f", a=1)).then_inc(qg, 16)
                g.dma_start(out=bo_row[:], in_=b_out.rearrange("(a f) -> a f", a=1)).then_inc(qg, 16)
                for src, dst in [(cn_b4, b4r), (gc_b3, gb3r), (ar_in, arr)]:
                    g.dma_start(out=dst[:],
                                in_=src.broadcast_to([1, 128]).rearrange("a p -> p a")).then_inc(qg, 16)
                g.drain().then_inc(SEM("pre"), 1)
                ncd.__exit__(None, None, None)
            I("qg", 16 * 12)
            I("pre")
            ph.add("gpsimd", f_pre)

            wslices = [] if SKIPW else [
                       (wg_sl, 0, 128, d_wgi), (wg_sl, 128, 128, d_wgi), (wg_sl, 256, 96, d_wgi),
                       (wu_sl, 0, 128, d_wui), (wu_sl, 128, 128, d_wui), (wu_sl, 256, 96, d_wui),
                       (wd_sl, 0, 128, d_wdi)]
            last_store_tick = 0
            prev_v1 = None
            prev_store = None
            for j, (src, r0, rr, dsti) in enumerate(wslices):
                n = DI if j == 6 else D
                _, tqg = I("qg", 16)

                def fload(g, src=src, r0=r0, rr=rr, n=n, pv=prev_v1):
                    if pv is not None:
                        g.wait_ge(*pv)
                    g.dma_start(out=wsl[0:rr, 0:n], in_=src[r0:r0 + rr, :]).then_inc(qg, 16)
                ph.add("gpsimd", fload)
                pw = (qg, prev_store) if prev_store is not None else None
                (s3t3, s1t1) = sched_quant(ph, (lambda rr=rr, n=n: wsl.ap()[0:rr, 0:n]),
                                           (lambda rr=rr, n=n: wsq.ap()[0:rr, 0:n]),
                                           n, rr, (qg, tqg), "wq", pre_wait=pw)
                prev_v1 = s1t1
                _, last_store_tick = I("qg", 16)
                prev_store = last_store_tick

                def fstore(g, r0=r0, rr=rr, n=n, dsti=dsti, s3t3=s3t3):
                    g.wait_ge(*s3t3)
                    g.dma_start(out=dsti[r0:r0 + rr, :], in_=wsq[0:rr, 0:n]).then_inc(qg, 16)
                ph.add("gpsimd", fstore)

            sccw, _ = I("ccw", 3)

            def fcoll(g, gate=last_store_tick):
                if SKIPW:
                    g.drain().then_inc(sccw, 3)
                    return
                g.wait_ge(qg, gate)
                for di, do in [(d_wgi, d_wgo), (d_wui, d_wuo), (d_wdi, d_wdo)]:
                    g.collective_compute("AllGather", ALU.bypass, replica_groups=allg,
                                         ins=[di], outs=[do]).then_inc(sccw, 1)
            ph.add("gpsimd", fcoll)
            ph.emit(); CKPT()
            es_p1.close()

            es_fm1 = ES()
            fmA = es_fm1.enter_context(nc.sbuf_tensor("fmA", [128, NT * D], BF16))
            es_kfm = ES()
            kfm = es_kfm.enter_context(nc.sbuf_tensor("kfm", [128, 8 * S], BF16))
            es_p2 = ES()
            wqs = es_p2.enter_context(nc.sbuf_tensor("wqs", [128, 8 * 1024], BF16))
            vstg = es_p2.enter_context(nc.sbuf_tensor("vstg", [128, 2 * 1040], BF16))
            es_x0h = ES()
            x0h = es_x0h.enter_context(nc.sbuf_tensor("x0h", [128, 8 * T], BF16, side="right"))
            es_ps2 = ES()
            PSQ = [es_ps2.enter_context(nc.psum_tensor("ps2_%d" % i, [128, 512], F32))
                   for i in range(8)]

            ph = Ph()
            SEM("p2t"); SEM("p2v"); SEM("p2vt"); SEM("p2vv"); SEM("ge")
            _, t_vm = I("ge")

            def fv_init(g):
                g.memset(vstg[:, 0:1040], 1.0)
                g.memset(vstg[:, 1040:2080], 1.0).then_inc(SEM("ge"), 1)
            ph.add("gpsimd", fv_init)

            bank_k = {i: 0 for i in range(8)}
            bank_v = {i: 0 for i in range(8)}
            qs0_v = cnts["qs"]
            nvst = 0
            for sh in range(2):
                prev_reads = cnts["p2vt"]
                qs0_h = cnts["qs"]
                for i in range(NT):
                    qs_gate = qs0_h + 16 * 8 * (i - 3) if i >= 4 else None
                    _, tqg = I("qg", 16)

                    def flh(g, i=i, sh=sh, qs_gate=qs_gate, prev_reads=prev_reads):
                        if i == 0 and prev_reads:
                            g.wait_ge(SEM("p2vt"), prev_reads)
                        if qs_gate is not None:
                            g.wait_ge(qs, qs_gate)
                        g.dma_start(out=scb[:, (i % 4) * D:(i % 4 + 1) * D],
                                    in_=x0_full[sh * T + i * 128: sh * T + (i + 1) * 128, :]
                                    ).then_inc(qg, 16)
                    ph.add("gpsimd", flh)
                    _, _tq = I("qs", 16 * 8)

                    def fth(sy, i=i, gate=tqg, prev_reads=prev_reads):
                        sy.wait_ge(qg, gate)
                        if i == 0 and prev_reads:
                            sy.wait_ge(SEM("p2vt"), prev_reads)
                        for fc in range(8):
                            sy.dma_start(
                                out=x0h[:, fc * T + i * 128: fc * T + (i + 1) * 128],
                                in_=scb[:, (i % 4) * D + fc * 128:(i % 4) * D + (fc + 1) * 128],
                                transpose=True).then_inc(qs, 16)
                    ph.add("sync", fth)
                t_x0h = cnts["qs"]

                if SEC2 < 2:
                    continue
                gate_prev_w = cnts["p2vt"]
                _, gate_wk = I("qg", 16 * 8)

                NOCAST = int(os.environ.get("NOCAST", "0"))
                SERK = int(os.environ.get("SERK", "0"))

                def flk(g, gate_prev_w=gate_prev_w, NOCAST=NOCAST, SERK=SERK, t_x0h=t_x0h):
                    if SERK:
                        g.wait_ge(qs, t_x0h)
                    if gate_prev_w:
                        g.wait_ge(SEM("p2vt"), gate_prev_w)
                    for ki in range(8):
                        if NOCAST:
                            g.dma_start(out=wqs[:, ki * 1024:(ki + 1) * 1024],
                                        in_=d_v[:, ki * 1024:(ki + 1) * 1024]).then_inc(qg, 16)
                        else:
                            g.dma_start(out=wqs[:, ki * 1024:(ki + 1) * 1024],
                                        in_=w_qkv[ki * 128:(ki + 1) * 128, 1024:2048]).then_inc(qg, 16)
                ph.add("gpsimd", flk)

                for mo in range(KMO):
                    grp = mo % 2
                    gates = [bank_k[4 * grp + tc] for tc in range(4)]
                    _, t_mm = I("p2t")

                    def fk_t(t, mo=mo, sh=sh, grp=grp, gates=gates, gate_wk=gate_wk, t_x0h=t_x0h):
                        if mo == 0:
                            t.wait_ge(qg, gate_wk)
                            t.wait_ge(qs, t_x0h)
                        gmax = max(gates)
                        if gmax:
                            t.wait_ge(SEM("p2v"), gmax)
                        KMM = int(os.environ.get("KMM", "0"))
                        if KMM == 3:
                            pass
                        elif KMM == 1:
                            t.matmul(PSQ[0][:, 0:512], wqs[:, 0:128], x0h[:, 0:512],
                                     start=True, stop=True)
                        else:
                            for tc in range(4):
                                for ki in range(8):
                                    t.matmul(PSQ[4 * grp + tc][:, 0:512],
                                             wqs[:, ki * 1024 + mo * 128: ki * 1024 + (mo + 1) * 128],
                                             x0h[:, ki * T + tc * 512: ki * T + (tc + 1) * 512],
                                             start=(ki == 0), stop=(ki == 7))
                        t.drain().then_inc(SEM("p2t"), 1)
                    ph.add("tensor", fk_t)

                    def fk_v(v, mo=mo, sh=sh, grp=grp, t_mm=t_mm):
                        v.wait_ge(SEM("p2t"), t_mm)
                        if int(os.environ.get("KEV", "1")) == 4:
                            v.wait_ge(qs, 4096)
                        KEV = int(os.environ.get("KEV", "1"))
                        for tc in range(4):
                            if KEV == 6:
                                ic = v.tensor_single_scalar(
                                    kfm[:, tc * 512:(tc + 1) * 512],
                                    PSQ[0][:, 0:512], 0.25, ALU.add)
                            elif KEV == 7:
                                ic = v.tensor_single_scalar(
                                    kfm[:, tc * 512:(tc + 1) * 512],
                                    PSQ[0][:, 0:512], consts[:, 1:2], ALU.add)
                            elif KEV == 5:
                                ic = v.tensor_single_scalar(
                                    fmA[:, tc * 512:(tc + 1) * 512],
                                    PSQ[0][:, 0:512], bq_sb[:, 8 + mo:9 + mo], ALU.add)
                            elif KEV == 3:
                                ic = v.tensor_single_scalar(
                                    kfm[:, tc * 512:(tc + 1) * 512],
                                    PSQ[0][:, 0:512], bq_sb[:, 8 + mo:9 + mo], ALU.add)
                            elif KEV in (1, 4):
                                ic = v.tensor_single_scalar(
                                    kfm[:, mo * S + sh * T + tc * 512: mo * S + sh * T + (tc + 1) * 512],
                                    PSQ[4 * grp + tc][:, 0:512], bq_sb[:, 8 + mo:9 + mo], ALU.add)
                            else:
                                ic = v.engine_nop()
                            ic.then_inc(SEM("p2v"), 1)
                    for tc in range(4):
                        bank_k[4 * grp + tc] = cnts["p2v"] + tc + 1
                    I("p2v", 4)
                    ph.add("vector", fk_v)
                t_k_mm = cnts["p2t"]

                if SEC2 < 3:
                    continue
                _, gate_wv = I("qg", 16 * 8)

                def flv(g, t_k_mm=t_k_mm):
                    g.wait_ge(SEM("p2t"), t_k_mm)
                    for ki in range(8):
                        g.dma_start(out=wqs[:, ki * 1024:(ki + 1) * 1024],
                                    in_=w_qkv[ki * 128:(ki + 1) * 128, 2048:3072]).then_inc(qg, 16)
                ph.add("gpsimd", flv)

                for j in range(NT):
                    kt = sh * NT + j
                    gates = [bank_v[(2 * kt + half) % 4 + 4] for half in range(2)]
                    _, t_mm = I("p2vt")

                    def fvt(t, j=j, kt=kt, sh=sh, gates=gates, gate_wv=gate_wv, t_vm=t_vm):
                        if j == 0:
                            t.wait_ge(qg, gate_wv)
                            t.wait_ge(SEM("ge"), t_vm)
                        gmax = max(gates)
                        if gmax:
                            t.wait_ge(SEM("p2vv"), gmax)
                        for half in range(2):
                            ps = PSQ[(2 * kt + half) % 4 + 4]
                            for ki in range(8):
                                t.matmul(ps[:, 0:512],
                                         x0h[:, ki * T + j * 128: ki * T + (j + 1) * 128],
                                         wqs[:, ki * 1024 + half * 512: ki * 1024 + (half + 1) * 512],
                                         start=(ki == 0), stop=False)
                            t.matmul(ps[:, 0:512], ones1b[:],
                                     bv_row[:, half * 512:(half + 1) * 512], start=False, stop=True)
                        t.drain().then_inc(SEM("p2vt"), 1)
                    ph.add("tensor", fvt)
                    st_gate = qs0_v + 16 * (nvst - 1) if nvst >= 2 else None

                    def fvv(v, kt=kt, t_mm=t_mm, st_gate=st_gate):
                        v.wait_ge(SEM("p2vt"), t_mm)
                        if st_gate is not None:
                            v.wait_ge(qs, st_gate)
                        for half in range(2):
                            ps = PSQ[(2 * kt + half) % 4 + 4]
                            base = (kt % 2) * 1040 + half * 520
                            ic = v.tensor_copy(
                                vstg.ap()[:, base:base + 520].rearrange("p (h c) -> p h c", c=65)[:, :, 0:64],
                                ps.ap()[:, 0:512].rearrange("p (h c) -> p h c", c=64))
                            ic.then_inc(SEM("p2vv"), 1)
                    for half in range(2):
                        bank_v[(2 * kt + half) % 4 + 4] = cnts["p2vv"] + half + 1
                    I("p2vv", 2)
                    ph.add("vector", fvv)
                    _, _tq = I("qs", 16)
                    nvst += 1

                    def fvs(sy, kt=kt, need=cnts["p2vv"]):
                        sy.wait_ge(SEM("p2vv"), need)
                        sy.dma_start(out=d_v[:, kt * 1040:(kt + 1) * 1040],
                                     in_=vstg[:, (kt % 2) * 1040:(kt % 2 + 1) * 1040]).then_inc(qs, 16)
                    ph.add("sync", fvs)
            ph.emit(); CKPT()
            es_x0h.close()

            es_ofm = ES()
            ofm = es_ofm.enter_context(nc.sbuf_tensor("ofm", [128, 8 * T], BF16, side="right"))
            ph = Ph()
            qs0_ofm = cnts["qs"]
            for i in range(NT):
                qs_gate = qs0_ofm + 16 * 8 * (i - 3) if i >= 4 else None
                _, tqg = I("qg", 16)

                def fl(g, i=i, qs_gate=qs_gate):
                    if qs_gate is not None:
                        g.wait_ge(qs, qs_gate)
                    g.dma_start(out=scb[:, (i % 4) * D:(i % 4 + 1) * D],
                                in_=x0_own[i * 128:(i + 1) * 128, :]).then_inc(qg, 16)
                ph.add("gpsimd", fl)
                _, _t = I("qs", 16 * 8)

                def ft(sy, i=i, gate=tqg):
                    sy.wait_ge(qg, gate)
                    for fc in range(8):
                        sy.dma_start(
                            out=ofm[:, fc * T + i * 128: fc * T + (i + 1) * 128],
                            in_=scb[:, (i % 4) * D + fc * 128:(i % 4) * D + (fc + 1) * 128],
                            transpose=True).then_inc(qs, 16)
                ph.add("sync", ft)
            t_ofm = cnts["qs"]

            _, gate_wq = I("qg", 16 * 8)

            def flq(g):
                for ki in range(8):
                    g.dma_start(out=wqs[:, ki * 1024:(ki + 1) * 1024],
                                in_=w_qkv[ki * 128:(ki + 1) * 128, 0:1024]).then_inc(qg, 16)
            ph.add("gpsimd", flq)

            bank_q = {i: 0 for i in range(8)}
            for mo in range(8):
                grp = mo % 2
                gates = [bank_q[4 * grp + tc] for tc in range(4)]
                _, t_mm = I("p2t")

                def fq_t(t, mo=mo, grp=grp, gates=gates):
                    if mo == 0:
                        t.wait_ge(qg, gate_wq)
                        t.wait_ge(qs, t_ofm)
                    gmax = max(gates)
                    if gmax:
                        t.wait_ge(SEM("p2v"), gmax)
                    for ki in range(8):
                        for tc in range(4):
                            t.matmul(PSQ[4 * grp + tc][:, 0:512],
                                     wqs[:, ki * 1024 + mo * 128: ki * 1024 + (mo + 1) * 128],
                                     ofm[:, ki * T + tc * 512: ki * T + (tc + 1) * 512],
                                     start=(ki == 0), stop=(ki == 7))
                    t.drain().then_inc(SEM("p2t"), 1)
                ph.add("tensor", fq_t)

                def fq_v(v, mo=mo, grp=grp, t_mm=t_mm):
                    v.wait_ge(SEM("p2t"), t_mm)
                    for tc in range(4):
                        ic = v.tensor_single_scalar(
                            fmA[:, mo * T + tc * 512: mo * T + (tc + 1) * 512],
                            PSQ[4 * grp + tc][:, 0:512], bq_sb[:, mo:mo + 1], ALU.add)
                        ic.then_inc(SEM("p2v"), 1)
                for tc in range(4):
                    bank_q[4 * grp + tc] = cnts["p2v"] + tc + 1
                I("p2v", 4)
                ph.add("vector", fq_v)
            ph.emit(); CKPT()
            es_ps2.close()
            es_ofm.close()
            es_p2.close()

            es_p3 = ES()
            v_sb = es_p3.enter_context(nc.sbuf_tensor("v_sb", [128, NKT * 1040], BF16))
            cmr = es_p3.enter_context(nc.sbuf_tensor("cmr", [128, 2 * 1024], BF16))
            es_ps3 = ES()
            ps_s = [es_ps3.enter_context(nc.psum_tensor("ps3s%d" % i, [128, 1024], F32))
                    for i in range(2)]
            ps_c = [es_ps3.enter_context(nc.psum_tensor("ps3c%d" % i, [65, 256], F32))
                    for i in range(2)]
            ps_zb = es_ps3.enter_context(nc.psum_tensor("ps3zb", [64, 256], F32))

            ph = Ph()
            SEM("p3t"); SEM("p3e"); SEM("p3m"); SEM("p3a"); SEM("p3z"); SEM("p3zb"); SEM("p3ev")
            expbuf_av = {0: 0, 1: 0}
            psc_ev = {0: 0, 1: 0}
            pss_exp = {0: 0, 1: 0}
            mask_used = {0: 0, 1: 0}
            stage_dma = {0: 0, 1: 0}
            scr_zb = 0
            zb_ev = 0
            vload = []
            for bi in range(NBLK):
                _, tvl = I("qg", 16 * 5)
                mu_gate = mask_used[bi % 2]

                def fld(g, bi=bi, mu_gate=mu_gate):
                    if mu_gate:
                        g.wait_ge(SEM("p3m"), mu_gate)
                    for j in range(4):
                        kt = 4 * bi + j
                        g.dma_start(out=v_sb[:, kt * 1040:(kt + 1) * 1040],
                                    in_=d_v[:, kt * 1040:(kt + 1) * 1040]).then_inc(qg, 16)
                    g.dma_start(
                        out=cmr.ap()[:, (bi % 2) * 1024:(bi % 2 + 1) * 1024]
                            .rearrange("p (m q) -> p m q", q=256),
                        in_=cmask[4 * bi:4 * bi + 4, :].rearrange("m (p q) -> p m q", p=128)
                    ).then_inc(qg, 16)
                ph.add("gpsimd", fld)
                vload.append(tvl)

                for h in range(H):
                    ch, ro = h >> 1, 64 * (h & 1)
                    nsc = bi + 1
                    av_tick = 0
                    for sc in range(nsc):
                        pp = sc % 2
                        gate_exp_reuse = pss_exp[pp]
                        _, t_mm = I("p3t")

                        def fsc_t(t, bi=bi, h=h, ch=ch, ro=ro, sc=sc, pp=pp,
                                  gate=gate_exp_reuse, tvl=tvl):
                            if sc == 0 and h == 0:
                                t.wait_ge(qg, tvl)
                            if gate:
                                t.wait_ge(SEM("p3e"), gate)
                            for j in range(4):
                                kt = 4 * sc + j
                                t.matmul(ps_s[pp][:, j * 256:(j + 1) * 256],
                                         kfm[ro:ro + 64, ch * S + kt * 128: ch * S + (kt + 1) * 128],
                                         fmA[ro:ro + 64, ch * T + bi * 256: ch * T + (bi + 1) * 256],
                                         start=True, stop=True)
                            t.drain().then_inc(SEM("p3t"), 1)
                        ph.add("tensor", fsc_t)

                        gate_av_reuse = expbuf_av[pp]
                        _, t_exp = I("p3e")

                        def fsc_s(s_, pp=pp, t_mm=t_mm, gate=gate_av_reuse):
                            s_.wait_ge(SEM("p3t"), t_mm)
                            if gate:
                                s_.wait_ge(SEM("p3a"), gate)
                            s_.activation(scb[:, pp * 1024:(pp + 1) * 1024], ps_s[pp][:, 0:1024],
                                          AF.Exp, bias=consts[:, 0:1],
                                          scale=0.125).then_inc(SEM("p3e"), 1)
                        ph.add("scalar", fsc_s)
                        pss_exp[pp] = t_exp

                        last = (sc == nsc - 1)
                        if last:
                            _, t_msk = I("p3m")

                            def fsc_m(v, bi=bi, pp=pp, t_exp=t_exp, tvl=tvl):
                                v.wait_ge(SEM("p3e"), t_exp)
                                v.wait_ge(qg, tvl)
                                v.tensor_tensor(scb[:, pp * 1024:(pp + 1) * 1024],
                                                scb[:, pp * 1024:(pp + 1) * 1024],
                                                cmr[:, (bi % 2) * 1024:(bi % 2 + 1) * 1024],
                                                ALU.mult).then_inc(SEM("p3m"), 1)
                            ph.add("vector", fsc_m)
                            mask_used[bi % 2] = t_msk
                            av_need = ("p3m", t_msk)
                        else:
                            av_need = ("p3e", t_exp)

                        gate_psc = psc_ev[h % 2] if sc == 0 else 0
                        _, t_av = I("p3a")

                        def fsc_a(t, bi=bi, h=h, sc=sc, pp=pp, nsc=nsc,
                                  av_need=av_need, gate_psc=gate_psc):
                            t.wait_ge(SEM(av_need[0]), av_need[1])
                            if gate_psc:
                                t.wait_ge(SEM("p3ev"), gate_psc)
                            for j in range(4):
                                kt = 4 * sc + j
                                t.matmul(ps_c[h % 2][:, 0:256],
                                         v_sb[:, kt * 1040 + h * 65: kt * 1040 + h * 65 + 65],
                                         scb[:, pp * 1024 + j * 256: pp * 1024 + (j + 1) * 256],
                                         start=(sc == 0 and j == 0),
                                         stop=(sc == nsc - 1 and j == 3))
                            t.drain().then_inc(SEM("p3a"), 1)
                        ph.add("tensor", fsc_a)
                        expbuf_av[pp] = t_av
                        av_tick = t_av

                    _, t_z = I("p3z")

                    def fz_v(v, h=h, av_tick=av_tick, scr_gate=scr_zb):
                        v.wait_ge(SEM("p3a"), av_tick)
                        if scr_gate:
                            v.wait_ge(SEM("p3zb"), scr_gate)
                        v.reciprocal(scr[0:1, 0:256], ps_c[h % 2][64:65, 0:256]).then_inc(SEM("p3z"), 1)
                    ph.add("vector", fz_v)
                    _, t_zb = I("p3zb")

                    def fz_t(t, t_z=t_z, zb_gate=zb_ev):
                        t.wait_ge(SEM("p3z"), t_z)
                        if zb_gate:
                            t.wait_ge(SEM("p3ev"), zb_gate)
                        t.matmul(ps_zb[:, 0:256], ones1f[0:1, 0:64], scr[0:1, 0:256],
                                 start=True, stop=True)
                        t.drain().then_inc(SEM("p3zb"), 1)
                    ph.add("tensor", fz_t)
                    scr_zb = t_zb
                    _, t_ev = I("p3ev")

                    def fz_e(v, h=h, t_zb=t_zb, dma_gate=stage_dma[h % 2]):
                        v.wait_ge(SEM("p3zb"), t_zb)
                        if dma_gate:
                            v.wait_ge(qs, dma_gate)
                        stg = scb[0:64, 2048 + (h % 2) * 256: 2048 + (h % 2) * 256 + 256]
                        v.tensor_copy(stg, ps_c[h % 2][0:64, 0:256])
                        v.tensor_tensor(stg, stg, ps_zb[0:64, 0:256],
                                        ALU.mult).then_inc(SEM("p3ev"), 1)
                    ph.add("vector", fz_e)
                    psc_ev[h % 2] = t_ev
                    zb_ev = t_ev
                    _, tq_st = I("qs", 16)

                    def fz_s(sy, bi=bi, h=h, t_ev=t_ev):
                        sy.wait_ge(SEM("p3ev"), t_ev)
                        sy.dma_start(out=d_ctx[h * 64:(h + 1) * 64, bi * 256:(bi + 1) * 256],
                                     in_=scb[0:64, 2048 + (h % 2) * 256: 2048 + (h % 2) * 256 + 256]
                                     ).then_inc(qs, 16)
                    ph.add("sync", fz_s)
                    stage_dma[h % 2] = tq_st
            ph.emit(); CKPT()
            es_ps3.close()
            es_p3.close()
            es_kfm.close()

            es_xf = ES()
            xf = es_xf.enter_context(nc.sbuf_tensor("xf", [128, NT * D], F32))
            es_p4 = ES()
            wo_sb = es_p4.enter_context(nc.sbuf_tensor("wo_sb", [128, 8 * 1024], BF16))
            ctxl = es_p4.enter_context(nc.sbuf_tensor("ctxl", [128, 2 * 1024], BF16))
            es_ps4 = ES()
            ps4 = [es_ps4.enter_context(nc.psum_tensor("ps4_%d" % i, [128, 1024], F32))
                   for i in range(2)]

            def phase_proj_ln(wsrc, brow, dctx_src, resid_loader, spill_dst, tag,
                              wo_sb, ctxl, ps4, fm_dst, xf_t, do_tr=True, extra_v_fn=None):
                ph = Ph()
                _, gate_w = I("qg", 16 * 8)

                def flw(g):
                    for ki in range(8):
                        g.dma_start(out=wo_sb[:, ki * 1024:(ki + 1) * 1024],
                                    in_=wsrc[ki * 128:(ki + 1) * 128, :]).then_inc(qg, 16)
                ph.add("gpsimd", flw)
                SEM(tag + "t")
                ln_a = {0: 0, 1: 0}
                ln_c = {0: 0, 1: 0}
                tr_done = {0: 0, 1: 0}
                resid_rd = 0
                for ti in range(NT):
                    pp = ti % 2
                    gate_ctx = cnts[tag + "t"] - 1 if ti >= 2 else 0
                    _, t_ld = I("qg", 32)

                    def fld(g, ti=ti, pp=pp, gate_ctx=gate_ctx, rr=resid_rd):
                        if gate_ctx:
                            g.wait_ge(SEM(tag + "t"), gate_ctx)
                        g.dma_start(
                            out=ctxl.ap()[:, pp * 1024:(pp + 1) * 1024].rearrange("p (k c) -> p k c", c=128),
                            in_=dctx_src.rearrange("(k p) t -> p k t", p=128)[:, :, ti * 128:(ti + 1) * 128]
                        ).then_inc(qg, 16)
                        if rr:
                            g.wait_ge(SEM(tag + "a"), rr)
                        resid_loader(g, ti, scr[:, 1024:2048])
                    ph.add("gpsimd", fld)

                    gate_psum = ln_a[pp]
                    _, t_mm = I(tag + "t")

                    def fmm(t, ti=ti, pp=pp, t_ld=t_ld, gate_w=gate_w, gate_psum=gate_psum):
                        t.wait_ge(qg, t_ld)
                        if ti == 0:
                            t.wait_ge(qg, gate_w)
                        if gate_psum:
                            t.wait_ge(SEM(tag + "a"), gate_psum)
                        for ki in range(8):
                            for nh in range(2):
                                t.matmul(ps4[pp][:, nh * 512:(nh + 1) * 512],
                                         ctxl[:, pp * 1024 + ki * 128: pp * 1024 + (ki + 1) * 128],
                                         wo_sb[:, ki * 1024 + nh * 512: ki * 1024 + (nh + 1) * 512],
                                         start=(ki == 0), stop=False)
                        for nh in range(2):
                            t.matmul(ps4[pp][:, nh * 512:(nh + 1) * 512], ones1b[:],
                                     brow[:, nh * 512:(nh + 1) * 512],
                                     start=False, stop=True)
                        t.drain().then_inc(SEM(tag + "t"), 1)
                    ph.add("tensor", fmm)

                    obf = scb.ap()[:, 2048 + pp * 1024: 2048 + (pp + 1) * 1024]
                    gate_tr = tr_done[pp]
                    sc_, tc = sched_ln(
                        ph,
                        (lambda pp=pp: ps4[pp][:, 0:1024]),
                        (lambda: scr[:, 1024:2048]),
                        xf_t[:, ti * D:(ti + 1) * D],
                        obf,
                        spill_dst[:, ti * D:(ti + 1) * D] if spill_dst is not None else None,
                        tag,
                        extra_wait=(lambda t_mm=t_mm: (SEM(tag + "t"), t_mm)),
                        outbf_wait=(qs, gate_tr))
                    ln_a[pp] = cnts[tag + "a"]
                    ln_c[pp] = tc
                    resid_rd = cnts[tag + "a"]
                    t_tr = I("qs", 16 * 8)[1] if do_tr else 0

                    def ftr(sy, ti=ti, pp=pp, tc=tc, obf=obf):
                        sy.wait_ge(SEM(tag + "c"), tc)
                        for fc in range(8):
                            sy.dma_start(
                                out=fm_dst[:, fc * T + ti * 128: fc * T + (ti + 1) * 128],
                                in_=obf[:, fc * 128:(fc + 1) * 128],
                                transpose=True).then_inc(qs, 16)
                    if do_tr:
                        ph.add("sync", ftr)
                        tr_done[pp] = t_tr
                    if extra_v_fn is not None:
                        extra_v_fn(ph, ti, tc)
                return ph

            def resid_x0(g, ti, dst):
                g.dma_start(out=dst, in_=x0_own[ti * 128:(ti + 1) * 128, :]).then_inc(qg, 16)

            ph = phase_proj_ln(w_out, bo_row, d_ctx, resid_x0, d_x1, "p4",
                               wo_sb, ctxl, ps4, fmA, xf)
            ph.emit(); CKPT()
            es_ps4.close()
            es_p4.close()

            es_p5 = ES()
            wcn1 = es_p5.enter_context(nc.sbuf_tensor("wcn1", [128, 8 * 1024], BF16))
            wcn2 = es_p5.enter_context(nc.sbuf_tensor("wcn2", [128, 8 * 512], BF16))
            wcn3 = es_p5.enter_context(nc.sbuf_tensor("wcn3", [128, 4 * 256], BF16))
            wc4b = es_p5.enter_context(nc.sbuf_tensor("wc4b", [128, 2], BF16))
            wgc1 = es_p5.enter_context(nc.sbuf_tensor("wgc1", [128, 8 * 512], BF16))
            wgc2 = es_p5.enter_context(nc.sbuf_tensor("wgc2", [128, 4 * 256], BF16))
            wgc3 = es_p5.enter_context(nc.sbuf_tensor("wgc3", [128, 2], BF16))
            gcbf = es_p5.enter_context(nc.sbuf_tensor("gcbf", [128, 8], BF16))
            h1b = es_p5.enter_context(nc.sbuf_tensor("h1b", [128, 8 * 512], BF16))
            h2b = es_p5.enter_context(nc.sbuf_tensor("h2b", [128, 4 * 512], BF16))
            h3b = es_p5.enter_context(nc.sbuf_tensor("h3b", [128, 2 * 512], BF16))
            es_ps5 = ES()
            ps5 = [es_ps5.enter_context(nc.psum_tensor("ps5_%d" % i, [128, 512], F32))
                   for i in range(4)]
            pt4 = es_ps5.enter_context(nc.psum_tensor("pt4", [128, 16], F32))
            psg = es_ps5.enter_context(nc.psum_tensor("psg", [128, 8], F32))
            ps1x = es_ps5.enter_context(nc.psum_tensor("ps1x", [1, 8], F32))

            ph = Ph()
            _, t_gcr = I("p5gc")

            def fgc_v(v):
                for ki in range(8):
                    v.reduce_sum(gsc[:, ki:ki + 1], fmA[:, ki * T:(ki + 1) * T], AX.X)
                v.drain().then_inc(SEM("p5gc"), 1)
            ph.add("vector", fgc_v)
            _, t_gci = I("qg", 16)

            def fgc_g(g, t_gcr=t_gcr):
                g.wait_ge(SEM("p5gc"), t_gcr)
                g.dma_start(out=d_gci, in_=gsc[:, 0:8]).then_inc(qg, 16)
            ph.add("gpsimd", fgc_g)
            _, t_cc = I("ccg")

            def fgc_c(g, t_gci=t_gci):
                g.wait_ge(qg, t_gci)
                g.collective_compute("AllReduce", ALU.add, replica_groups=pairs,
                                     ins=[d_gci], outs=[d_gco]).then_inc(SEM("ccg"), 1)
            ph.add("gpsimd", fgc_c)
            _, t_gco = I("qg", 16)

            def fgc_g2(g, t_cc=t_cc):
                g.wait_ge(SEM("ccg"), t_cc)
                g.dma_start(out=gsc[:, 8:16], in_=d_gco).then_inc(qg, 16)
            ph.add("gpsimd", fgc_g2)

            _, t_wcn = I("qg", 16 * (8 + 8 + 4 + 1 + 8 + 4 + 1))

            def fwcn(g):
                ncd = nc.allow_non_contiguous_dma(reason="tiny cn_w4/gc_w3 loads")
                ncd.__enter__()
                for ki in range(8):
                    g.dma_start(out=wcn1[:, ki * 1024:(ki + 1) * 1024],
                                in_=cn_w1[ki * 128:(ki + 1) * 128, :]).then_inc(qg, 16)
                for ki in range(8):
                    g.dma_start(out=wcn2[:, ki * 512:(ki + 1) * 512],
                                in_=cn_w2[ki * 128:(ki + 1) * 128, :]).then_inc(qg, 16)
                for ki in range(4):
                    g.dma_start(out=wcn3[:, ki * 256:(ki + 1) * 256],
                                in_=cn_w3[ki * 128:(ki + 1) * 128, :]).then_inc(qg, 16)
                g.dma_start(out=wc4b[:], in_=cn_w4.rearrange("(c p) o -> p (c o)", p=128)).then_inc(qg, 16)
                for ki in range(8):
                    g.dma_start(out=wgc1[:, ki * 512:(ki + 1) * 512],
                                in_=gc_w1[ki * 128:(ki + 1) * 128, :]).then_inc(qg, 16)
                for ki in range(4):
                    g.dma_start(out=wgc2[:, ki * 256:(ki + 1) * 256],
                                in_=gc_w2[ki * 128:(ki + 1) * 128, :]).then_inc(qg, 16)
                g.dma_start(out=wgc3[:], in_=gc_w3.rearrange("(c p) o -> p (c o)", p=128)).then_inc(qg, 16)
                ncd.__exit__(None, None, None)
            ph.add("gpsimd", fwcn)

            _, t_gc2 = I("p5gf")

            def fgf_v(v, t_gco=t_gco):
                v.wait_ge(qg, t_gco)
                v.tensor_scalar_mul(gsc[:, 16:24], gsc[:, 8:16], 1.0 / 4096.0)
                v.tensor_scalar_mul(gcbf[:], gsc[:, 8:16], 1.0 / 4096.0)
                v.drain().then_inc(SEM("p5gf"), 1)
            ph.add("vector", fgf_v)
            _, t_gf1 = I("p5t")

            def fgf_t(t, t_gc2=t_gc2, t_wcn=t_wcn):
                t.wait_ge(SEM("p5gf"), t_gc2)
                t.wait_ge(qg, t_wcn)
                for mo in range(4):
                    for ki in range(8):
                        t.matmul(psg[:, mo:mo + 1], wgc1[:, ki * 512 + mo * 128: ki * 512 + (mo + 1) * 128],
                                 gcbf[:, ki:ki + 1], start=(ki == 0), stop=(ki == 7))
                t.drain().then_inc(SEM("p5t"), 1)
            ph.add("tensor", fgf_t)
            _, t_gf1s = I("p5s")

            def fgf_s(s_, t_gf1=t_gf1):
                s_.wait_ge(SEM("p5t"), t_gf1)
                for mo in range(4):
                    ic = s_.activation(scb[:, mo:mo + 1], psg[:, mo:mo + 1], AF.Gelu,
                                       bias=gcb1[:, mo:mo + 1], scale=1.0)
                ic.then_inc(SEM("p5s"), 1)
            ph.add("scalar", fgf_s)
            _, t_gf2 = I("p5t")

            def fgf_t2(t, t_gf1s=t_gf1s):
                t.wait_ge(SEM("p5s"), t_gf1s)
                for mo in range(2):
                    for ki in range(4):
                        t.matmul(psg[:, 4 + mo:5 + mo], wgc2[:, ki * 256 + mo * 128: ki * 256 + (mo + 1) * 128],
                                 scb[:, ki:ki + 1], start=(ki == 0), stop=(ki == 3))
                t.drain().then_inc(SEM("p5t"), 1)
            ph.add("tensor", fgf_t2)
            _, t_gf2s = I("p5s")

            def fgf_s2(s_, t_gf2=t_gf2):
                s_.wait_ge(SEM("p5t"), t_gf2)
                for mo in range(2):
                    ic = s_.activation(scb[:, 4 + mo:5 + mo], psg[:, 4 + mo:5 + mo], AF.Gelu,
                                       bias=gcb2[:, mo:mo + 1], scale=1.0)
                ic.then_inc(SEM("p5s"), 1)
            ph.add("scalar", fgf_s2)
            _, t_gf3 = I("p5t")

            def fgf_t3(t, t_gf2s=t_gf2s):
                t.wait_ge(SEM("p5s"), t_gf2s)
                for ki in range(2):
                    t.matmul(ps1x[0:1, 0:1], wgc3[:, ki:ki + 1], scb[:, 4 + ki:5 + ki],
                             start=(ki == 0), stop=(ki == 1))
                t.drain().then_inc(SEM("p5t"), 1)
            ph.add("tensor", fgf_t3)
            _, t_gf3s = I("p5s")

            def fgf_s3(s_, t_gf3=t_gf3):
                s_.wait_ge(SEM("p5t"), t_gf3)
                s_.activation(scr[0:1, 0:1], ps1x[0:1, 0:1], AF.Sigmoid,
                              bias=gb3r[0:1, 0:1], scale=1.0).then_inc(SEM("p5s"), 1)
            ph.add("scalar", fgf_s3)
            _, t_gfb = I("p5t")

            def fgf_t4(t, t_gf3s=t_gf3s):
                t.wait_ge(SEM("p5s"), t_gf3s)
                t.matmul(psg[:, 6:7], ones1f[0:1, :], scr[0:1, 0:1],
                         start=True, stop=True)
                t.drain().then_inc(SEM("p5t"), 1)
            ph.add("tensor", fgf_t4)
            _, t_gfv = I("p5v")

            def fgf_v2(v, t_gfb=t_gfb):
                v.wait_ge(SEM("p5t"), t_gfb)
                v.tensor_scalar_mul(gfr[:], psg[:, 6:7], 1.0)
                v.tensor_scalar_mul(a6r[:], arr[:], 0.6)
                v.tensor_scalar_mul(gsc[:, 30:31], arr[:], -0.6)
                v.tensor_tensor(bvr[:], arr[:], gfr[:], ALU.mult)
                v.tensor_scalar_mul(gsc[:, 24:25], consts[:, 4:5], 1.0)
                v.tensor_scalar_mul(gsc[:, 25:26], consts[:, 4:5], 1.0)
                v.tensor_scalar_mul(bvr[:], bvr[:], 1.4)
                v.tensor_scalar_mul(gsc[:, 24:25], consts[:, 4:5], 1.0)
                v.tensor_scalar_mul(gsc[:, 25:26], consts[:, 4:5], 1.0)
                v.scalar_tensor_tensor(bvr[:], arr[:], -1.0, bvr[:], ALU.mult, ALU.add)
                v.tensor_scalar_mul(gsc[:, 24:25], consts[:, 4:5], 1.0)
                v.tensor_scalar_mul(gsc[:, 25:26], consts[:, 4:5], 1.0)
                v.tensor_scalar_add(bvr[:], bvr[:], 0.1)
                v.tensor_scalar_mul(gsc[:, 24:25], consts[:, 4:5], 1.0)
                v.tensor_scalar_mul(gsc[:, 25:26], consts[:, 4:5], 1.0)
                v.tensor_scalar_mul(gsc[:, 31:32], bvr[:], -1.0)
                v.drain().then_inc(SEM("p5v"), 1)
            ph.add("vector", fgf_v2)

            bank_ev = {0: 0, 1: 0, 2: 0, 3: 0}
            refresh_tr = {0: 0, 1: 0}
            refresh_q = 0
            dbg_prev_dump = [0]
            for cyc in range(3):
                for tc in range(4):
                    for mo in range(8):
                        gate_ps = bank_ev[mo % 4]
                        _, t_mm = I("p5t")

                        def f1t(t, mo=mo, tc=tc, cyc=cyc, gate_ps=gate_ps,
                                t_wcn=t_wcn, rq=refresh_q):
                            if cyc == 0 and tc == 0 and mo == 0:
                                t.wait_ge(qg, t_wcn)
                            if rq and tc == 0 and mo == 0:
                                t.wait_ge(qs, rq)
                            if gate_ps:
                                t.wait_ge(SEM("p5s"), gate_ps)
                            for ki in range(8):
                                t.matmul(ps5[mo % 4][:, 0:512],
                                         wcn1[:, ki * 1024 + mo * 128: ki * 1024 + (mo + 1) * 128],
                                         fmA[:, ki * T + tc * 512: ki * T + (tc + 1) * 512],
                                         start=(ki == 0), stop=(ki == 7))
                            t.drain().then_inc(SEM("p5t"), 1)
                        ph.add("tensor", f1t)
                        _, t_ev = I("p5s")

                        def f1s(s_, mo=mo, t_mm=t_mm):
                            s_.wait_ge(SEM("p5t"), t_mm)
                            s_.activation(h1b[:, mo * 512:(mo + 1) * 512], ps5[mo % 4][:, 0:512],
                                          AF.Gelu, bias=cnb1[:, mo:mo + 1],
                                          scale=1.0).then_inc(SEM("p5s"), 1)
                        ph.add("scalar", f1s)
                        bank_ev[mo % 4] = cnts["p5s"]
                    h1done = cnts["p5s"]
                    for mo in range(4):
                        gate_ps = bank_ev[mo % 4]
                        _, t_mm = I("p5t")

                        def f2t(t, mo=mo, h1done=h1done, gate_ps=gate_ps):
                            t.wait_ge(SEM("p5s"), max(h1done, gate_ps))
                            for ki in range(8):
                                t.matmul(ps5[mo % 4][:, 0:512],
                                         wcn2[:, ki * 512 + mo * 128: ki * 512 + (mo + 1) * 128],
                                         h1b[:, ki * 512:(ki + 1) * 512],
                                         start=(ki == 0), stop=(ki == 7))
                            t.drain().then_inc(SEM("p5t"), 1)
                        ph.add("tensor", f2t)
                        _, t_ev = I("p5s")

                        def f2s(s_, mo=mo, t_mm=t_mm):
                            s_.wait_ge(SEM("p5t"), t_mm)
                            s_.activation(h2b[:, mo * 512:(mo + 1) * 512], ps5[mo % 4][:, 0:512],
                                          AF.Gelu, bias=cnb2[:, mo:mo + 1],
                                          scale=1.0).then_inc(SEM("p5s"), 1)
                        ph.add("scalar", f2s)
                        bank_ev[mo % 4] = cnts["p5s"]
                    h2done = cnts["p5s"]
                    for mo in range(2):
                        gate_ps = bank_ev[mo]
                        _, t_mm = I("p5t")

                        def f3t(t, mo=mo, h2done=h2done, gate_ps=gate_ps):
                            t.wait_ge(SEM("p5s"), max(h2done, gate_ps))
                            for ki in range(4):
                                t.matmul(ps5[mo][:, 0:512],
                                         wcn3[:, ki * 256 + mo * 128: ki * 256 + (mo + 1) * 128],
                                         h2b[:, ki * 512:(ki + 1) * 512],
                                         start=(ki == 0), stop=(ki == 3))
                            t.drain().then_inc(SEM("p5t"), 1)
                        ph.add("tensor", f3t)
                        _, t_ev = I("p5s")

                        def f3s(s_, mo=mo, t_mm=t_mm):
                            s_.wait_ge(SEM("p5t"), t_mm)
                            s_.activation(h3b[:, mo * 512:(mo + 1) * 512], ps5[mo][:, 0:512],
                                          AF.Gelu, bias=cnb3[:, mo:mo + 1],
                                          scale=1.0).then_inc(SEM("p5s"), 1)
                        ph.add("scalar", f3s)
                        bank_ev[mo] = cnts["p5s"]
                    h3done = cnts["p5s"]
                    _, t_mm = I("p5t")

                    def f4t(t, tc=tc, h3done=h3done, cyc=cyc, t_gfv=t_gfv):
                        t.wait_ge(SEM("p5s"), h3done)
                        if cyc == 0 and tc == 0:
                            t.wait_ge(SEM("p5v"), t_gfv)
                        for tj in range(4):
                            ti = tc * 4 + tj
                            for kc in range(2):
                                t.matmul(pt4[:, ti:ti + 1],
                                         h3b[:, kc * 512 + tj * 128: kc * 512 + (tj + 1) * 128],
                                         wc4b[:, kc:kc + 1],
                                         start=(kc == 0), stop=(kc == 1))
                        t.drain().then_inc(SEM("p5t"), 1)
                    ph.add("tensor", f4t)
                t4done = cnts["p5t"]
                _, t_sg = I("p5s")

                def fsg(s_, t4done=t4done, dwait=(dbg_prev_dump[0] if DBG else 0)):
                    s_.wait_ge(SEM("p5t"), t4done)
                    if dwait:
                        s_.wait_ge(qg, dwait)
                    s_.activation(tfc[:], pt4[:], AF.Sigmoid, bias=b4r[:, 0:1],
                                  scale=1.0).then_inc(SEM("p5s"), 1)
                ph.add("scalar", fsg)
                _, t_cp = I("p5v")

                def fcp(v, t_sg=t_sg, t_gfv=t_gfv):
                    v.wait_ge(SEM("p5s"), t_sg)
                    v.wait_ge(SEM("p5v"), t_gfv)
                    v.tensor_scalar(cplb[:], tfc[:], a6r[:, 0:1], bvr[:, 0:1], ALU.mult, ALU.add)
                    v.tensor_scalar(ncplb[:], tfc[:], gsc[:, 30:31], gsc[:, 31:32],
                                    ALU.mult, ALU.add).then_inc(SEM("p5v"), 1)
                ph.add("vector", fcp)
                if DBG and cyc == 0:
                    _, t_d5 = I("qg", 16 * 15)

                    def fd5(g, t_cp=t_cp):
                        ncd = nc.allow_non_contiguous_dma(reason="debug dump")
                        ncd.__enter__()
                        g.wait_ge(SEM("p5v"), t_cp)
                        g.dma_start(out=o_p5[:, 0:1], in_=gfr[:]).then_inc(qg, 16)
                        g.dma_start(out=o_p5[:, 1:2], in_=a6r[:]).then_inc(qg, 16)
                        g.dma_start(out=o_p5[:, 2:3], in_=bvr[:]).then_inc(qg, 16)
                        g.dma_start(out=o_p5[:, 16:32], in_=tfc[:]).then_inc(qg, 16)
                        g.dma_start(out=o_p5[:, 32:48], in_=cplb[:]).then_inc(qg, 16)
                        g.dma_start(out=o_p5[:, 3:11], in_=gsc[:, 8:16]).then_inc(qg, 16)
                        g.dma_start(out=o_p5[:, 48:56], in_=gsc[:, 16:24]).then_inc(qg, 16)
                        g.dma_start(out=o_p5b[:, 0:8], in_=scb[:, 0:8]).then_inc(qg, 16)
                        g.dma_start(out=o_p5[:, 11:12], in_=gfr[:]).then_inc(qg, 16)
                        g.dma_start(out=o_p5[:, 56:64], in_=gsc[:, 24:32]).then_inc(qg, 16)
                        g.dma_start(out=o_p5c[:, 0:8], in_=wgc1[:, 0:8]).then_inc(qg, 16)
                        g.dma_start(out=o_p5c[:, 8:16], in_=wgc1[:, 512:520]).then_inc(qg, 16)
                        g.dma_start(out=o_p5c[:, 16:24], in_=gcbf[:, 0:8]).then_inc(qg, 16)
                        g.dma_start(out=o_p5[:, 12:16], in_=scr[:, 512:516]).then_inc(qg, 16)
                        g.dma_start(out=o_p5[:, 60:64], in_=scr[:, 528:532]).then_inc(qg, 16)
                        ncd.__exit__(None, None, None)
                    ph.add("gpsimd", fd5)
                    dbg_prev_dump[0] = t_d5

                for ti in range(NT):
                    _, t_tw = I("p5v")

                    def ftw(v, ti=ti, t_cp=t_cp):
                        v.wait_ge(SEM("p5v"), t_cp)
                        q0 = xf[:, ti * D + 0:ti * D + 256]
                        q1 = xf[:, ti * D + 256:ti * D + 512]
                        q2 = xf[:, ti * D + 512:ti * D + 768]
                        q3 = xf[:, ti * D + 768:ti * D + 1024]
                        cp = cplb[:, ti:ti + 1]
                        ncp = ncplb[:, ti:ti + 1]
                        v.tensor_scalar_mul(scr[:, 0:256], q0, 1.0)
                        v.scalar_tensor_tensor(q0, q2, cp, q0, ALU.mult, ALU.add)
                        v.scalar_tensor_tensor(q2, scr[:, 0:256], ncp, q2, ALU.mult, ALU.add)
                        v.tensor_scalar_mul(scr[:, 0:256], q1, 1.0)
                        v.scalar_tensor_tensor(q1, q3, ncp, q1, ALU.mult, ALU.add)
                        ic = v.scalar_tensor_tensor(q3, scr[:, 0:256], cp, q3, ALU.mult, ALU.add)
                        ic.then_inc(SEM("p5v"), 1)
                    ph.add("vector", ftw)
                    if cyc < 2:
                        h1_read_done = cnts["p5t"]
                        _, t_cast = I("p5v")

                        def fcst(v, ti=ti, t_tw=t_tw, trg=refresh_tr[ti % 2]):
                            if trg:
                                v.wait_ge(qs, trg)
                            v.tensor_copy(scb[:, 2048 + (ti % 2) * 1024: 2048 + (ti % 2 + 1) * 1024],
                                          xf[:, ti * D:(ti + 1) * D]).then_inc(SEM("p5v"), 1)
                        ph.add("vector", fcst)
                        _, t_tr = I("qs", 16 * 8)

                        def ftr5(sy, ti=ti, t_cast=t_cast, h1_read_done=h1_read_done):
                            sy.wait_ge(SEM("p5v"), t_cast)
                            sy.wait_ge(SEM("p5t"), h1_read_done)
                            for fc in range(8):
                                sy.dma_start(
                                    out=fmA[:, fc * T + ti * 128: fc * T + (ti + 1) * 128],
                                    in_=scb[:, 2048 + (ti % 2) * 1024 + fc * 128:
                                            2048 + (ti % 2) * 1024 + (fc + 1) * 128],
                                    transpose=True).then_inc(qs, 16)
                        ph.add("sync", ftr5)
                        refresh_tr[ti % 2] = cnts["qs"]
                        refresh_q = cnts["qs"]
            ph.emit(); CKPT()

            ph = Ph()
            ln_a = 0
            tr_done2 = {0: 0, 1: 0}
            for ti in range(NT):
                pp = ti % 2
                _, t_ld = I("qg", 16)

                def fldx1(g, ti=ti, rr=ln_a):
                    if rr:
                        g.wait_ge(SEM("p5a"), rr)
                    g.dma_start(out=scr[:, 1024:2048],
                                in_=d_x1[:, ti * D:(ti + 1) * D]).then_inc(qg, 16)
                ph.add("gpsimd", fldx1)
                obf = scb.ap()[:, 2048 + pp * 1024: 2048 + (pp + 1) * 1024]
                sc_, tc = sched_ln(
                    ph,
                    (lambda ti=ti: xf[:, ti * D:(ti + 1) * D]),
                    (lambda: scr[:, 1024:2048]),
                    xf[:, ti * D:(ti + 1) * D],
                    obf,
                    d_x2[:, ti * D:(ti + 1) * D],
                    "p5",
                    extra_wait=(lambda t_ld=t_ld: (qg, t_ld)),
                    outbf_wait=(qs, tr_done2[pp]))
                ln_a = cnts["p5a"]
                _, t_tr = I("qs", 16 * 8)

                def ftr2(sy, ti=ti, pp=pp, tc=tc, obf=obf):
                    sy.wait_ge(SEM("p5c"), tc)
                    for fc in range(8):
                        sy.dma_start(
                            out=fmA[:, fc * T + ti * 128: fc * T + (ti + 1) * 128],
                            in_=obf[:, fc * 128:(fc + 1) * 128],
                            transpose=True).then_inc(qs, 16)
                ph.add("sync", ftr2)
                tr_done2[pp] = t_tr
            ph.emit(); CKPT()
            es_ps5.close()
            es_p5.close()

            es_qfm = ES()
            qfm = es_qfm.enter_context(nc.sbuf_tensor("qfm", [128, 8 * T], BF16, side="right"))
            es_p6a = ES()
            wL1 = es_p6a.enter_context(nc.sbuf_tensor("wL1", [128, 8 * 1024], BF16))
            vst6 = es_p6a.enter_context(nc.sbuf_tensor("vst6", [128, 2 * 1040], BF16))
            kst6 = es_p6a.enter_context(nc.sbuf_tensor("kst6", [128, 2 * 2048], BF16))
            es_ps6a = ES()
            PS6 = [es_ps6a.enter_context(nc.psum_tensor("ps6a%d" % i, [128, 512], F32))
                   for i in range(8)]

            ph = Ph()
            SEM("p6t"); SEM("p6v"); SEM("p6q")

            def wload6(src, gate_need):
                def fl(g, src=src, gate_need=gate_need):
                    if gate_need:
                        g.wait_ge(SEM("p6t"), gate_need)
                    for ki in range(8):
                        g.dma_start(out=wL1[:, ki * 1024:(ki + 1) * 1024],
                                    in_=src[ki * 128:(ki + 1) * 128, :]).then_inc(qg, 16)
                ph.add("gpsimd", fl)
                return I("qg", 16 * 8)[1]

            gate_wq6 = wload6(Wq, 0)
            bank_ev6 = {i: 0 for i in range(8)}
            for mo in range(8):
                grp = mo % 2
                gates = [bank_ev6[4 * grp + tc] for tc in range(4)]
                _, t_mm = I("p6t")

                def fq6(t, mo=mo, grp=grp, gates=gates, gate_w=gate_wq6):
                    if mo == 0:
                        t.wait_ge(qg, gate_w)
                    gmax = max(gates)
                    if gmax:
                        t.wait_ge(SEM("p6v"), gmax)
                    for ki in range(8):
                        for tc in range(4):
                            t.matmul(PS6[4 * grp + tc][:, 0:512],
                                     wL1[:, ki * 1024 + mo * 128: ki * 1024 + (mo + 1) * 128],
                                     fmA[:, ki * T + tc * 512: ki * T + (tc + 1) * 512],
                                     start=(ki == 0), stop=(ki == 7))
                    t.drain().then_inc(SEM("p6t"), 1)
                ph.add("tensor", fq6)

                def fq6v(v, mo=mo, grp=grp, t_mm=t_mm):
                    v.wait_ge(SEM("p6t"), t_mm)
                    for tc in range(4):
                        ic = v.tensor_copy(qfm[:, mo * T + tc * 512: mo * T + (tc + 1) * 512],
                                           PS6[4 * grp + tc][:, 0:512])
                        ic.then_inc(SEM("p6v"), 1)
                for tc in range(4):
                    bank_ev6[4 * grp + tc] = cnts["p6v"] + tc + 1
                I("p6v", 4)
                ph.add("vector", fq6v)
            t_q6t = cnts["p6t"]

            gate_wk6 = wload6(Wk, t_q6t)
            kst_dma = {0: 0, 1: 0}
            for mo in range(8):
                grp = mo % 2
                gates = [bank_ev6[4 * grp + tc] for tc in range(4)]
                _, t_mm = I("p6t")

                def fk6(t, mo=mo, grp=grp, gates=gates, gate_w=gate_wk6):
                    if mo == 0:
                        t.wait_ge(qg, gate_w)
                    gmax = max(gates)
                    if gmax:
                        t.wait_ge(SEM("p6v"), gmax)
                    for ki in range(8):
                        for tc in range(4):
                            t.matmul(PS6[4 * grp + tc][:, 0:512],
                                     wL1[:, ki * 1024 + mo * 128: ki * 1024 + (mo + 1) * 128],
                                     fmA[:, ki * T + tc * 512: ki * T + (tc + 1) * 512],
                                     start=(ki == 0), stop=(ki == 7))
                    t.drain().then_inc(SEM("p6t"), 1)
                ph.add("tensor", fk6)

                def fk6v(v, mo=mo, grp=grp, t_mm=t_mm, dgate=kst_dma[mo % 2]):
                    v.wait_ge(SEM("p6t"), t_mm)
                    if dgate:
                        v.wait_ge(qs, dgate)
                    for tc in range(4):
                        ic = v.tensor_copy(kst6[:, (mo % 2) * 2048 + tc * 512: (mo % 2) * 2048 + (tc + 1) * 512],
                                           PS6[4 * grp + tc][:, 0:512])
                        ic.then_inc(SEM("p6v"), 1)
                for tc in range(4):
                    bank_ev6[4 * grp + tc] = cnts["p6v"] + tc + 1
                t_kv = I("p6v", 4)[1]
                ph.add("vector", fk6v)
                _, t_kd = I("qs", 16)

                def fk6s(sy, mo=mo, t_kv=t_kv):
                    sy.wait_ge(SEM("p6v"), t_kv)
                    sy.dma_start(out=d_exi[:, mo * T:(mo + 1) * T],
                                 in_=kst6[:, (mo % 2) * 2048:(mo % 2 + 1) * 2048]).then_inc(qs, 16)
                ph.add("sync", fk6s)
                kst_dma[mo % 2] = t_kd
            t_k6t = cnts["p6t"]

            gate_wv6 = wload6(Wv, t_k6t)
            _, t_vm6 = I("ge")

            def fvm6(g):
                g.memset(vst6[:, 0:1040], 1.0)
                g.memset(vst6[:, 1040:2080], 1.0).then_inc(SEM("ge"), 1)
            ph.add("gpsimd", fvm6)
            vst_dma = {0: 0, 1: 0}
            for j in range(16):
                gates = [bank_ev6[(2 * j + half) % 4] for half in range(2)]
                _, t_mm = I("p6t")

                def fv6(t, j=j, gates=gates, gate_w=gate_wv6, t_vm6=t_vm6):
                    if j == 0:
                        t.wait_ge(qg, gate_w)
                        t.wait_ge(SEM("ge"), t_vm6)
                    gmax = max(gates)
                    if gmax:
                        t.wait_ge(SEM("p6v"), gmax)
                    for half in range(2):
                        ps = PS6[(2 * j + half) % 4]
                        for ki in range(8):
                            t.matmul(ps[:, 0:512],
                                     fmA[:, ki * T + j * 128: ki * T + (j + 1) * 128],
                                     wL1[:, ki * 1024 + half * 512: ki * 1024 + (half + 1) * 512],
                                     start=(ki == 0), stop=(ki == 7))
                    t.drain().then_inc(SEM("p6t"), 1)
                ph.add("tensor", fv6)

                def fv6v(v, j=j, t_mm=t_mm, dgate=vst_dma[j % 2]):
                    v.wait_ge(SEM("p6t"), t_mm)
                    if dgate:
                        v.wait_ge(qs, dgate)
                    for half in range(2):
                        ps = PS6[(2 * j + half) % 4]
                        base = (j % 2) * 1040 + half * 520
                        ic = v.tensor_copy(
                            vst6.ap()[:, base:base + 520].rearrange("p (h c) -> p h c", c=65)[:, :, 0:64],
                            ps.ap()[:, 0:512].rearrange("p (h c) -> p h c", c=64))
                        ic.then_inc(SEM("p6v"), 1)
                for half in range(2):
                    bank_ev6[(2 * j + half) % 4] = cnts["p6v"] + half + 1
                t_vv = I("p6v", 2)[1]
                ph.add("vector", fv6v)
                _, t_vd = I("qs", 16)

                def fv6s(sy, j=j, t_vv=t_vv):
                    sy.wait_ge(SEM("p6v"), t_vv)
                    sy.dma_start(out=d_exi[:, EXK + j * 1040: EXK + (j + 1) * 1040],
                                 in_=vst6[:, (j % 2) * 1040:(j % 2 + 1) * 1040]).then_inc(qs, 16)
                ph.add("sync", fv6s)
                vst_dma[j % 2] = t_vd
            t_all_st = cnts["qs"]
            _, t_cc6 = I("cc6")

            def fex(g, t_all_st=t_all_st):
                g.wait_ge(qs, t_all_st)
                g.collective_compute("AllGather", ALU.bypass, replica_groups=pairs,
                                     ins=[d_exi], outs=[d_exo]).then_inc(SEM("cc6"), 1)
            ph.add("gpsimd", fex)
            ph.emit(); CKPT()
            es_ps6a.close()
            es_p6a.close()
            es_xf.close()
            es_fm1.close()

            es_kv = ES()
            kful = es_kv.enter_context(nc.sbuf_tensor("kful", [128, 8 * S], BF16))
            vful = es_kv.enter_context(nc.sbuf_tensor("vful", [128, NKT * 1040], BF16))
            es_ps6b = ES()
            ps6s = [es_ps6b.enter_context(nc.psum_tensor("ps6s%d" % i, [128, 512], F32))
                    for i in range(2)]
            psv6 = es_ps6b.enter_context(nc.psum_tensor("psv6", [65, 16], F32))

            ph = Ph()
            _, t_asm = I("qg", 16 * (128 + 32))

            def fasm(g, t_cc6=t_cc6):
                g.wait_ge(SEM("cc6"), t_cc6)
                for ki in range(8):
                    for gb in range(16):
                        o = gb & 1
                        i_own = gb >> 1
                        g.dma_start(
                            out=kful[:, ki * S + gb * 256: ki * S + (gb + 1) * 256],
                            in_=d_exo[o * 128:(o + 1) * 128,
                                      ki * T + i_own * 256: ki * T + (i_own + 1) * 256]
                        ).then_inc(qg, 16)
                for kt in range(NKT):
                    o = (kt >> 1) & 1
                    j = ((kt >> 2) << 1) | (kt & 1)
                    g.dma_start(
                        out=vful[:, kt * 1040:(kt + 1) * 1040],
                        in_=d_exo[o * 128:(o + 1) * 128, EXK + j * 1040: EXK + (j + 1) * 1040]
                    ).then_inc(qg, 16)
            ph.add("gpsimd", fasm)

            SEM("p6bt"); SEM("p6bv")
            for kt in range(NKT):
                gate_ev = cnts["p6bv"] - 1 if kt >= 2 else 0
                _, t_mm = I("p6bt")

                def fvt6(t, kt=kt, t_asm=t_asm, gate_ev=gate_ev):
                    if kt == 0:
                        t.wait_ge(qg, t_asm)
                    if gate_ev:
                        t.wait_ge(SEM("p6bv"), gate_ev)
                    for h in range(H):
                        t.matmul(psv6[:, h:h + 1],
                                 vful[:, kt * 1040 + h * 65: kt * 1040 + h * 65 + 65],
                                 onescb[:], start=True, stop=True)
                    t.drain().then_inc(SEM("p6bt"), 1)
                ph.add("tensor", fvt6)
                _, t_ev = I("p6bv")

                def fvt6v(v, kt=kt, t_mm=t_mm):
                    v.wait_ge(SEM("p6bt"), t_mm)
                    ic = v.tensor_copy(
                        vtsb.ap()[0:65, :].rearrange("p (h k) -> p h k", k=NKT)[:, :, kt:kt + 1],
                        psv6.ap()[0:65, :].broadcast_to([65, 16, 1]))
                    ic.then_inc(SEM("p6bv"), 1)
                ph.add("vector", fvt6v)
            _, t_vs = I("p6bv")

            def fvs6(v, need=cnts["p6bv"] - 1):
                v.wait_ge(SEM("p6bv"), need)
                v.tensor_reduce(vsumb[:, 0:16],
                                vtsb.ap()[:, :].rearrange("p (h k) -> p h k", k=NKT),
                                AX.X, ALU.add)
                for t_ in range(NT):
                    kb = band_kbase(t_) // 128
                    v.tensor_reduce(scr[:, t_ * 16:(t_ + 1) * 16],
                                    vtsb.ap()[:, :].rearrange("p (h k) -> p h k", k=NKT)[:, :, kb:kb + 5],
                                    AX.X, ALU.add)
                for t_ in range(NT):
                    v.tensor_sub(cbuf[:, t_ * 16:(t_ + 1) * 16], vsumb[:, 0:16],
                                 scr[:, t_ * 16:(t_ + 1) * 16])
                v.engine_nop().then_inc(SEM("p6bv"), 1)
            ph.add("vector", fvs6)

            for t_ in range(NT):
                for h in range(H):
                    ch, ro = h >> 1, 64 * (h & 1)
                    red_ticks = []
                    for kc in range(8):
                        pp = kc % 2
                        gate_red = red_ticks[kc - 2] if kc >= 2 else 0
                        _, t_mm = I("p6bt")

                        def frm_t(t, t_=t_, h=h, ch=ch, ro=ro, kc=kc, pp=pp, gate_red=gate_red):
                            if gate_red:
                                t.wait_ge(SEM("p6bv"), gate_red)
                            t.matmul(ps6s[pp][:, 0:512],
                                     qfm[ro:ro + 64, ch * T + t_ * 128: ch * T + (t_ + 1) * 128],
                                     kful[ro:ro + 64, ch * S + kc * 512: ch * S + (kc + 1) * 512],
                                     start=True, stop=True)
                            t.drain().then_inc(SEM("p6bt"), 1)
                        ph.add("tensor", frm_t)
                        _, t_red = I("p6bv")

                        def frm_v(v, kc=kc, pp=pp, t_mm=t_mm):
                            v.wait_ge(SEM("p6bt"), t_mm)
                            v.reduce_max(scr[:, kc:kc + 1], ps6s[pp][:, 0:512],
                                         AX.X).then_inc(SEM("p6bv"), 1)
                        ph.add("vector", frm_v)
                        red_ticks.append(cnts["p6bv"])
                    _, t_fin = I("p6bv")

                    def frm_f(v, t_=t_, h=h, need=red_ticks[-1]):
                        v.wait_ge(SEM("p6bv"), need)
                        v.reduce_max(mbuf[:, t_ * 16 + h: t_ * 16 + h + 1], scr[:, 0:8],
                                     AX.X).then_inc(SEM("p6bv"), 1)
                    ph.add("vector", frm_f)
            _, t_nm8 = I("p6bv")

            def fnm8(s_, need=cnts["p6bv"] - 1):
                s_.wait_ge(SEM("p6bv"), need)
                s_.activation(nm8[:], mbuf[:], AF.Copy, bias=0.0,
                              scale=-0.125).then_inc(SEM("p6bv"), 1)
            ph.add("scalar", fnm8)
            ph.emit(); CKPT()
            es_ps6b.close()

            es_ps6c = ES()
            bmr = es_ps6c.enter_context(nc.sbuf_tensor("bmr", [128, 2 * BW], BF16))
            ps6b = [es_ps6c.enter_context(nc.psum_tensor("ps6b%d" % i, [128, BW], F32))
                    for i in range(2)]
            psc6 = [es_ps6c.enter_context(nc.psum_tensor("psc6%d" % i, [65, 128], F32))
                    for i in range(2)]
            pzb6 = es_ps6c.enter_context(nc.psum_tensor("pzb6", [64, 128], F32))

            ph = Ph()
            SEM("c6t"); SEM("c6e"); SEM("c6m"); SEM("c6f"); SEM("c6a"); SEM("c6z")
            SEM("c6zb"); SEM("c6ev")
            bm_used = {0: 0, 1: 0}
            e1_used = {0: 0, 1: 0}
            e2_used = {0: 0, 1: 0}
            eT_used = {0: 0, 1: 0}
            psb_used = {0: 0, 1: 0}
            psc_used = {0: 0, 1: 0}
            stg_used = {0: 0, 1: 0}
            scr_zb6 = 0
            zb_ev6 = 0
            for t_ in range(NT):
                kb = band_kbase(t_)
                _, t_bm = I("qg", 16)

                def fbm(g, t_=t_, mu=bm_used[t_ % 2]):
                    if mu:
                        g.wait_ge(SEM("c6m"), mu)
                    g.dma_start(out=bmr[:, (t_ % 2) * BW:(t_ % 2 + 1) * BW],
                                in_=bmask[t_, :].rearrange("(p q) -> p q", p=128)
                                ).then_inc(qg, 16)
                ph.add("gpsimd", fbm)
                for h in range(H):
                    ch, ro = h >> 1, 64 * (h & 1)
                    pp = h % 2
                    _, t_mm = I("c6t")

                    def fb_t(t, t_=t_, h=h, ch=ch, ro=ro, pp=pp, kb=kb, eg=psb_used[pp]):
                        if eg:
                            t.wait_ge(SEM("c6e"), eg)
                        t.matmul(ps6b[pp][:, 0:512],
                                 qfm[ro:ro + 64, ch * T + t_ * 128: ch * T + (t_ + 1) * 128],
                                 kful[ro:ro + 64, ch * S + kb: ch * S + kb + 512],
                                 start=True, stop=True)
                        t.matmul(ps6b[pp][:, 512:BW],
                                 qfm[ro:ro + 64, ch * T + t_ * 128: ch * T + (t_ + 1) * 128],
                                 kful[ro:ro + 64, ch * S + kb + 512: ch * S + kb + BW],
                                 start=True, stop=True)
                        t.drain().then_inc(SEM("c6t"), 1)
                    ph.add("tensor", fb_t)
                    _, t_e1 = I("c6e")

                    def fb_e1(s_, t_=t_, h=h, pp=pp, t_mm=t_mm, eg=e1_used[pp]):
                        s_.wait_ge(SEM("c6t"), t_mm)
                        if eg:
                            s_.wait_ge(SEM("c6m"), eg)
                        s_.activation(scb[:, pp * BW:(pp + 1) * BW], ps6b[pp][:, 0:BW],
                                      AF.Exp, bias=nm8[:, t_ * 16 + h: t_ * 16 + h + 1],
                                      scale=0.125).then_inc(SEM("c6e"), 1)
                    ph.add("scalar", fb_e1)
                    psb_used[pp] = t_e1
                    _, t_msk = I("c6m")

                    def fb_m(v, t_=t_, pp=pp, t_e1=t_e1, t_bm=t_bm):
                        v.wait_ge(SEM("c6e"), t_e1)
                        v.wait_ge(qg, t_bm)
                        v.tensor_tensor(scb[:, pp * BW:(pp + 1) * BW],
                                        scb[:, pp * BW:(pp + 1) * BW],
                                        bmr[:, (t_ % 2) * BW:(t_ % 2 + 1) * BW],
                                        ALU.mult).then_inc(SEM("c6m"), 1)
                    ph.add("vector", fb_m)
                    e1_used[pp] = t_msk
                    bm_used[t_ % 2] = t_msk
                    _, t_e2 = I("c6f")

                    def fb_e2(s_, pp=pp, t_msk=t_msk, eg=e2_used[pp]):
                        s_.wait_ge(SEM("c6m"), t_msk)
                        if eg:
                            s_.wait_ge(qs, eg)
                        s_.activation(scb[:, 1280 + pp * BW: 1280 + (pp + 1) * BW],
                                      scb[:, pp * BW:(pp + 1) * BW],
                                      AF.Exp, bias=consts[:, 3:4],
                                      scale=1.0).then_inc(SEM("c6f"), 1)
                    ph.add("scalar", fb_e2)
                    _, t_tr = I("qs", 16 * 5)

                    def fb_tr(sy, pp=pp, t_e2=t_e2, eg=eT_used[pp]):
                        sy.wait_ge(SEM("c6f"), t_e2)
                        if eg:
                            sy.wait_ge(SEM("c6a"), eg)
                        for j in range(5):
                            sy.dma_start(
                                out=scb[:, 2560 + pp * BW + j * 128: 2560 + pp * BW + (j + 1) * 128],
                                in_=scb[:, 1280 + pp * BW + j * 128: 1280 + pp * BW + (j + 1) * 128],
                                transpose=True).then_inc(qs, 16)
                    ph.add("sync", fb_tr)
                    e2_used[pp] = t_tr
                    _, t_av = I("c6a")

                    def fb_av(t, t_=t_, h=h, pp=pp, kb=kb, t_tr=t_tr, eg=psc_used[pp]):
                        t.wait_ge(qs, t_tr)
                        if eg:
                            t.wait_ge(SEM("c6ev"), eg)
                        for j in range(5):
                            kt = kb // 128 + j
                            t.matmul(psc6[pp][:, 0:128],
                                     vful[:, kt * 1040 + h * 65: kt * 1040 + h * 65 + 65],
                                     scb[:, 2560 + pp * BW + j * 128: 2560 + pp * BW + (j + 1) * 128],
                                     start=(j == 0), stop=(j == 4))
                        t.drain().then_inc(SEM("c6a"), 1)
                    ph.add("tensor", fb_av)
                    eT_used[pp] = t_av
                    _, t_z = I("c6z")

                    def fb_z(v, pp=pp, t_av=t_av, sg=scr_zb6):
                        v.wait_ge(SEM("c6a"), t_av)
                        if sg:
                            v.wait_ge(SEM("c6zb"), sg)
                        v.tensor_scalar_add(scr[0:1, 32:160], psc6[pp][64:65, 0:128],
                                            float(S - BW))
                        v.reciprocal(scr[0:1, 32:160], scr[0:1, 32:160]).then_inc(SEM("c6z"), 1)
                    ph.add("vector", fb_z)
                    _, t_zb = I("c6zb")

                    def fb_zt(t, t_z=t_z, eg=zb_ev6):
                        t.wait_ge(SEM("c6z"), t_z)
                        if eg:
                            t.wait_ge(SEM("c6ev"), eg)
                        t.matmul(pzb6[:, 0:128], ones1f[0:1, 0:64], scr[0:1, 32:160],
                                 start=True, stop=True)
                        t.drain().then_inc(SEM("c6zb"), 1)
                    ph.add("tensor", fb_zt)
                    scr_zb6 = t_zb
                    _, t_ev = I("c6ev")

                    def fb_ev(v, t_=t_, h=h, pp=pp, t_zb=t_zb, dg=stg_used[pp]):
                        v.wait_ge(SEM("c6zb"), t_zb)
                        if dg:
                            v.wait_ge(qs, dg)
                        stg = scb[0:64, 3840 + pp * 128: 3840 + (pp + 1) * 128]
                        v.tensor_single_scalar(stg, psc6[pp][0:64, 0:128],
                                               cbuf[0:64, t_ * 16 + h: t_ * 16 + h + 1], ALU.add)
                        v.tensor_tensor(stg, stg, pzb6[0:64, 0:128],
                                        ALU.mult).then_inc(SEM("c6ev"), 1)
                    ph.add("vector", fb_ev)
                    psc_used[pp] = t_ev
                    zb_ev6 = t_ev
                    _, t_st = I("qs", 16)

                    def fb_st(sy, t_=t_, h=h, pp=pp, t_ev=t_ev):
                        sy.wait_ge(SEM("c6ev"), t_ev)
                        sy.dma_start(out=d_ctx[h * 64:(h + 1) * 64, t_ * 128:(t_ + 1) * 128],
                                     in_=scb[0:64, 3840 + pp * 128: 3840 + (pp + 1) * 128]
                                     ).then_inc(qs, 16)
                    ph.add("sync", fb_st)
                    stg_used[pp] = t_st
            ph.emit(); CKPT()
            es_ps6c.close()
            es_kv.close()
            es_qfm.close()

            es_xf2 = ES()
            xf2 = es_xf2.enter_context(nc.sbuf_tensor("xf2", [128, NT * D], F32))
            es_p6d = ES()
            wo6 = es_p6d.enter_context(nc.sbuf_tensor("wo6", [128, 8 * 1024], BF16))
            ctxl6 = es_p6d.enter_context(nc.sbuf_tensor("ctxl6", [128, 2 * 1024], BF16))
            es_ps6d = ES()
            ps6d = [es_ps6d.enter_context(nc.psum_tensor("ps6d%d" % i, [128, 1024], F32))
                    for i in range(2)]

            def resid_x2(g, ti, dst):
                g.dma_start(out=dst, in_=d_x2[:, ti * D:(ti + 1) * D]).then_inc(qg, 16)

            ph = phase_proj_ln(Wo, zrow, d_ctx, resid_x2, None, "p6d",
                               wo6, ctxl6, ps6d, None, xf2, do_tr=False)
            ph.emit(); CKPT()
            es_ps6d.close()
            es_p6d.close()

            es_p7 = ES()
            alloc_quant(es_p7, "x")
            es_xqf = ES()
            xqf = es_xqf.enter_context(nc.sbuf_tensor("xqf", [128, 8 * T], BF16, side="right"))
            wgt6 = es_p7.enter_context(nc.sbuf_tensor("wgt6", [128, 2 * 8 * 128], BF16))
            es_ps7 = ES()
            PS7 = [es_ps7.enter_context(nc.psum_tensor("ps7_%d" % i, [128, 512], F32))
                   for i in range(4)]

            ph = Ph()
            SEM("x7")
            xq_tr = {0: 0, 1: 0}
            for ti in range(NT):
                pp = ti % 2
                (s3t3, _) = sched_quant(
                    ph,
                    (lambda ti=ti: xf2.ap()[:, ti * D:(ti + 1) * D]),
                    (lambda pp=pp: scb.ap()[:, pp * 1024:(pp + 1) * 1024]),
                    D, 128, None, "xq",
                    pre_wait=((qs, xq_tr[pp]) if xq_tr[pp] else None))
                _, t_tr = I("qs", 16 * 8)

                def fxtr(sy, ti=ti, pp=pp, s3t3=s3t3):
                    sy.wait_ge(*s3t3)
                    for fc in range(8):
                        sy.dma_start(
                            out=xqf[:, fc * T + ti * 128: fc * T + (ti + 1) * 128],
                            in_=scb[:, pp * 1024 + fc * 128: pp * 1024 + (fc + 1) * 128],
                            transpose=True).then_inc(qs, 16)
                ph.add("sync", fxtr)
                xq_tr[pp] = t_tr
            t_xq = cnts["qs"]

            SEM("g7t"); SEM("g7s"); SEM("g7v")
            hst_dma = {0: 0, 1: 0}
            for ncol in range(NC22):
                gate_mm = cnts["g7t"] if ncol >= 1 else 0
                _, t_wl = I("qs", 16 * 16)

                def fwl(g_, ncol=ncol, gate_mm=gate_mm, t_ccw=cnts["ccw"]):
                    if ncol == 0:
                        g_.wait_ge(SEM("ccw"), t_ccw)
                        g_.wait_ge(qs, t_xq)
                    if gate_mm:
                        g_.wait_ge(SEM("g7t"), gate_mm)
                    for hc in range(8):
                        g_.dma_start(out=wgt6[:, hc * 128:(hc + 1) * 128],
                                     in_=d_wgo[ncol * 128:(ncol + 1) * 128,
                                               hc * 128:(hc + 1) * 128],
                                     transpose=True).then_inc(qs, 16)
                    for hc in range(8):
                        g_.dma_start(out=wgt6[:, 1024 + hc * 128: 1024 + (hc + 1) * 128],
                                     in_=d_wuo[ncol * 128:(ncol + 1) * 128,
                                               hc * 128:(hc + 1) * 128],
                                     transpose=True).then_inc(qs, 16)
                ph.add("sync", fwl)
                for tc in range(4):
                    n = ncol * 4 + tc
                    gate_ev = cnts["g7v"] - 1 if n >= 2 else 0
                    _, t_mm = I("g7t")

                    def fgu(t, ncol=ncol, tc=tc, t_wl=t_wl, gate_ev=gate_ev, pp=tc % 2):
                        t.wait_ge(qs, t_wl)
                        if gate_ev:
                            t.wait_ge(SEM("g7v"), gate_ev)
                        for hc in range(8):
                            t.matmul(PS7[2 * pp][:, 0:512],
                                     wgt6[:, hc * 128:(hc + 1) * 128],
                                     xqf[:, hc * T + tc * 512:(hc) * T + (tc + 1) * 512],
                                     start=(hc == 0), stop=(hc == 7))
                        for hc in range(8):
                            t.matmul(PS7[2 * pp + 1][:, 0:512],
                                     wgt6[:, 1024 + hc * 128: 1024 + (hc + 1) * 128],
                                     xqf[:, hc * T + tc * 512:(hc) * T + (tc + 1) * 512],
                                     start=(hc == 0), stop=(hc == 7))
                        t.drain().then_inc(SEM("g7t"), 1)
                    ph.add("tensor", fgu)
                    _, t_sil = I("g7s")

                    def fsil(s_, t_mm=t_mm, pp=tc % 2):
                        s_.wait_ge(SEM("g7t"), t_mm)
                        s_.activation(scb[:, 2048 + pp * 512: 2048 + (pp + 1) * 512],
                                      PS7[2 * pp][:, 0:512], AF.Silu,
                                      bias=consts[:, 3:4], scale=1.0).then_inc(SEM("g7s"), 1)
                    ph.add("scalar", fsil)
                    _, t_hv = I("g7v")

                    def fhv(v, t_sil=t_sil, pp=tc % 2, dg=hst_dma[tc % 2]):
                        v.wait_ge(SEM("g7s"), t_sil)
                        if dg:
                            v.wait_ge(qs, dg)
                        v.tensor_tensor(scb[:, 3072 + pp * 512: 3072 + (pp + 1) * 512],
                                        scb[:, 2048 + pp * 512: 2048 + (pp + 1) * 512],
                                        PS7[2 * pp + 1][:, 0:512],
                                        ALU.mult).then_inc(SEM("g7v"), 1)
                    ph.add("vector", fhv)
                    _, t_hd = I("qs", 16)

                    def fhd(sy, ncol=ncol, tc=tc, t_hv=t_hv, pp=tc % 2):
                        sy.wait_ge(SEM("g7v"), t_hv)
                        sy.dma_start(out=d_h[:, ncol * T + tc * 512: ncol * T + (tc + 1) * 512],
                                     in_=scb[:, 3072 + pp * 512: 3072 + (pp + 1) * 512]
                                     ).then_inc(qs, 16)
                    ph.add("sync", fhd)
                    hst_dma[tc % 2] = t_hd
            ph.emit(); CKPT()
            es_xqf.close()

            es_wd = ES()
            wdT = es_wd.enter_context(nc.sbuf_tensor("wdT", [128, NC22 * 1024], BF16))
            hqf = es_wd.enter_context(nc.sbuf_tensor("hqf", [128, NC22 * 128], BF16))
            hbuf = es_wd.enter_context(nc.sbuf_tensor("hbuf", [128, 2 * DI], BF16))
            es_ps7d = ES()
            ps7d = [es_ps7d.enter_context(nc.psum_tensor("ps7d%d" % i, [128, 1024], F32))
                    for i in range(2)]

            ph = Ph()
            if DBG:
                _, _t = I("qg", 16 * 4)

                def fdbg(g):
                    g.dma_start(out=o_x1, in_=d_x1).then_inc(qg, 16)
                    g.dma_start(out=o_x2, in_=d_x2).then_inc(qg, 16)
                    g.dma_start(out=o_ctx, in_=d_ctx).then_inc(qg, 16)
                    for t_ in range(NT):
                        pass
                    g.dma_start(out=o_x3[:, 0:NT * D], in_=xf2[:, 0:NT * D]).then_inc(qg, 16)
                ph.add("gpsimd", fdbg)
            _, t_wdl = I("qs", 16 * NC22 * 8)

            def fwdl(sy):
                for ic in range(NC22):
                    for nch in range(8):
                        sy.dma_start(out=wdT[:, ic * 1024 + nch * 128: ic * 1024 + (nch + 1) * 128],
                                     in_=d_wdo[nch * 128:(nch + 1) * 128,
                                               ic * 128:(ic + 1) * 128],
                                     transpose=True).then_inc(qs, 16)
            ph.add("sync", fwdl)

            SEM("d7t"); SEM("hq1")
            ln_a7 = {0: 0, 1: 0}
            hq_tr = 0
            prev_qtr = 0
            out_ticks = []
            for ti in range(NT):
                pp = ti % 2
                _, t_hl = I("qs", 16 * NC22)

                def fhl(sy, ti=ti, hq=hq_tr):
                    if hq:
                        sy.wait_ge(SEM("hq1"), hq)
                    for ic in range(NC22):
                        sy.dma_start(out=hbuf[:, ic * 128:(ic + 1) * 128],
                                     in_=d_h[:, ic * T + ti * 128: ic * T + (ti + 1) * 128],
                                     transpose=True).then_inc(qs, 16)
                ph.add("sync", fhl)
                qs_snapshot = cnts["qs"]
                (s3t3, s1t1) = sched_quant(
                    ph,
                    (lambda: hbuf.ap()[:, 0:DI]),
                    (lambda: hbuf.ap()[:, DI:2 * DI]),
                    DI, 128, (qs, qs_snapshot), "hq",
                    pre_wait=((qs, prev_qtr) if ti > 0 else None))
                hq_tr = s1t1[1]
                _, t_qtr = I("qs", 16 * NC22)
                dn_prev = cnts["d7t"]

                def fqtr(sy, ti=ti, s3t3=s3t3, dn_prev=dn_prev):
                    sy.wait_ge(*s3t3)
                    if dn_prev:
                        sy.wait_ge(SEM("d7t"), dn_prev)
                    for ic in range(NC22):
                        sy.dma_start(out=hqf[:, ic * 128:(ic + 1) * 128],
                                     in_=hbuf[:, DI + ic * 128: DI + (ic + 1) * 128],
                                     transpose=True).then_inc(qs, 16)
                ph.add("sync", fqtr)
                prev_qtr = t_qtr
                gate_ps = ln_a7[pp]
                _, t_dn = I("d7t")

                def fdn(t, ti=ti, pp=pp, t_qtr=t_qtr, gate_ps=gate_ps, t_wdl=t_wdl):
                    t.wait_ge(qs, t_qtr)
                    if ti == 0:
                        t.wait_ge(qs, t_wdl)
                    if gate_ps:
                        t.wait_ge(SEM("p7a"), gate_ps)
                    for ic in range(NC22):
                        for nh in range(2):
                            t.matmul(ps7d[pp][:, nh * 512:(nh + 1) * 512],
                                     hqf[:, ic * 128:(ic + 1) * 128],
                                     wdT[:, ic * 1024 + nh * 512: ic * 1024 + (nh + 1) * 512],
                                     start=(ic == 0), stop=(ic == NC22 - 1))
                    t.drain().then_inc(SEM("d7t"), 1)
                ph.add("tensor", fdn)
                sc_, tc_ = sched_ln(
                    ph,
                    (lambda pp=pp: ps7d[pp][:, 0:1024]),
                    (lambda ti=ti: xf2[:, ti * D:(ti + 1) * D]),
                    xf2[:, ti * D:(ti + 1) * D],
                    None,
                    None,
                    "p7",
                    extra_wait=(lambda t_dn=t_dn: (SEM("d7t"), t_dn)))
                ln_a7[pp] = cnts["p7a"]
                _, t_out = I("qs", 16)

                def fout(sy, ti=ti, tc_=tc_):
                    sy.wait_ge(SEM("p7c"), tc_)
                    sy.dma_start(out=out_t[ti * 128:(ti + 1) * 128, :],
                                 in_=xf2[:, ti * D:(ti + 1) * D]).then_inc(qs, 16)
                ph.add("sync", fout)
                out_ticks.append(cnts["qs"])
            ph.emit(); CKPT()
            es_ps7d.close()
            es_wd.close()
            es_p7.close()
            es_xf2.close()

        except StopBuild:
            for e in reversed(es_list):
                try:
                    e.close()
                except Exception:
                    pass
        if phct[0] < 99 and KPH < 99:
            ph = Ph()

            def fzro(g):
                g.memset(scr[:, 0:1024], 0.0).then_inc(SEM("ge"), 1)
            tz = I("ge")[1]
            ph.add("gpsimd", fzro)

            def fdump(sy, tz=tz):
                sy.wait_ge(SEM("ge"), tz)
                for ti in range(NT):
                    sy.dma_start(out=out_t[ti * 128:(ti + 1) * 128, :],
                                 in_=scr[:, 0:1024]).then_inc(qs, 16)
            I("qs", 16 * NT)
            ph.add("sync", fdump)
            ph.emit()
    return nc


_NC_CACHE = {}


def _masks_for_core(par):
    cm = np.zeros((NBLK * 4, 128, 256), dtype=np.float32)
    for bi in range(NBLK):
        g = 2 * bi + par
        for j in range(4):
            kt = 4 * bi + j
            ii = kt * 128 + np.arange(128)[:, None]
            jj = g * 256 + np.arange(256)[None, :]
            cm[bi * 4 + j] = (ii <= jj).astype(np.float32)
    bm = np.zeros((NT, 128, BW), dtype=np.float32)
    for t_ in range(NT):
        gq = 4 * (t_ >> 1) + 2 * par + (t_ & 1)
        kb = band_kbase(t_)
        ii = gq * 128 + np.arange(128)[:, None]
        jj = kb + np.arange(BW)[None, :]
        d = jj - ii
        bm[t_] = ((d >= -128) & (d <= 127) & (jj < S)).astype(np.float32)
    bf = ml_dtypes.bfloat16
    return (cm.reshape(NBLK * 4, 128 * 256).astype(bf),
            bm.reshape(NT, 128 * BW).astype(bf))


def kernel(**inputs):
    ins = {k: np.asarray(v) for k, v in inputs.items()}
    x0 = ins["L0_attn_x"]
    if "nc" not in _NC_CACHE:
        _NC_CACHE["nc"] = build()
    nc = _NC_CACHE["nc"]
    in_maps = []
    for c in range(8):
        b, par = c // 2, c % 2
        own_rows = np.concatenate(
            [np.arange((2 * i + par) * 256, (2 * i + par) * 256 + 256) for i in range(8)])
        cm, bm = _masks_for_core(par)
        m = {
            "x0_own": np.ascontiguousarray(x0[b][own_rows]),
            "x0_full": np.ascontiguousarray(x0[b]),
            "w_qkv": ins["L0_attn_w_qkv"], "b_qkv": ins["L0_attn_b_qkv"],
            "w_out": ins["L0_attn_w_out"], "b_out": ins["L0_attn_b_out"],
            "cn_w1": ins["L0_ffn_cn_w1"], "cn_b1": ins["L0_ffn_cn_b1"],
            "cn_w2": ins["L0_ffn_cn_w2"], "cn_b2": ins["L0_ffn_cn_b2"],
            "cn_w3": ins["L0_ffn_cn_w3"], "cn_b3": ins["L0_ffn_cn_b3"],
            "cn_w4": ins["L0_ffn_cn_w4"], "cn_b4": ins["L0_ffn_cn_b4"],
            "gc_w1": ins["L0_ffn_gc_w1"], "gc_b1": ins["L0_ffn_gc_b1"],
            "gc_w2": ins["L0_ffn_gc_w2"], "gc_b2": ins["L0_ffn_gc_b2"],
            "gc_w3": ins["L0_ffn_gc_w3"], "gc_b3": ins["L0_ffn_gc_b3"],
            "ar": ins["L0_ffn_adaptive_range"],
            "Wq": ins["L1_attn_Wq"], "Wk": ins["L1_attn_Wk"],
            "Wv": ins["L1_attn_Wv"], "Wo": ins["L1_attn_Wo"],
            "wg_sl": np.ascontiguousarray(ins["L1_ffn_w_gate"][c * 352:(c + 1) * 352]),
            "wu_sl": np.ascontiguousarray(ins["L1_ffn_w_up"][c * 352:(c + 1) * 352]),
            "wd_sl": np.ascontiguousarray(ins["L1_ffn_w_down"][c * 128:(c + 1) * 128]),
            "cmask": cm, "bmask": bm,
        }
        in_maps.append(m)
    res = run_bass_kernel_spmd(nc, in_maps, core_ids=list(range(8)))
    hook = globals().get("_DBG_HOOK")
    if hook is not None:
        hook(res)
    out = np.zeros((B, S, D), dtype=np.float32)
    for c in range(8):
        b, par = c // 2, c % 2
        o = res.results[c]["out"]
        for i in range(8):
            g = 2 * i + par
            out[b, g * 256:(g + 1) * 256] = o[i * 256:(i + 1) * 256]
    return out


# version 09
# speedup vs baseline: 1.2258x; 1.2258x over previous
import numpy as np
import ml_dtypes
from contextlib import ExitStack

import os
import concourse.bass as bass
import concourse.mybir as mybir
from concourse.bass_utils import run_bass_kernel_spmd

F32 = mybir.dt.float32
BF16 = mybir.dt.bfloat16
U8 = mybir.dt.uint8
AF = mybir.ActivationFunctionType
ALU = mybir.AluOpType
AX = mybir.AxisListType

B, S, D = 4, 4096, 1024
H, HD = 16, 64
T, NT, NBLK, NKT = 2048, 16, 8, 32
BW = 640
DI = 2816
NC22 = DI // 128
VC = float(2 ** 22 + 1)
MG = float(3 * 2 ** 22)
ENGS = ("tensor", "vector", "scalar", "gpsimd", "sync")


def band_kbase(t):
    gq0 = 4 * (t >> 1) + (t & 1)
    return min(max(gq0 * 128 - 128, 0), S - BW)


def build():
    nc = bass.Bass("TRN2", target_bir_lowering=False)

    def inp(name, shape, dt=F32):
        return nc.dram_tensor(name, shape, dt, kind="ExternalInput").ap()

    x0_own = inp("x0_own", [T, D])
    x0_full = inp("x0_full", [S, D])
    w_qkv = inp("w_qkv", [D, 3 * D]); b_qkv = inp("b_qkv", [3 * D])
    w_out = inp("w_out", [D, D]); b_out = inp("b_out", [D])
    cn_w1 = inp("cn_w1", [D, D]); cn_b1 = inp("cn_b1", [D])
    cn_w2 = inp("cn_w2", [D, 512]); cn_b2 = inp("cn_b2", [512])
    cn_w3 = inp("cn_w3", [512, 256]); cn_b3 = inp("cn_b3", [256])
    cn_w4 = inp("cn_w4", [256, 1]); cn_b4 = inp("cn_b4", [1])
    gc_w1 = inp("gc_w1", [D, 512]); gc_b1 = inp("gc_b1", [512])
    gc_w2 = inp("gc_w2", [512, 256]); gc_b2 = inp("gc_b2", [256])
    gc_w3 = inp("gc_w3", [256, 1]); gc_b3 = inp("gc_b3", [1])
    ar_in = inp("ar", [1])
    Wq = inp("Wq", [D, D]); Wk = inp("Wk", [D, D])
    Wv = inp("Wv", [D, D]); Wo = inp("Wo", [D, D])
    wg_sl = inp("wg_sl", [352, D]); wu_sl = inp("wu_sl", [352, D])
    wd_sl = inp("wd_sl", [128, DI])
    cmask = inp("cmask", [NBLK * 4, 128 * 256], BF16)
    identb = inp("identb", [128, 128], BF16)
    bmask = inp("bmask", [NT, 128 * BW], BF16)

    out_t = nc.dram_tensor("out", [T, D], F32, kind="ExternalOutput").ap()
    DBG = int(os.environ.get("DBG", "0"))
    if DBG:
        o_x1 = nc.dram_tensor("o_x1", [128, NT * D], F32, kind="ExternalOutput").ap()
        o_x2 = nc.dram_tensor("o_x2", [128, NT * D], F32, kind="ExternalOutput").ap()
        o_x3 = nc.dram_tensor("o_x3", [128, NT * D], F32, kind="ExternalOutput").ap()
        o_ctx = nc.dram_tensor("o_ctx", [D, T], BF16, kind="ExternalOutput").ap()
        o_p5 = nc.dram_tensor("o_p5", [128, 64], F32, kind="ExternalOutput").ap()
        o_p5b = nc.dram_tensor("o_p5b", [128, 8], BF16, kind="ExternalOutput").ap()
        o_p5c = nc.dram_tensor("o_p5c", [128, 24], BF16, kind="ExternalOutput").ap()

    def dram(name, shape, dt=BF16):
        return nc.dram_tensor(name, shape, dt).ap()

    d_v = dram("d_v", [128, NKT * 1040])
    d_x1 = dram("d_x1", [128, NT * D], F32)
    d_x2 = dram("d_x2", [128, NT * D], F32)
    d_ctx = dram("d_ctx", [D, T])
    d_h = dram("d_h", [128, NC22 * T])
    EXK = 8 * T; EXV = 16 * 1040; EXW = EXK + EXV + 8
    d_exi = dram("d_exi", [128, EXW]); d_exo = dram("d_exo", [256, EXW])
    d_gci = dram("d_gci", [128, 8], F32); d_gco = dram("d_gco", [128, 8], F32)
    d_wgi = dram("d_wgi", [352, D]); d_wgo = dram("d_wgo", [DI, D])
    d_wui = dram("d_wui", [352, D]); d_wuo = dram("d_wuo", [DI, D])
    d_wdi = dram("d_wdi", [128, DI]); d_wdo = dram("d_wdo", [D, DI])

    pairs = [[0, 1], [2, 3], [4, 5], [6, 7]]
    allg = [list(range(8))]

    sems, cnts = {}, {}
    stk = ExitStack()
    with stk:
        def SEM(n):
            if n not in sems:
                sems[n] = stk.enter_context(nc.semaphore(n))
                cnts[n] = 0
            return sems[n]

        def I(n, k=1):
            s = SEM(n)
            cnts[n] += k
            return s, cnts[n]

        def C(n):
            SEM(n)
            return cnts[n]

        sb = lambda n, s, d=F32: stk.enter_context(nc.sbuf_tensor(n, s, d))

        consts = sb("consts", [128, 8])
        ones1b = sb("ones1b", [1, 128], BF16)
        ones1f = sb("ones1f", [1, 128], F32)
        onescb = sb("onescb", [128, 1], BF16)
        bq_sb = sb("bq_sb", [128, 16])
        bv_row = sb("bv_row", [1, D], BF16)
        bo_row = sb("bo_row", [1, D], BF16)
        cnb1 = sb("cnb1", [128, 8]); cnb2 = sb("cnb2", [128, 4]); cnb3 = sb("cnb3", [128, 2])
        gcb1 = sb("gcb1", [128, 4]); gcb2 = sb("gcb2", [128, 2])
        b4r = sb("b4r", [128, 1]); gb3r = sb("gb3r", [128, 1]); arr = sb("arr", [128, 1])
        mbuf = sb("mbuf", [128, NT * H])
        nm8 = sb("nm8", [128, NT * H])
        vtsb = sb("vtsb", [128, NKT * H])
        vsumb = sb("vsumb", [128, H])
        cbuf = sb("cbuf", [128, NT * H])
        tfc = sb("tfc", [128, NT]); cplb = sb("cplb", [128, NT]); ncplb = sb("ncplb", [128, NT])
        gfr = sb("gfr", [128, 1]); a6r = sb("a6r", [128, 1]); bvr = sb("bvr", [128, 1])
        lnst = sb("lnst", [128, 12])
        gsc = sb("gsc", [128, 32])
        scr = sb("scr", [128, 2048])
        scb = sb("scb", [128, 4096], BF16)
        zrow = sb("zrow", [1, D], BF16)
        idsb = sb("idsb", [128, 128], BF16)
        QT = {}

        def alloc_quant(es, tagn):
            QT["xn"] = es.enter_context(nc.sbuf_tensor("qxn" + tagn, [128, DI], F32))
            QT["A"] = es.enter_context(nc.sbuf_tensor("qA" + tagn, [128, DI], F32))
            QT["B"] = es.enter_context(nc.sbuf_tensor("qB" + tagn, [128, DI], F32))
            QT["am"] = es.enter_context(nc.sbuf_tensor("qam" + tagn, [128, DI // 16], F32))
            QT["sc"] = es.enter_context(nc.sbuf_tensor("qsc" + tagn, [128, DI // 16], F32))
            QT["rc"] = es.enter_context(nc.sbuf_tensor("qrc" + tagn, [128, DI // 16], F32))
            QT["mk"] = es.enter_context(nc.sbuf_tensor("qmk" + tagn, [128, DI], U8))

        qs = SEM("qs"); qg = SEM("qg"); BAR = SEM("bar")
        nbar = [0]

        class Ph:
            def __init__(self):
                self.ops = {e: [] for e in ENGS}

            def add(self, e, fn):
                self.ops[e].append(fn)

            def emit(self):
                base = nbar[0] * len(ENGS)
                qsv, qgv = cnts["qs"], cnts["qg"]
                with nc.Block() as blk:
                    for ename in ENGS:
                        def mk(ename=ename):
                            def body(eng):
                                eng.wait_ge(BAR, base)
                                for fn in self.ops[ename]:
                                    fn(eng)
                                if ename == "sync":
                                    eng.wait_ge(qs, qsv)
                                if ename == "gpsimd":
                                    eng.wait_ge(qg, qgv)
                                eng.drain().then_inc(BAR, 1)
                            return body
                        getattr(blk, ename)(mk())
                cnts["bar"] += len(ENGS)
                nbar[0] += 1

        KPH = int(os.environ.get("KPH", "99"))
        SEC2 = int(os.environ.get("SEC2", "9"))
        SKIPW = int(os.environ.get("SKIPW", "0"))
        KMO = int(os.environ.get("KMO", "8"))
        KEV = int(os.environ.get("KEV", "1"))
        phct = [0]
        es_list = []

        def ES():
            e = ExitStack()
            es_list.append(e)
            return e

        class StopBuild(Exception):
            pass

        def CKPT():
            phct[0] += 1
            if phct[0] >= KPH:
                raise StopBuild()

        def sched_quant(ph, src_fn, dst_fn, n, rows, gate, tag, pre_wait=None):
            r = slice(0, rows)
            nb = n // 16
            s1, t1 = I(tag + "1")

            def fv1(v):
                src = src_fn()
                qxn, qA, qB = QT["xn"], QT["A"], QT["B"]
                qam, qsc, qrc = QT["am"], QT["sc"], QT["rc"]
                if gate is not None:
                    v.wait_ge(*gate)
                v.tensor_reduce(qam[r, 0:nb], src.rearrange("p (g e) -> p g e", e=16),
                                AX.X, ALU.max, apply_absolute_value=True)
                v.drain()
                v.tensor_scalar(qsc[r, 0:nb], qam[r, 0:nb], 1.0 / 6.0, 1e-12, ALU.mult, ALU.max)
                v.reciprocal(qrc[r, 0:nb], qsc[r, 0:nb])
                v.tensor_tensor(qxn.ap()[r, 0:n].rearrange("p (g e) -> p g e", e=16),
                                src.rearrange("p (g e) -> p g e", e=16),
                                qrc.ap()[r, 0:nb].broadcast_to([rows, nb, 16]),
                                ALU.mult)
                v.scalar_tensor_tensor(qB[r, 0:n], qxn[r, 0:n], VC, qxn[r, 0:n], ALU.mult, ALU.subtract)
                v.scalar_tensor_tensor(qA[r, 0:n], qxn[r, 0:n], VC, qB[r, 0:n],
                                       ALU.mult, ALU.subtract).then_inc(s1, 1)
            ph.add("vector", fv1)
            s2, t2 = I(tag + "2")

            def fs(s_):
                qxn, qB = QT["xn"], QT["B"]
                s_.wait_ge(s1, t1)
                s_.activation(qB[r, 0:n], qxn[r, 0:n], AF.Copy, bias=MG, scale=2.0)
                s_.activation(qB[r, 0:n], qB[r, 0:n], AF.Copy, bias=-MG / 2.0,
                              scale=0.5).then_inc(s2, 1)
            ph.add("scalar", fs)
            s3, t3 = I(tag + "3")

            def fv2(v):
                dst = dst_fn()
                qxn, qA, qB, qsc, msku = QT["xn"], QT["A"], QT["B"], QT["sc"], QT["mk"]
                v.wait_ge(s2, t2)
                if pre_wait is not None:
                    v.wait_ge(*pre_wait)
                v.scalar_tensor_tensor(qxn[r, 0:n], qA[r, 0:n], 1.0, qA[r, 0:n], ALU.mult, ALU.mult)
                v.tensor_single_scalar(msku[r, 0:n], qxn[r, 0:n], 1.0, ALU.is_ge)
                v.select(qxn[r, 0:n], msku[r, 0:n], qA[r, 0:n], qB[r, 0:n])
                v.tensor_tensor(dst.rearrange("p (g e) -> p g e", e=16),
                                qxn.ap()[r, 0:n].rearrange("p (g e) -> p g e", e=16),
                                qsc.ap()[r, 0:nb].broadcast_to([rows, nb, 16]),
                                ALU.mult).then_inc(s3, 1)
            ph.add("vector", fv2)
            return (s3, t3), (s1, t1)

        def sched_ln(ph, psum_fn, resid_fn, out32, outbf, spill, tag, extra_wait=None,
                     outbf_wait=None):
            sa, ta = I(tag + "a")

            def fva(v):
                ps = psum_fn(); res = resid_fn()
                if extra_wait is not None:
                    v.wait_ge(*extra_wait())
                v.scalar_tensor_tensor(out32, ps, 1.0, res, ALU.mult, ALU.add,
                                       accum_out=lnst[:, 0:1])
                v.scalar_tensor_tensor(scr[:, 0:1024], out32, 1.0, out32,
                                       ALU.mult, ALU.mult, accum_out=lnst[:, 1:2])
                v.tensor_scalar_mul(lnst[:, 2:3], lnst[:, 0:1], -1.0 / 1024.0)
                v.tensor_scalar_mul(lnst[:, 3:4], lnst[:, 1:2], 1.0 / 1024.0)
                v.scalar_tensor_tensor(lnst[:, 3:4], lnst[:, 2:3], lnst[:, 2:3],
                                       lnst[:, 3:4], ALU.mult, ALU.subtract).then_inc(sa, 1)
            ph.add("vector", fva)
            sbv, tb = I(tag + "b")

            def fsc(s_):
                s_.wait_ge(sa, ta)
                s_.activation(lnst[:, 5:6], lnst[:, 3:4], AF.Sqrt,
                              bias=consts[:, 1:2], scale=-1.0).then_inc(sbv, 1)
            ph.add("scalar", fsc)
            sc_, tc = I(tag + "c")

            def fvc(v):
                v.wait_ge(sbv, tb)
                if outbf_wait is not None and outbf_wait[1]:
                    v.wait_ge(*outbf_wait)
                v.reciprocal(lnst[:, 6:7], lnst[:, 5:6])
                v.tensor_scalar(out32, out32, lnst[:, 2:3], lnst[:, 6:7], ALU.add, ALU.mult)
                if outbf is not None:
                    v.tensor_copy(outbf, out32).then_inc(sc_, 1)
                else:
                    v.engine_nop().then_inc(sc_, 1)
            ph.add("vector", fvc)
            if spill is not None:
                _, tq = I("qs", 16)

                def fsp(sy):
                    sy.wait_ge(sc_, tc)
                    sy.dma_start(out=spill, in_=out32).then_inc(qs, 16)
                ph.add("sync", fsp)
            return sc_, tc

        try:
            es_p1 = ES()
            wsl = es_p1.enter_context(nc.sbuf_tensor("wsl", [128, DI], F32))
            wsq = es_p1.enter_context(nc.sbuf_tensor("wsq", [128, DI], BF16))
            alloc_quant(es_p1, "w")

            ph = Ph()

            def f_pre(g):
                ncd = nc.allow_non_contiguous_dma(reason="tiny bias/const loads")
                ncd.__enter__()
                g.memset(consts[:, 0:1], -4.0)
                g.memset(consts[:, 1:2], 1e-5)
                g.memset(consts[:, 2:3], -1.0)
                g.memset(consts[:, 3:4], 0.0)
                g.memset(consts[:, 4:5], 1.0)
                g.memset(ones1b[:], 1.0)
                g.memset(ones1f[:], 1.0)
                g.memset(onescb[:], 1.0)
                g.memset(zrow[:], 0.0)
                for src, dst in [(b_qkv[0:1024], bq_sb[:, 0:8]), (b_qkv[1024:2048], bq_sb[:, 8:16]),
                                 (cn_b1, cnb1[:]), (cn_b2, cnb2[:]), (cn_b3, cnb3[:]),
                                 (gc_b1, gcb1[:]), (gc_b2, gcb2[:])]:
                    g.dma_start(out=dst, in_=src.rearrange("(c p) -> p c", p=128)).then_inc(qg, 16)
                g.dma_start(out=bv_row[:], in_=b_qkv[2048:3072].rearrange("(a f) -> a f", a=1)).then_inc(qg, 16)
                g.dma_start(out=bo_row[:], in_=b_out.rearrange("(a f) -> a f", a=1)).then_inc(qg, 16)
                g.dma_start(out=idsb[:], in_=identb).then_inc(qg, 16)
                for src, dst in [(cn_b4, b4r), (gc_b3, gb3r), (ar_in, arr)]:
                    g.dma_start(out=dst[:],
                                in_=src.broadcast_to([1, 128]).rearrange("a p -> p a")).then_inc(qg, 16)
                g.drain().then_inc(SEM("pre"), 1)
                ncd.__exit__(None, None, None)
            I("qg", 16 * 13)
            I("pre")
            ph.add("gpsimd", f_pre)

            wslices = [] if SKIPW else [
                       (wg_sl, 0, 128, d_wgi), (wg_sl, 128, 128, d_wgi), (wg_sl, 256, 96, d_wgi),
                       (wu_sl, 0, 128, d_wui), (wu_sl, 128, 128, d_wui), (wu_sl, 256, 96, d_wui),
                       (wd_sl, 0, 128, d_wdi)]
            last_store_tick = 0
            prev_v1 = None
            prev_store = None
            for j, (src, r0, rr, dsti) in enumerate(wslices):
                n = DI if j == 6 else D
                _, tqg = I("qg", 16)

                def fload(g, src=src, r0=r0, rr=rr, n=n, pv=prev_v1):
                    if pv is not None:
                        g.wait_ge(*pv)
                    g.dma_start(out=wsl[0:rr, 0:n], in_=src[r0:r0 + rr, :]).then_inc(qg, 16)
                ph.add("gpsimd", fload)
                pw = (qg, prev_store) if prev_store is not None else None
                (s3t3, s1t1) = sched_quant(ph, (lambda rr=rr, n=n: wsl.ap()[0:rr, 0:n]),
                                           (lambda rr=rr, n=n: wsq.ap()[0:rr, 0:n]),
                                           n, rr, (qg, tqg), "wq", pre_wait=pw)
                prev_v1 = s1t1
                _, last_store_tick = I("qg", 16)
                prev_store = last_store_tick

                def fstore(g, r0=r0, rr=rr, n=n, dsti=dsti, s3t3=s3t3):
                    g.wait_ge(*s3t3)
                    g.dma_start(out=dsti[r0:r0 + rr, :], in_=wsq[0:rr, 0:n]).then_inc(qg, 16)
                ph.add("gpsimd", fstore)

            sccw, _ = I("ccw", 3)

            def fcoll(g, gate=last_store_tick):
                if SKIPW:
                    g.drain().then_inc(sccw, 3)
                    return
                g.wait_ge(qg, gate)
                for di, do in [(d_wgi, d_wgo), (d_wui, d_wuo), (d_wdi, d_wdo)]:
                    g.collective_compute("AllGather", ALU.bypass, replica_groups=allg,
                                         ins=[di], outs=[do]).then_inc(sccw, 1)
            ph.add("gpsimd", fcoll)
            ph.emit(); CKPT()
            es_p1.close()

            es_fm1 = ES()
            fmA = es_fm1.enter_context(nc.sbuf_tensor("fmA", [128, NT * D], BF16))
            es_kfm = ES()
            kfm = es_kfm.enter_context(nc.sbuf_tensor("kfm", [128, 8 * S], BF16))
            es_p2 = ES()
            wqs = es_p2.enter_context(nc.sbuf_tensor("wqs", [128, 8 * 1024], BF16))
            vstg = es_p2.enter_context(nc.sbuf_tensor("vstg", [128, 2 * 1040], BF16))
            es_x0h = ES()
            x0h = es_x0h.enter_context(nc.sbuf_tensor("x0h", [128, 8 * T], BF16, side="right"))
            es_ps2 = ES()
            PSQ = [es_ps2.enter_context(nc.psum_tensor("ps2_%d" % i, [128, 512], F32))
                   for i in range(8)]

            ph = Ph()
            SEM("p2t"); SEM("p2v"); SEM("p2vt"); SEM("p2vv"); SEM("ge")
            _, t_vm = I("ge")

            def fv_init(g):
                g.memset(vstg[:, 0:1040], 1.0)
                g.memset(vstg[:, 1040:2080], 1.0).then_inc(SEM("ge"), 1)
            ph.add("gpsimd", fv_init)

            bank_k = {i: 0 for i in range(8)}
            bank_v = {i: 0 for i in range(8)}
            qs0_v = cnts["qs"]
            nvst = 0
            for sh in range(2):
                prev_reads = cnts["p2vt"]
                qs0_h = cnts["qs"]
                for i in range(NT):
                    qs_gate = qs0_h + 16 * 8 * (i - 3) if i >= 4 else None
                    _, tqg = I("qg", 16)

                    def flh(g, i=i, sh=sh, qs_gate=qs_gate, prev_reads=prev_reads):
                        if i == 0 and prev_reads:
                            g.wait_ge(SEM("p2vt"), prev_reads)
                        if qs_gate is not None:
                            g.wait_ge(qs, qs_gate)
                        g.dma_start(out=scb[:, (i % 4) * D:(i % 4 + 1) * D],
                                    in_=x0_full[sh * T + i * 128: sh * T + (i + 1) * 128, :]
                                    ).then_inc(qg, 16)
                    ph.add("gpsimd", flh)
                    _, _tq = I("qs", 16 * 8)

                    def fth(sy, i=i, gate=tqg, prev_reads=prev_reads):
                        sy.wait_ge(qg, gate)
                        if i == 0 and prev_reads:
                            sy.wait_ge(SEM("p2vt"), prev_reads)
                        for fc in range(8):
                            sy.dma_start(
                                out=x0h[:, fc * T + i * 128: fc * T + (i + 1) * 128],
                                in_=scb[:, (i % 4) * D + fc * 128:(i % 4) * D + (fc + 1) * 128],
                                transpose=True).then_inc(qs, 16)
                    ph.add("sync", fth)
                t_x0h = cnts["qs"]

                if SEC2 < 2:
                    continue
                gate_prev_w = cnts["p2vt"]
                _, gate_wk = I("qg", 16 * 8)

                NOCAST = int(os.environ.get("NOCAST", "0"))
                SERK = int(os.environ.get("SERK", "0"))

                def flk(g, gate_prev_w=gate_prev_w, NOCAST=NOCAST, SERK=SERK, t_x0h=t_x0h):
                    if SERK:
                        g.wait_ge(qs, t_x0h)
                    if gate_prev_w:
                        g.wait_ge(SEM("p2vt"), gate_prev_w)
                    for ki in range(8):
                        if NOCAST:
                            g.dma_start(out=wqs[:, ki * 1024:(ki + 1) * 1024],
                                        in_=d_v[:, ki * 1024:(ki + 1) * 1024]).then_inc(qg, 16)
                        else:
                            g.dma_start(out=wqs[:, ki * 1024:(ki + 1) * 1024],
                                        in_=w_qkv[ki * 128:(ki + 1) * 128, 1024:2048]).then_inc(qg, 16)
                ph.add("gpsimd", flk)

                for mo in range(KMO):
                    grp = mo % 2
                    gates = [bank_k[4 * grp + tc] for tc in range(4)]
                    _, t_mm = I("p2t")

                    def fk_t(t, mo=mo, sh=sh, grp=grp, gates=gates, gate_wk=gate_wk, t_x0h=t_x0h):
                        if mo == 0:
                            t.wait_ge(qg, gate_wk)
                            t.wait_ge(qs, t_x0h)
                        gmax = max(gates)
                        if gmax:
                            t.wait_ge(SEM("p2v"), gmax)
                        KMM = int(os.environ.get("KMM", "0"))
                        if KMM == 3:
                            pass
                        elif KMM == 1:
                            t.matmul(PSQ[0][:, 0:512], wqs[:, 0:128], x0h[:, 0:512],
                                     start=True, stop=True)
                        else:
                            for tc in range(4):
                                for ki in range(8):
                                    t.matmul(PSQ[4 * grp + tc][:, 0:512],
                                             wqs[:, ki * 1024 + mo * 128: ki * 1024 + (mo + 1) * 128],
                                             x0h[:, ki * T + tc * 512: ki * T + (tc + 1) * 512],
                                             start=(ki == 0), stop=(ki == 7))
                        t.drain().then_inc(SEM("p2t"), 1)
                    ph.add("tensor", fk_t)

                    def fk_v(v, mo=mo, sh=sh, grp=grp, t_mm=t_mm):
                        v.wait_ge(SEM("p2t"), t_mm)
                        if int(os.environ.get("KEV", "1")) == 4:
                            v.wait_ge(qs, 4096)
                        KEV = int(os.environ.get("KEV", "1"))
                        for tc in range(4):
                            if KEV == 6:
                                ic = v.tensor_single_scalar(
                                    kfm[:, tc * 512:(tc + 1) * 512],
                                    PSQ[0][:, 0:512], 0.25, ALU.add)
                            elif KEV == 7:
                                ic = v.tensor_single_scalar(
                                    kfm[:, tc * 512:(tc + 1) * 512],
                                    PSQ[0][:, 0:512], consts[:, 1:2], ALU.add)
                            elif KEV == 5:
                                ic = v.tensor_single_scalar(
                                    fmA[:, tc * 512:(tc + 1) * 512],
                                    PSQ[0][:, 0:512], bq_sb[:, 8 + mo:9 + mo], ALU.add)
                            elif KEV == 3:
                                ic = v.tensor_single_scalar(
                                    kfm[:, tc * 512:(tc + 1) * 512],
                                    PSQ[0][:, 0:512], bq_sb[:, 8 + mo:9 + mo], ALU.add)
                            elif KEV in (1, 4):
                                ic = v.tensor_single_scalar(
                                    kfm[:, mo * S + sh * T + tc * 512: mo * S + sh * T + (tc + 1) * 512],
                                    PSQ[4 * grp + tc][:, 0:512], bq_sb[:, 8 + mo:9 + mo], ALU.add)
                            else:
                                ic = v.engine_nop()
                            ic.then_inc(SEM("p2v"), 1)
                    for tc in range(4):
                        bank_k[4 * grp + tc] = cnts["p2v"] + tc + 1
                    I("p2v", 4)
                    ph.add("vector", fk_v)
                t_k_mm = cnts["p2t"]

                if SEC2 < 3:
                    continue
                _, gate_wv = I("qg", 16 * 8)

                def flv(g, t_k_mm=t_k_mm):
                    g.wait_ge(SEM("p2t"), t_k_mm)
                    for ki in range(8):
                        g.dma_start(out=wqs[:, ki * 1024:(ki + 1) * 1024],
                                    in_=w_qkv[ki * 128:(ki + 1) * 128, 2048:3072]).then_inc(qg, 16)
                ph.add("gpsimd", flv)

                for j in range(NT):
                    kt = sh * NT + j
                    gates = [bank_v[(2 * kt + half) % 4 + 4] for half in range(2)]
                    _, t_mm = I("p2vt")

                    def fvt(t, j=j, kt=kt, sh=sh, gates=gates, gate_wv=gate_wv, t_vm=t_vm):
                        if j == 0:
                            t.wait_ge(qg, gate_wv)
                            t.wait_ge(SEM("ge"), t_vm)
                        gmax = max(gates)
                        if gmax:
                            t.wait_ge(SEM("p2vv"), gmax)
                        for half in range(2):
                            ps = PSQ[(2 * kt + half) % 4 + 4]
                            for ki in range(8):
                                t.matmul(ps[:, 0:512],
                                         x0h[:, ki * T + j * 128: ki * T + (j + 1) * 128],
                                         wqs[:, ki * 1024 + half * 512: ki * 1024 + (half + 1) * 512],
                                         start=(ki == 0), stop=False)
                            t.matmul(ps[:, 0:512], ones1b[:],
                                     bv_row[:, half * 512:(half + 1) * 512], start=False, stop=True)
                        t.drain().then_inc(SEM("p2vt"), 1)
                    ph.add("tensor", fvt)
                    st_gate = qs0_v + 16 * (nvst - 1) if nvst >= 2 else None

                    def fvv(v, kt=kt, t_mm=t_mm, st_gate=st_gate):
                        v.wait_ge(SEM("p2vt"), t_mm)
                        if st_gate is not None:
                            v.wait_ge(qs, st_gate)
                        for half in range(2):
                            ps = PSQ[(2 * kt + half) % 4 + 4]
                            base = (kt % 2) * 1040 + half * 520
                            ic = v.tensor_copy(
                                vstg.ap()[:, base:base + 520].rearrange("p (h c) -> p h c", c=65)[:, :, 0:64],
                                ps.ap()[:, 0:512].rearrange("p (h c) -> p h c", c=64))
                            ic.then_inc(SEM("p2vv"), 1)
                    for half in range(2):
                        bank_v[(2 * kt + half) % 4 + 4] = cnts["p2vv"] + half + 1
                    I("p2vv", 2)
                    ph.add("vector", fvv)
                    _, _tq = I("qs", 16)
                    nvst += 1

                    def fvs(sy, kt=kt, need=cnts["p2vv"]):
                        sy.wait_ge(SEM("p2vv"), need)
                        sy.dma_start(out=d_v[:, kt * 1040:(kt + 1) * 1040],
                                     in_=vstg[:, (kt % 2) * 1040:(kt % 2 + 1) * 1040]).then_inc(qs, 16)
                    ph.add("sync", fvs)
            ph.emit(); CKPT()
            es_x0h.close()

            es_ofm = ES()
            ofm = es_ofm.enter_context(nc.sbuf_tensor("ofm", [128, 8 * T], BF16, side="right"))
            ph = Ph()
            qs0_ofm = cnts["qs"]
            for i in range(NT):
                qs_gate = qs0_ofm + 16 * 8 * (i - 3) if i >= 4 else None
                _, tqg = I("qg", 16)

                def fl(g, i=i, qs_gate=qs_gate):
                    if qs_gate is not None:
                        g.wait_ge(qs, qs_gate)
                    g.dma_start(out=scb[:, (i % 4) * D:(i % 4 + 1) * D],
                                in_=x0_own[i * 128:(i + 1) * 128, :]).then_inc(qg, 16)
                ph.add("gpsimd", fl)
                _, _t = I("qs", 16 * 8)

                def ft(sy, i=i, gate=tqg):
                    sy.wait_ge(qg, gate)
                    for fc in range(8):
                        sy.dma_start(
                            out=ofm[:, fc * T + i * 128: fc * T + (i + 1) * 128],
                            in_=scb[:, (i % 4) * D + fc * 128:(i % 4) * D + (fc + 1) * 128],
                            transpose=True).then_inc(qs, 16)
                ph.add("sync", ft)
            t_ofm = cnts["qs"]

            _, gate_wq = I("qg", 16 * 8)

            def flq(g):
                for ki in range(8):
                    g.dma_start(out=wqs[:, ki * 1024:(ki + 1) * 1024],
                                in_=w_qkv[ki * 128:(ki + 1) * 128, 0:1024]).then_inc(qg, 16)
            ph.add("gpsimd", flq)

            bank_q = {i: 0 for i in range(8)}
            for mo in range(8):
                grp = mo % 2
                gates = [bank_q[4 * grp + tc] for tc in range(4)]
                _, t_mm = I("p2t")

                def fq_t(t, mo=mo, grp=grp, gates=gates):
                    if mo == 0:
                        t.wait_ge(qg, gate_wq)
                        t.wait_ge(qs, t_ofm)
                    gmax = max(gates)
                    if gmax:
                        t.wait_ge(SEM("p2v"), gmax)
                    for ki in range(8):
                        for tc in range(4):
                            t.matmul(PSQ[4 * grp + tc][:, 0:512],
                                     wqs[:, ki * 1024 + mo * 128: ki * 1024 + (mo + 1) * 128],
                                     ofm[:, ki * T + tc * 512: ki * T + (tc + 1) * 512],
                                     start=(ki == 0), stop=(ki == 7))
                    t.drain().then_inc(SEM("p2t"), 1)
                ph.add("tensor", fq_t)

                def fq_v(v, mo=mo, grp=grp, t_mm=t_mm):
                    v.wait_ge(SEM("p2t"), t_mm)
                    for tc in range(4):
                        ic = v.tensor_single_scalar(
                            fmA[:, mo * T + tc * 512: mo * T + (tc + 1) * 512],
                            PSQ[4 * grp + tc][:, 0:512], bq_sb[:, mo:mo + 1], ALU.add)
                        ic.then_inc(SEM("p2v"), 1)
                for tc in range(4):
                    bank_q[4 * grp + tc] = cnts["p2v"] + tc + 1
                I("p2v", 4)
                ph.add("vector", fq_v)
            ph.emit(); CKPT()
            es_ps2.close()
            es_ofm.close()
            es_p2.close()

            es_p3 = ES()
            v_sb = es_p3.enter_context(nc.sbuf_tensor("v_sb", [128, NKT * 1040], BF16))
            cmr = es_p3.enter_context(nc.sbuf_tensor("cmr", [128, 2 * 1024], BF16))
            es_ps3 = ES()
            ps_s = [es_ps3.enter_context(nc.psum_tensor("ps3s%d" % i, [128, 1024], F32))
                    for i in range(2)]
            ps_c = [es_ps3.enter_context(nc.psum_tensor("ps3c%d" % i, [65, 256], F32))
                    for i in range(2)]
            ps_zb = es_ps3.enter_context(nc.psum_tensor("ps3zb", [64, 256], F32))

            ph = Ph()
            SEM("p3t"); SEM("p3e"); SEM("p3m"); SEM("p3a"); SEM("p3z"); SEM("p3zb"); SEM("p3ev")
            expbuf_av = {0: 0, 1: 0}
            psc_ev = {0: 0, 1: 0}
            pss_exp = {0: 0, 1: 0}
            mask_used = {0: 0, 1: 0}
            stage_dma = {0: 0, 1: 0}
            scr_zb = 0
            zb_ev = 0
            vload = []
            pending_tail = None
            P3ST = {"scr_zb": 0, "zb_ev": 0}

            def mk_tail3(bi, h, av_tick):
                _, t_z = I("p3z")

                def fz_v(v, h=h, av_tick=av_tick, scr_gate=P3ST["scr_zb"]):
                    v.wait_ge(SEM("p3a"), av_tick)
                    if scr_gate:
                        v.wait_ge(SEM("p3zb"), scr_gate)
                    v.reciprocal(scr[0:1, 0:256], ps_c[h % 2][64:65, 0:256]).then_inc(SEM("p3z"), 1)
                ph.add("vector", fz_v)
                _, t_zb = I("p3zb")

                def fz_t(t, t_z=t_z, zb_gate=P3ST["zb_ev"]):
                    t.wait_ge(SEM("p3z"), t_z)
                    if zb_gate:
                        t.wait_ge(SEM("p3ev"), zb_gate)
                    t.matmul(ps_zb[:, 0:256], ones1f[0:1, 0:64], scr[0:1, 0:256],
                             start=True, stop=True)
                    t.drain().then_inc(SEM("p3zb"), 1)
                ph.add("tensor", fz_t)
                P3ST["scr_zb"] = t_zb
                _, t_ev = I("p3ev")

                def fz_e(v, h=h, t_zb=t_zb, dma_gate=stage_dma[h % 2]):
                    v.wait_ge(SEM("p3zb"), t_zb)
                    if dma_gate:
                        v.wait_ge(qs, dma_gate)
                    stg = scb[0:64, 2048 + (h % 2) * 256: 2048 + (h % 2) * 256 + 256]
                    v.tensor_copy(stg, ps_c[h % 2][0:64, 0:256])
                    v.tensor_tensor(stg, stg, ps_zb[0:64, 0:256],
                                    ALU.mult).then_inc(SEM("p3ev"), 1)
                ph.add("vector", fz_e)
                psc_ev[h % 2] = t_ev
                P3ST["zb_ev"] = t_ev
                _, tq_st = I("qs", 16)

                def fz_s(sy, bi=bi, h=h, t_ev=t_ev):
                    sy.wait_ge(SEM("p3ev"), t_ev)
                    sy.dma_start(out=d_ctx[h * 64:(h + 1) * 64, bi * 256:(bi + 1) * 256],
                                 in_=scb[0:64, 2048 + (h % 2) * 256: 2048 + (h % 2) * 256 + 256]
                                 ).then_inc(qs, 16)
                ph.add("sync", fz_s)
                stage_dma[h % 2] = tq_st

            for bi in range(NBLK):
                _, tvl = I("qg", 16 * 5)
                mu_gate = mask_used[bi % 2]

                def fld(g, bi=bi, mu_gate=mu_gate):
                    if mu_gate:
                        g.wait_ge(SEM("p3m"), mu_gate)
                    for j in range(4):
                        kt = 4 * bi + j
                        g.dma_start(out=v_sb[:, kt * 1040:(kt + 1) * 1040],
                                    in_=d_v[:, kt * 1040:(kt + 1) * 1040]).then_inc(qg, 16)
                    g.dma_start(
                        out=cmr.ap()[:, (bi % 2) * 1024:(bi % 2 + 1) * 1024]
                            .rearrange("p (m q) -> p m q", q=256),
                        in_=cmask[4 * bi:4 * bi + 4, :].rearrange("m (p q) -> p m q", p=128)
                    ).then_inc(qg, 16)
                ph.add("gpsimd", fld)
                vload.append(tvl)

                for h in range(H):
                    ch, ro = h >> 1, 64 * (h & 1)
                    nsc = bi + 1
                    av_tick = 0
                    for sc in range(nsc):
                        pp = sc % 2
                        gate_exp_reuse = pss_exp[pp]
                        _, t_mm = I("p3t")

                        def fsc_t(t, bi=bi, h=h, ch=ch, ro=ro, sc=sc, pp=pp,
                                  gate=gate_exp_reuse, tvl=tvl):
                            if sc == 0 and h == 0:
                                t.wait_ge(qg, tvl)
                            if gate:
                                t.wait_ge(SEM("p3e"), gate)
                            for j in range(4):
                                kt = 4 * sc + j
                                t.matmul(ps_s[pp][:, j * 256:(j + 1) * 256],
                                         kfm[ro:ro + 64, ch * S + kt * 128: ch * S + (kt + 1) * 128],
                                         fmA[ro:ro + 64, ch * T + bi * 256: ch * T + (bi + 1) * 256],
                                         start=True, stop=True)
                            t.drain().then_inc(SEM("p3t"), 1)
                        ph.add("tensor", fsc_t)

                        gate_av_reuse = expbuf_av[pp]
                        _, t_exp = I("p3e")

                        def fsc_s(s_, pp=pp, t_mm=t_mm, gate=gate_av_reuse):
                            s_.wait_ge(SEM("p3t"), t_mm)
                            if gate:
                                s_.wait_ge(SEM("p3a"), gate)
                            s_.activation(scb[:, pp * 1024:(pp + 1) * 1024], ps_s[pp][:, 0:1024],
                                          AF.Exp, bias=consts[:, 0:1],
                                          scale=0.125).then_inc(SEM("p3e"), 1)
                        ph.add("scalar", fsc_s)
                        pss_exp[pp] = t_exp

                        last = (sc == nsc - 1)
                        if last:
                            _, t_msk = I("p3m")

                            def fsc_m(v, bi=bi, pp=pp, t_exp=t_exp, tvl=tvl):
                                v.wait_ge(SEM("p3e"), t_exp)
                                v.wait_ge(qg, tvl)
                                v.tensor_tensor(scb[:, pp * 1024:(pp + 1) * 1024],
                                                scb[:, pp * 1024:(pp + 1) * 1024],
                                                cmr[:, (bi % 2) * 1024:(bi % 2 + 1) * 1024],
                                                ALU.mult).then_inc(SEM("p3m"), 1)
                            ph.add("vector", fsc_m)
                            mask_used[bi % 2] = t_msk
                            av_need = ("p3m", t_msk)
                        else:
                            av_need = ("p3e", t_exp)

                        gate_psc = psc_ev[h % 2] if sc == 0 else 0
                        _, t_av = I("p3a")

                        def fsc_a(t, bi=bi, h=h, sc=sc, pp=pp, nsc=nsc,
                                  av_need=av_need, gate_psc=gate_psc):
                            t.wait_ge(SEM(av_need[0]), av_need[1])
                            if gate_psc:
                                t.wait_ge(SEM("p3ev"), gate_psc)
                            for j in range(4):
                                kt = 4 * sc + j
                                t.matmul(ps_c[h % 2][:, 0:256],
                                         v_sb[:, kt * 1040 + h * 65: kt * 1040 + h * 65 + 65],
                                         scb[:, pp * 1024 + j * 256: pp * 1024 + (j + 1) * 256],
                                         start=(sc == 0 and j == 0),
                                         stop=(sc == nsc - 1 and j == 3))
                            t.drain().then_inc(SEM("p3a"), 1)
                        ph.add("tensor", fsc_a)
                        expbuf_av[pp] = t_av
                        av_tick = t_av

                    if pending_tail is not None:
                        mk_tail3(*pending_tail)
                    pending_tail = (bi, h, av_tick)
            if pending_tail is not None:
                mk_tail3(*pending_tail)
            ph.emit(); CKPT()
            es_ps3.close()
            es_p3.close()
            es_kfm.close()

            es_xf = ES()
            xf = es_xf.enter_context(nc.sbuf_tensor("xf", [128, NT * D], F32))
            es_p4 = ES()
            wo_sb = es_p4.enter_context(nc.sbuf_tensor("wo_sb", [128, 8 * 1024], BF16))
            ctxl = es_p4.enter_context(nc.sbuf_tensor("ctxl", [128, 2 * 1024], BF16))
            es_ps4 = ES()
            ps4 = [es_ps4.enter_context(nc.psum_tensor("ps4_%d" % i, [128, 1024], F32))
                   for i in range(2)]

            def phase_proj_ln(wsrc, brow, dctx_src, resid_loader, spill_dst, tag,
                              wo_sb, ctxl, ps4, fm_dst, xf_t, do_tr=True, extra_v_fn=None):
                ph = Ph()
                _, gate_w = I("qg", 16 * 8)

                def flw(g):
                    for ki in range(8):
                        g.dma_start(out=wo_sb[:, ki * 1024:(ki + 1) * 1024],
                                    in_=wsrc[ki * 128:(ki + 1) * 128, :]).then_inc(qg, 16)
                ph.add("gpsimd", flw)
                SEM(tag + "t")
                ln_a = {0: 0, 1: 0}
                ln_c = {0: 0, 1: 0}
                tr_done = {0: 0, 1: 0}
                resid_rd = 0
                for ti in range(NT):
                    pp = ti % 2
                    gate_ctx = cnts[tag + "t"] - 1 if ti >= 2 else 0
                    _, t_ld = I("qg", 32)

                    def fld(g, ti=ti, pp=pp, gate_ctx=gate_ctx, rr=resid_rd):
                        if gate_ctx:
                            g.wait_ge(SEM(tag + "t"), gate_ctx)
                        g.dma_start(
                            out=ctxl.ap()[:, pp * 1024:(pp + 1) * 1024].rearrange("p (k c) -> p k c", c=128),
                            in_=dctx_src.rearrange("(k p) t -> p k t", p=128)[:, :, ti * 128:(ti + 1) * 128]
                        ).then_inc(qg, 16)
                        if rr:
                            g.wait_ge(SEM(tag + "a"), rr)
                        resid_loader(g, ti, scr[:, 1024:2048])
                    ph.add("gpsimd", fld)

                    gate_psum = ln_a[pp]
                    _, t_mm = I(tag + "t")

                    def fmm(t, ti=ti, pp=pp, t_ld=t_ld, gate_w=gate_w, gate_psum=gate_psum):
                        t.wait_ge(qg, t_ld)
                        if ti == 0:
                            t.wait_ge(qg, gate_w)
                        if gate_psum:
                            t.wait_ge(SEM(tag + "a"), gate_psum)
                        for ki in range(8):
                            for nh in range(2):
                                t.matmul(ps4[pp][:, nh * 512:(nh + 1) * 512],
                                         ctxl[:, pp * 1024 + ki * 128: pp * 1024 + (ki + 1) * 128],
                                         wo_sb[:, ki * 1024 + nh * 512: ki * 1024 + (nh + 1) * 512],
                                         start=(ki == 0), stop=False)
                        for nh in range(2):
                            t.matmul(ps4[pp][:, nh * 512:(nh + 1) * 512], ones1b[:],
                                     brow[:, nh * 512:(nh + 1) * 512],
                                     start=False, stop=True)
                        t.drain().then_inc(SEM(tag + "t"), 1)
                    ph.add("tensor", fmm)

                    obf = scb.ap()[:, 2048 + pp * 1024: 2048 + (pp + 1) * 1024]
                    gate_tr = tr_done[pp]
                    sc_, tc = sched_ln(
                        ph,
                        (lambda pp=pp: ps4[pp][:, 0:1024]),
                        (lambda: scr[:, 1024:2048]),
                        xf_t[:, ti * D:(ti + 1) * D],
                        obf,
                        spill_dst[:, ti * D:(ti + 1) * D] if spill_dst is not None else None,
                        tag,
                        extra_wait=(lambda t_mm=t_mm: (SEM(tag + "t"), t_mm)),
                        outbf_wait=(qs, gate_tr))
                    ln_a[pp] = cnts[tag + "a"]
                    ln_c[pp] = tc
                    resid_rd = cnts[tag + "a"]
                    t_tr = I("qs", 16 * 8)[1] if do_tr else 0

                    def ftr(sy, ti=ti, pp=pp, tc=tc, obf=obf):
                        sy.wait_ge(SEM(tag + "c"), tc)
                        for fc in range(8):
                            sy.dma_start(
                                out=fm_dst[:, fc * T + ti * 128: fc * T + (ti + 1) * 128],
                                in_=obf[:, fc * 128:(fc + 1) * 128],
                                transpose=True).then_inc(qs, 16)
                    if do_tr:
                        ph.add("sync", ftr)
                        tr_done[pp] = t_tr
                    if extra_v_fn is not None:
                        extra_v_fn(ph, ti, tc)
                return ph

            def resid_x0(g, ti, dst):
                g.dma_start(out=dst, in_=x0_own[ti * 128:(ti + 1) * 128, :]).then_inc(qg, 16)

            ph = phase_proj_ln(w_out, bo_row, d_ctx, resid_x0, d_x1, "p4",
                               wo_sb, ctxl, ps4, fmA, xf)
            ph.emit(); CKPT()
            es_ps4.close()
            es_p4.close()

            es_p5 = ES()
            wcn1 = es_p5.enter_context(nc.sbuf_tensor("wcn1", [128, 8 * 1024], BF16))
            wcn2 = es_p5.enter_context(nc.sbuf_tensor("wcn2", [128, 8 * 512], BF16))
            wcn3 = es_p5.enter_context(nc.sbuf_tensor("wcn3", [128, 4 * 256], BF16))
            wc4b = es_p5.enter_context(nc.sbuf_tensor("wc4b", [128, 2], BF16))
            wgc1 = es_p5.enter_context(nc.sbuf_tensor("wgc1", [128, 8 * 512], BF16))
            wgc2 = es_p5.enter_context(nc.sbuf_tensor("wgc2", [128, 4 * 256], BF16))
            wgc3 = es_p5.enter_context(nc.sbuf_tensor("wgc3", [128, 2], BF16))
            gcbf = es_p5.enter_context(nc.sbuf_tensor("gcbf", [128, 8], BF16))
            h1b = es_p5.enter_context(nc.sbuf_tensor("h1b", [128, 8 * 512], BF16))
            h2b = es_p5.enter_context(nc.sbuf_tensor("h2b", [128, 4 * 512], BF16))
            h3b = es_p5.enter_context(nc.sbuf_tensor("h3b", [128, 2 * 512], BF16))
            es_ps5 = ES()
            ps5 = [es_ps5.enter_context(nc.psum_tensor("ps5_%d" % i, [128, 512], F32))
                   for i in range(4)]
            pt4 = es_ps5.enter_context(nc.psum_tensor("pt4", [128, 16], F32))
            psg = es_ps5.enter_context(nc.psum_tensor("psg", [128, 8], F32))
            ps1x = es_ps5.enter_context(nc.psum_tensor("ps1x", [1, 8], F32))

            ph = Ph()
            _, t_gcr = I("p5gc")

            def fgc_v(v):
                for ki in range(8):
                    v.reduce_sum(gsc[:, ki:ki + 1], fmA[:, ki * T:(ki + 1) * T], AX.X)
                v.drain().then_inc(SEM("p5gc"), 1)
            ph.add("vector", fgc_v)
            _, t_gci = I("qg", 16)

            def fgc_g(g, t_gcr=t_gcr):
                g.wait_ge(SEM("p5gc"), t_gcr)
                g.dma_start(out=d_gci, in_=gsc[:, 0:8]).then_inc(qg, 16)
            ph.add("gpsimd", fgc_g)
            _, t_cc = I("ccg")

            def fgc_c(g, t_gci=t_gci):
                g.wait_ge(qg, t_gci)
                g.collective_compute("AllReduce", ALU.add, replica_groups=pairs,
                                     ins=[d_gci], outs=[d_gco]).then_inc(SEM("ccg"), 1)
            ph.add("gpsimd", fgc_c)
            _, t_gco = I("qg", 16)

            def fgc_g2(g, t_cc=t_cc):
                g.wait_ge(SEM("ccg"), t_cc)
                g.dma_start(out=gsc[:, 8:16], in_=d_gco).then_inc(qg, 16)
            ph.add("gpsimd", fgc_g2)

            _, t_wcn = I("qg", 16 * (8 + 8 + 4 + 1 + 8 + 4 + 1))

            def fwcn(g):
                ncd = nc.allow_non_contiguous_dma(reason="tiny cn_w4/gc_w3 loads")
                ncd.__enter__()
                for ki in range(8):
                    g.dma_start(out=wcn1[:, ki * 1024:(ki + 1) * 1024],
                                in_=cn_w1[ki * 128:(ki + 1) * 128, :]).then_inc(qg, 16)
                for ki in range(8):
                    g.dma_start(out=wcn2[:, ki * 512:(ki + 1) * 512],
                                in_=cn_w2[ki * 128:(ki + 1) * 128, :]).then_inc(qg, 16)
                for ki in range(4):
                    g.dma_start(out=wcn3[:, ki * 256:(ki + 1) * 256],
                                in_=cn_w3[ki * 128:(ki + 1) * 128, :]).then_inc(qg, 16)
                g.dma_start(out=wc4b[:], in_=cn_w4.rearrange("(c p) o -> p (c o)", p=128)).then_inc(qg, 16)
                for ki in range(8):
                    g.dma_start(out=wgc1[:, ki * 512:(ki + 1) * 512],
                                in_=gc_w1[ki * 128:(ki + 1) * 128, :]).then_inc(qg, 16)
                for ki in range(4):
                    g.dma_start(out=wgc2[:, ki * 256:(ki + 1) * 256],
                                in_=gc_w2[ki * 128:(ki + 1) * 128, :]).then_inc(qg, 16)
                g.dma_start(out=wgc3[:], in_=gc_w3.rearrange("(c p) o -> p (c o)", p=128)).then_inc(qg, 16)
                ncd.__exit__(None, None, None)
            ph.add("gpsimd", fwcn)

            _, t_gc2 = I("p5gf")

            def fgf_v(v, t_gco=t_gco):
                v.wait_ge(qg, t_gco)
                v.tensor_scalar_mul(gsc[:, 16:24], gsc[:, 8:16], 1.0 / 4096.0)
                v.tensor_scalar_mul(gcbf[:], gsc[:, 8:16], 1.0 / 4096.0)
                v.drain().then_inc(SEM("p5gf"), 1)
            ph.add("vector", fgf_v)
            _, t_gf1 = I("p5t")

            def fgf_t(t, t_gc2=t_gc2, t_wcn=t_wcn):
                t.wait_ge(SEM("p5gf"), t_gc2)
                t.wait_ge(qg, t_wcn)
                for mo in range(4):
                    for ki in range(8):
                        t.matmul(psg[:, mo:mo + 1], wgc1[:, ki * 512 + mo * 128: ki * 512 + (mo + 1) * 128],
                                 gcbf[:, ki:ki + 1], start=(ki == 0), stop=(ki == 7))
                t.drain().then_inc(SEM("p5t"), 1)
            ph.add("tensor", fgf_t)
            _, t_gf1s = I("p5s")

            def fgf_s(s_, t_gf1=t_gf1):
                s_.wait_ge(SEM("p5t"), t_gf1)
                for mo in range(4):
                    ic = s_.activation(scb[:, mo:mo + 1], psg[:, mo:mo + 1], AF.Gelu,
                                       bias=gcb1[:, mo:mo + 1], scale=1.0)
                ic.then_inc(SEM("p5s"), 1)
            ph.add("scalar", fgf_s)
            _, t_gf2 = I("p5t")

            def fgf_t2(t, t_gf1s=t_gf1s):
                t.wait_ge(SEM("p5s"), t_gf1s)
                for mo in range(2):
                    for ki in range(4):
                        t.matmul(psg[:, 4 + mo:5 + mo], wgc2[:, ki * 256 + mo * 128: ki * 256 + (mo + 1) * 128],
                                 scb[:, ki:ki + 1], start=(ki == 0), stop=(ki == 3))
                t.drain().then_inc(SEM("p5t"), 1)
            ph.add("tensor", fgf_t2)
            _, t_gf2s = I("p5s")

            def fgf_s2(s_, t_gf2=t_gf2):
                s_.wait_ge(SEM("p5t"), t_gf2)
                for mo in range(2):
                    ic = s_.activation(scb[:, 4 + mo:5 + mo], psg[:, 4 + mo:5 + mo], AF.Gelu,
                                       bias=gcb2[:, mo:mo + 1], scale=1.0)
                ic.then_inc(SEM("p5s"), 1)
            ph.add("scalar", fgf_s2)
            _, t_gf3 = I("p5t")

            def fgf_t3(t, t_gf2s=t_gf2s):
                t.wait_ge(SEM("p5s"), t_gf2s)
                for ki in range(2):
                    t.matmul(ps1x[0:1, 0:1], wgc3[:, ki:ki + 1], scb[:, 4 + ki:5 + ki],
                             start=(ki == 0), stop=(ki == 1))
                t.drain().then_inc(SEM("p5t"), 1)
            ph.add("tensor", fgf_t3)
            _, t_gf3s = I("p5s")

            def fgf_s3(s_, t_gf3=t_gf3):
                s_.wait_ge(SEM("p5t"), t_gf3)
                s_.activation(scr[0:1, 0:1], ps1x[0:1, 0:1], AF.Sigmoid,
                              bias=gb3r[0:1, 0:1], scale=1.0).then_inc(SEM("p5s"), 1)
            ph.add("scalar", fgf_s3)
            _, t_gfb = I("p5t")

            def fgf_t4(t, t_gf3s=t_gf3s):
                t.wait_ge(SEM("p5s"), t_gf3s)
                t.matmul(psg[:, 6:7], ones1f[0:1, :], scr[0:1, 0:1],
                         start=True, stop=True)
                t.drain().then_inc(SEM("p5t"), 1)
            ph.add("tensor", fgf_t4)
            _, t_gfv = I("p5v")

            def fgf_v2(v, t_gfb=t_gfb):
                v.wait_ge(SEM("p5t"), t_gfb)
                v.tensor_scalar_mul(gfr[:], psg[:, 6:7], 1.0)
                v.tensor_scalar_mul(a6r[:], arr[:], 0.6)
                v.tensor_scalar_mul(gsc[:, 30:31], arr[:], -0.6)
                v.tensor_tensor(bvr[:], arr[:], gfr[:], ALU.mult)
                v.tensor_scalar_mul(gsc[:, 24:25], consts[:, 4:5], 1.0)
                v.tensor_scalar_mul(gsc[:, 25:26], consts[:, 4:5], 1.0)
                v.tensor_scalar_mul(bvr[:], bvr[:], 1.4)
                v.tensor_scalar_mul(gsc[:, 24:25], consts[:, 4:5], 1.0)
                v.tensor_scalar_mul(gsc[:, 25:26], consts[:, 4:5], 1.0)
                v.scalar_tensor_tensor(bvr[:], arr[:], -1.0, bvr[:], ALU.mult, ALU.add)
                v.tensor_scalar_mul(gsc[:, 24:25], consts[:, 4:5], 1.0)
                v.tensor_scalar_mul(gsc[:, 25:26], consts[:, 4:5], 1.0)
                v.tensor_scalar_add(bvr[:], bvr[:], 0.1)
                v.tensor_scalar_mul(gsc[:, 24:25], consts[:, 4:5], 1.0)
                v.tensor_scalar_mul(gsc[:, 25:26], consts[:, 4:5], 1.0)
                v.tensor_scalar_mul(gsc[:, 31:32], bvr[:], -1.0)
                v.drain().then_inc(SEM("p5v"), 1)
            ph.add("vector", fgf_v2)

            bank_ev = {0: 0, 1: 0, 2: 0, 3: 0}
            refresh_tr = {0: 0, 1: 0}
            refresh_q = 0
            dbg_prev_dump = [0]
            for cyc in range(3):
                for tc in range(4):
                    for mo in range(8):
                        gate_ps = bank_ev[mo % 4]
                        _, t_mm = I("p5t")

                        def f1t(t, mo=mo, tc=tc, cyc=cyc, gate_ps=gate_ps,
                                t_wcn=t_wcn, rq=refresh_q):
                            if cyc == 0 and tc == 0 and mo == 0:
                                t.wait_ge(qg, t_wcn)
                            if rq and tc == 0 and mo == 0:
                                t.wait_ge(qs, rq)
                            if gate_ps:
                                t.wait_ge(SEM("p5s"), gate_ps)
                            for ki in range(8):
                                t.matmul(ps5[mo % 4][:, 0:512],
                                         wcn1[:, ki * 1024 + mo * 128: ki * 1024 + (mo + 1) * 128],
                                         fmA[:, ki * T + tc * 512: ki * T + (tc + 1) * 512],
                                         start=(ki == 0), stop=(ki == 7))
                            t.drain().then_inc(SEM("p5t"), 1)
                        ph.add("tensor", f1t)
                        _, t_ev = I("p5s")

                        def f1s(s_, mo=mo, t_mm=t_mm):
                            s_.wait_ge(SEM("p5t"), t_mm)
                            s_.activation(h1b[:, mo * 512:(mo + 1) * 512], ps5[mo % 4][:, 0:512],
                                          AF.Gelu, bias=cnb1[:, mo:mo + 1],
                                          scale=1.0).then_inc(SEM("p5s"), 1)
                        ph.add("scalar", f1s)
                        bank_ev[mo % 4] = cnts["p5s"]
                    h1done = cnts["p5s"]
                    for mo in range(4):
                        gate_ps = bank_ev[mo % 4]
                        _, t_mm = I("p5t")

                        def f2t(t, mo=mo, h1done=h1done, gate_ps=gate_ps):
                            t.wait_ge(SEM("p5s"), max(h1done, gate_ps))
                            for ki in range(8):
                                t.matmul(ps5[mo % 4][:, 0:512],
                                         wcn2[:, ki * 512 + mo * 128: ki * 512 + (mo + 1) * 128],
                                         h1b[:, ki * 512:(ki + 1) * 512],
                                         start=(ki == 0), stop=(ki == 7))
                            t.drain().then_inc(SEM("p5t"), 1)
                        ph.add("tensor", f2t)
                        _, t_ev = I("p5s")

                        def f2s(s_, mo=mo, t_mm=t_mm):
                            s_.wait_ge(SEM("p5t"), t_mm)
                            s_.activation(h2b[:, mo * 512:(mo + 1) * 512], ps5[mo % 4][:, 0:512],
                                          AF.Gelu, bias=cnb2[:, mo:mo + 1],
                                          scale=1.0).then_inc(SEM("p5s"), 1)
                        ph.add("scalar", f2s)
                        bank_ev[mo % 4] = cnts["p5s"]
                    h2done = cnts["p5s"]
                    for mo in range(2):
                        gate_ps = bank_ev[mo]
                        _, t_mm = I("p5t")

                        def f3t(t, mo=mo, h2done=h2done, gate_ps=gate_ps):
                            t.wait_ge(SEM("p5s"), max(h2done, gate_ps))
                            for ki in range(4):
                                t.matmul(ps5[mo][:, 0:512],
                                         wcn3[:, ki * 256 + mo * 128: ki * 256 + (mo + 1) * 128],
                                         h2b[:, ki * 512:(ki + 1) * 512],
                                         start=(ki == 0), stop=(ki == 3))
                            t.drain().then_inc(SEM("p5t"), 1)
                        ph.add("tensor", f3t)
                        _, t_ev = I("p5s")

                        def f3s(s_, mo=mo, t_mm=t_mm):
                            s_.wait_ge(SEM("p5t"), t_mm)
                            s_.activation(h3b[:, mo * 512:(mo + 1) * 512], ps5[mo][:, 0:512],
                                          AF.Gelu, bias=cnb3[:, mo:mo + 1],
                                          scale=1.0).then_inc(SEM("p5s"), 1)
                        ph.add("scalar", f3s)
                        bank_ev[mo] = cnts["p5s"]
                    h3done = cnts["p5s"]
                    _, t_mm = I("p5t")

                    def f4t(t, tc=tc, h3done=h3done, cyc=cyc, t_gfv=t_gfv):
                        t.wait_ge(SEM("p5s"), h3done)
                        if cyc == 0 and tc == 0:
                            t.wait_ge(SEM("p5v"), t_gfv)
                        for tj in range(4):
                            ti = tc * 4 + tj
                            for kc in range(2):
                                t.matmul(pt4[:, ti:ti + 1],
                                         h3b[:, kc * 512 + tj * 128: kc * 512 + (tj + 1) * 128],
                                         wc4b[:, kc:kc + 1],
                                         start=(kc == 0), stop=(kc == 1))
                        t.drain().then_inc(SEM("p5t"), 1)
                    ph.add("tensor", f4t)
                t4done = cnts["p5t"]
                _, t_sg = I("p5s")

                def fsg(s_, t4done=t4done, dwait=(dbg_prev_dump[0] if DBG else 0)):
                    s_.wait_ge(SEM("p5t"), t4done)
                    if dwait:
                        s_.wait_ge(qg, dwait)
                    s_.activation(tfc[:], pt4[:], AF.Sigmoid, bias=b4r[:, 0:1],
                                  scale=1.0).then_inc(SEM("p5s"), 1)
                ph.add("scalar", fsg)
                _, t_cp = I("p5v")

                def fcp(v, t_sg=t_sg, t_gfv=t_gfv):
                    v.wait_ge(SEM("p5s"), t_sg)
                    v.wait_ge(SEM("p5v"), t_gfv)
                    v.tensor_scalar(cplb[:], tfc[:], a6r[:, 0:1], bvr[:, 0:1], ALU.mult, ALU.add)
                    v.tensor_scalar(ncplb[:], tfc[:], gsc[:, 30:31], gsc[:, 31:32],
                                    ALU.mult, ALU.add).then_inc(SEM("p5v"), 1)
                ph.add("vector", fcp)
                if DBG and cyc == 0:
                    _, t_d5 = I("qg", 16 * 15)

                    def fd5(g, t_cp=t_cp):
                        ncd = nc.allow_non_contiguous_dma(reason="debug dump")
                        ncd.__enter__()
                        g.wait_ge(SEM("p5v"), t_cp)
                        g.dma_start(out=o_p5[:, 0:1], in_=gfr[:]).then_inc(qg, 16)
                        g.dma_start(out=o_p5[:, 1:2], in_=a6r[:]).then_inc(qg, 16)
                        g.dma_start(out=o_p5[:, 2:3], in_=bvr[:]).then_inc(qg, 16)
                        g.dma_start(out=o_p5[:, 16:32], in_=tfc[:]).then_inc(qg, 16)
                        g.dma_start(out=o_p5[:, 32:48], in_=cplb[:]).then_inc(qg, 16)
                        g.dma_start(out=o_p5[:, 3:11], in_=gsc[:, 8:16]).then_inc(qg, 16)
                        g.dma_start(out=o_p5[:, 48:56], in_=gsc[:, 16:24]).then_inc(qg, 16)
                        g.dma_start(out=o_p5b[:, 0:8], in_=scb[:, 0:8]).then_inc(qg, 16)
                        g.dma_start(out=o_p5[:, 11:12], in_=gfr[:]).then_inc(qg, 16)
                        g.dma_start(out=o_p5[:, 56:64], in_=gsc[:, 24:32]).then_inc(qg, 16)
                        g.dma_start(out=o_p5c[:, 0:8], in_=wgc1[:, 0:8]).then_inc(qg, 16)
                        g.dma_start(out=o_p5c[:, 8:16], in_=wgc1[:, 512:520]).then_inc(qg, 16)
                        g.dma_start(out=o_p5c[:, 16:24], in_=gcbf[:, 0:8]).then_inc(qg, 16)
                        g.dma_start(out=o_p5[:, 12:16], in_=scr[:, 512:516]).then_inc(qg, 16)
                        g.dma_start(out=o_p5[:, 60:64], in_=scr[:, 528:532]).then_inc(qg, 16)
                        ncd.__exit__(None, None, None)
                    ph.add("gpsimd", fd5)
                    dbg_prev_dump[0] = t_d5

                for ti in range(NT):
                    _, t_tw = I("p5v")

                    def ftw(v, ti=ti, t_cp=t_cp):
                        v.wait_ge(SEM("p5v"), t_cp)
                        q0 = xf[:, ti * D + 0:ti * D + 256]
                        q1 = xf[:, ti * D + 256:ti * D + 512]
                        q2 = xf[:, ti * D + 512:ti * D + 768]
                        q3 = xf[:, ti * D + 768:ti * D + 1024]
                        cp = cplb[:, ti:ti + 1]
                        ncp = ncplb[:, ti:ti + 1]
                        v.tensor_scalar_mul(scr[:, 0:256], q0, 1.0)
                        v.scalar_tensor_tensor(q0, q2, cp, q0, ALU.mult, ALU.add)
                        v.scalar_tensor_tensor(q2, scr[:, 0:256], ncp, q2, ALU.mult, ALU.add)
                        v.tensor_scalar_mul(scr[:, 0:256], q1, 1.0)
                        v.scalar_tensor_tensor(q1, q3, ncp, q1, ALU.mult, ALU.add)
                        ic = v.scalar_tensor_tensor(q3, scr[:, 0:256], cp, q3, ALU.mult, ALU.add)
                        ic.then_inc(SEM("p5v"), 1)
                    ph.add("vector", ftw)
                    if cyc < 2:
                        h1_read_done = cnts["p5t"]
                        _, t_cast = I("p5v")

                        def fcst(v, ti=ti, t_tw=t_tw, trg=refresh_tr[ti % 2]):
                            if trg:
                                v.wait_ge(qs, trg)
                            v.tensor_copy(scb[:, 2048 + (ti % 2) * 1024: 2048 + (ti % 2 + 1) * 1024],
                                          xf[:, ti * D:(ti + 1) * D]).then_inc(SEM("p5v"), 1)
                        ph.add("vector", fcst)
                        _, t_tr = I("qs", 16 * 8)

                        def ftr5(sy, ti=ti, t_cast=t_cast, h1_read_done=h1_read_done):
                            sy.wait_ge(SEM("p5v"), t_cast)
                            sy.wait_ge(SEM("p5t"), h1_read_done)
                            for fc in range(8):
                                sy.dma_start(
                                    out=fmA[:, fc * T + ti * 128: fc * T + (ti + 1) * 128],
                                    in_=scb[:, 2048 + (ti % 2) * 1024 + fc * 128:
                                            2048 + (ti % 2) * 1024 + (fc + 1) * 128],
                                    transpose=True).then_inc(qs, 16)
                        ph.add("sync", ftr5)
                        refresh_tr[ti % 2] = cnts["qs"]
                        refresh_q = cnts["qs"]
            ph.emit(); CKPT()

            ph = Ph()
            ln_a = 0
            tr_done2 = {0: 0, 1: 0}
            for ti in range(NT):
                pp = ti % 2
                _, t_ld = I("qg", 16)

                def fldx1(g, ti=ti, rr=ln_a):
                    if rr:
                        g.wait_ge(SEM("p5a"), rr)
                    g.dma_start(out=scr[:, 1024:2048],
                                in_=d_x1[:, ti * D:(ti + 1) * D]).then_inc(qg, 16)
                ph.add("gpsimd", fldx1)
                obf = scb.ap()[:, 2048 + pp * 1024: 2048 + (pp + 1) * 1024]
                sc_, tc = sched_ln(
                    ph,
                    (lambda ti=ti: xf[:, ti * D:(ti + 1) * D]),
                    (lambda: scr[:, 1024:2048]),
                    xf[:, ti * D:(ti + 1) * D],
                    obf,
                    d_x2[:, ti * D:(ti + 1) * D],
                    "p5",
                    extra_wait=(lambda t_ld=t_ld: (qg, t_ld)),
                    outbf_wait=(qs, tr_done2[pp]))
                ln_a = cnts["p5a"]
                _, t_tr = I("qs", 16 * 8)

                def ftr2(sy, ti=ti, pp=pp, tc=tc, obf=obf):
                    sy.wait_ge(SEM("p5c"), tc)
                    for fc in range(8):
                        sy.dma_start(
                            out=fmA[:, fc * T + ti * 128: fc * T + (ti + 1) * 128],
                            in_=obf[:, fc * 128:(fc + 1) * 128],
                            transpose=True).then_inc(qs, 16)
                ph.add("sync", ftr2)
                tr_done2[pp] = t_tr
            ph.emit(); CKPT()
            es_ps5.close()
            es_p5.close()

            es_qfm = ES()
            qfm = es_qfm.enter_context(nc.sbuf_tensor("qfm", [128, 8 * T], BF16, side="right"))
            es_p6a = ES()
            wL1 = es_p6a.enter_context(nc.sbuf_tensor("wL1", [128, 8 * 1024], BF16))
            vst6 = es_p6a.enter_context(nc.sbuf_tensor("vst6", [128, 2 * 1040], BF16))
            kst6 = es_p6a.enter_context(nc.sbuf_tensor("kst6", [128, 2 * 2048], BF16))
            es_ps6a = ES()
            PS6 = [es_ps6a.enter_context(nc.psum_tensor("ps6a%d" % i, [128, 512], F32))
                   for i in range(8)]

            ph = Ph()
            SEM("p6t"); SEM("p6v"); SEM("p6q")

            def wload6(src, gate_need):
                def fl(g, src=src, gate_need=gate_need):
                    if gate_need:
                        g.wait_ge(SEM("p6t"), gate_need)
                    for ki in range(8):
                        g.dma_start(out=wL1[:, ki * 1024:(ki + 1) * 1024],
                                    in_=src[ki * 128:(ki + 1) * 128, :]).then_inc(qg, 16)
                ph.add("gpsimd", fl)
                return I("qg", 16 * 8)[1]

            gate_wq6 = wload6(Wq, 0)
            bank_ev6 = {i: 0 for i in range(8)}
            for mo in range(8):
                grp = mo % 2
                gates = [bank_ev6[4 * grp + tc] for tc in range(4)]
                _, t_mm = I("p6t")

                def fq6(t, mo=mo, grp=grp, gates=gates, gate_w=gate_wq6):
                    if mo == 0:
                        t.wait_ge(qg, gate_w)
                    gmax = max(gates)
                    if gmax:
                        t.wait_ge(SEM("p6v"), gmax)
                    for ki in range(8):
                        for tc in range(4):
                            t.matmul(PS6[4 * grp + tc][:, 0:512],
                                     wL1[:, ki * 1024 + mo * 128: ki * 1024 + (mo + 1) * 128],
                                     fmA[:, ki * T + tc * 512: ki * T + (tc + 1) * 512],
                                     start=(ki == 0), stop=(ki == 7))
                    t.drain().then_inc(SEM("p6t"), 1)
                ph.add("tensor", fq6)

                def fq6v(v, mo=mo, grp=grp, t_mm=t_mm):
                    v.wait_ge(SEM("p6t"), t_mm)
                    for tc in range(4):
                        ic = v.tensor_copy(qfm[:, mo * T + tc * 512: mo * T + (tc + 1) * 512],
                                           PS6[4 * grp + tc][:, 0:512])
                        ic.then_inc(SEM("p6v"), 1)
                for tc in range(4):
                    bank_ev6[4 * grp + tc] = cnts["p6v"] + tc + 1
                I("p6v", 4)
                ph.add("vector", fq6v)
            t_q6t = cnts["p6t"]

            gate_wk6 = wload6(Wk, t_q6t)
            kst_dma = {0: 0, 1: 0}
            for mo in range(8):
                grp = mo % 2
                gates = [bank_ev6[4 * grp + tc] for tc in range(4)]
                _, t_mm = I("p6t")

                def fk6(t, mo=mo, grp=grp, gates=gates, gate_w=gate_wk6):
                    if mo == 0:
                        t.wait_ge(qg, gate_w)
                    gmax = max(gates)
                    if gmax:
                        t.wait_ge(SEM("p6v"), gmax)
                    for ki in range(8):
                        for tc in range(4):
                            t.matmul(PS6[4 * grp + tc][:, 0:512],
                                     wL1[:, ki * 1024 + mo * 128: ki * 1024 + (mo + 1) * 128],
                                     fmA[:, ki * T + tc * 512: ki * T + (tc + 1) * 512],
                                     start=(ki == 0), stop=(ki == 7))
                    t.drain().then_inc(SEM("p6t"), 1)
                ph.add("tensor", fk6)

                def fk6v(v, mo=mo, grp=grp, t_mm=t_mm, dgate=kst_dma[mo % 2]):
                    v.wait_ge(SEM("p6t"), t_mm)
                    if dgate:
                        v.wait_ge(qs, dgate)
                    for tc in range(4):
                        ic = v.tensor_copy(kst6[:, (mo % 2) * 2048 + tc * 512: (mo % 2) * 2048 + (tc + 1) * 512],
                                           PS6[4 * grp + tc][:, 0:512])
                        ic.then_inc(SEM("p6v"), 1)
                for tc in range(4):
                    bank_ev6[4 * grp + tc] = cnts["p6v"] + tc + 1
                t_kv = I("p6v", 4)[1]
                ph.add("vector", fk6v)
                _, t_kd = I("qs", 16)

                def fk6s(sy, mo=mo, t_kv=t_kv):
                    sy.wait_ge(SEM("p6v"), t_kv)
                    sy.dma_start(out=d_exi[:, mo * T:(mo + 1) * T],
                                 in_=kst6[:, (mo % 2) * 2048:(mo % 2 + 1) * 2048]).then_inc(qs, 16)
                ph.add("sync", fk6s)
                kst_dma[mo % 2] = t_kd
            t_k6t = cnts["p6t"]

            gate_wv6 = wload6(Wv, t_k6t)
            _, t_vm6 = I("ge")

            def fvm6(g):
                g.memset(vst6[:, 0:1040], 1.0)
                g.memset(vst6[:, 1040:2080], 1.0).then_inc(SEM("ge"), 1)
            ph.add("gpsimd", fvm6)
            vst_dma = {0: 0, 1: 0}
            for j in range(16):
                gates = [bank_ev6[(2 * j + half) % 4] for half in range(2)]
                _, t_mm = I("p6t")

                def fv6(t, j=j, gates=gates, gate_w=gate_wv6, t_vm6=t_vm6):
                    if j == 0:
                        t.wait_ge(qg, gate_w)
                        t.wait_ge(SEM("ge"), t_vm6)
                    gmax = max(gates)
                    if gmax:
                        t.wait_ge(SEM("p6v"), gmax)
                    for half in range(2):
                        ps = PS6[(2 * j + half) % 4]
                        for ki in range(8):
                            t.matmul(ps[:, 0:512],
                                     fmA[:, ki * T + j * 128: ki * T + (j + 1) * 128],
                                     wL1[:, ki * 1024 + half * 512: ki * 1024 + (half + 1) * 512],
                                     start=(ki == 0), stop=(ki == 7))
                    t.drain().then_inc(SEM("p6t"), 1)
                ph.add("tensor", fv6)

                def fv6v(v, j=j, t_mm=t_mm, dgate=vst_dma[j % 2]):
                    v.wait_ge(SEM("p6t"), t_mm)
                    if dgate:
                        v.wait_ge(qs, dgate)
                    for half in range(2):
                        ps = PS6[(2 * j + half) % 4]
                        base = (j % 2) * 1040 + half * 520
                        ic = v.tensor_copy(
                            vst6.ap()[:, base:base + 520].rearrange("p (h c) -> p h c", c=65)[:, :, 0:64],
                            ps.ap()[:, 0:512].rearrange("p (h c) -> p h c", c=64))
                        ic.then_inc(SEM("p6v"), 1)
                for half in range(2):
                    bank_ev6[(2 * j + half) % 4] = cnts["p6v"] + half + 1
                t_vv = I("p6v", 2)[1]
                ph.add("vector", fv6v)
                _, t_vd = I("qs", 16)

                def fv6s(sy, j=j, t_vv=t_vv):
                    sy.wait_ge(SEM("p6v"), t_vv)
                    sy.dma_start(out=d_exi[:, EXK + j * 1040: EXK + (j + 1) * 1040],
                                 in_=vst6[:, (j % 2) * 1040:(j % 2 + 1) * 1040]).then_inc(qs, 16)
                ph.add("sync", fv6s)
                vst_dma[j % 2] = t_vd
            t_all_st = cnts["qs"]
            _, t_pad = I("qg", 16)

            def fpad(g):
                g.memset(scb[:, 0:8], 0.0)
                g.dma_start(out=d_exi[:, EXK + EXV:EXW], in_=scb[:, 0:8]).then_inc(qg, 16)
            ph.add("gpsimd", fpad)
            _, t_cc6 = I("cc6")

            def fex(g, t_all_st=t_all_st, t_pad=t_pad):
                g.wait_ge(qs, t_all_st)
                g.wait_ge(qg, t_pad)
                g.collective_compute("AllGather", ALU.bypass, replica_groups=pairs,
                                     ins=[d_exi], outs=[d_exo]).then_inc(SEM("cc6"), 1)
            ph.add("gpsimd", fex)
            ph.emit(); CKPT()
            es_ps6a.close()
            es_p6a.close()
            es_xf.close()
            es_fm1.close()

            es_kv = ES()
            kful = es_kv.enter_context(nc.sbuf_tensor("kful", [128, 8 * S], BF16))
            vful = es_kv.enter_context(nc.sbuf_tensor("vful", [128, NKT * 1040], BF16))
            es_ps6b = ES()
            ps6s = [es_ps6b.enter_context(nc.psum_tensor("ps6s%d" % i, [128, 512], F32))
                    for i in range(4)]
            psv6 = es_ps6b.enter_context(nc.psum_tensor("psv6", [65, 16], F32))

            ph = Ph()
            _, t_asm = I("qg", 16 * (128 + 32))

            def fasm(g, t_cc6=t_cc6):
                g.wait_ge(SEM("cc6"), t_cc6)
                for ki in range(8):
                    for gb in range(16):
                        o = gb & 1
                        i_own = gb >> 1
                        g.dma_start(
                            out=kful[:, ki * S + gb * 256: ki * S + (gb + 1) * 256],
                            in_=d_exo[o * 128:(o + 1) * 128,
                                      ki * T + i_own * 256: ki * T + (i_own + 1) * 256]
                        ).then_inc(qg, 16)
                for kt in range(NKT):
                    o = (kt >> 1) & 1
                    j = ((kt >> 2) << 1) | (kt & 1)
                    g.dma_start(
                        out=vful[:, kt * 1040:(kt + 1) * 1040],
                        in_=d_exo[o * 128:(o + 1) * 128, EXK + j * 1040: EXK + (j + 1) * 1040]
                    ).then_inc(qg, 16)
            ph.add("gpsimd", fasm)

            SEM("p6bt"); SEM("p6bv")
            for kt in range(NKT):
                gate_ev = cnts["p6bv"] - 1 if kt >= 2 else 0
                _, t_mm = I("p6bt")

                def fvt6(t, kt=kt, t_asm=t_asm, gate_ev=gate_ev):
                    if kt == 0:
                        t.wait_ge(qg, t_asm)
                    if gate_ev:
                        t.wait_ge(SEM("p6bv"), gate_ev)
                    for h in range(H):
                        t.matmul(psv6[:, h:h + 1],
                                 vful[:, kt * 1040 + h * 65: kt * 1040 + h * 65 + 65],
                                 onescb[:], start=True, stop=True)
                    t.drain().then_inc(SEM("p6bt"), 1)
                ph.add("tensor", fvt6)
                _, t_ev = I("p6bv")

                def fvt6v(v, kt=kt, t_mm=t_mm):
                    v.wait_ge(SEM("p6bt"), t_mm)
                    ic = v.tensor_copy(
                        vtsb.ap()[0:65, :].rearrange("p (h k) -> p h k", k=NKT)[:, :, kt:kt + 1],
                        psv6.ap()[0:65, :].broadcast_to([65, 16, 1]))
                    ic.then_inc(SEM("p6bv"), 1)
                ph.add("vector", fvt6v)
            _, t_vs = I("p6bv")

            def fvs6(v, need=cnts["p6bv"] - 1):
                v.wait_ge(SEM("p6bv"), need)
                v.tensor_reduce(vsumb[0:65, 0:16],
                                vtsb.ap()[0:65, :].rearrange("p (h k) -> p h k", k=NKT),
                                AX.X, ALU.add)
                for t_ in range(NT):
                    kb = band_kbase(t_) // 128
                    v.tensor_reduce(scr[0:65, t_ * 16:(t_ + 1) * 16],
                                    vtsb.ap()[0:65, :].rearrange("p (h k) -> p h k", k=NKT)[:, :, kb:kb + 5],
                                    AX.X, ALU.add)
                for t_ in range(NT):
                    v.tensor_sub(cbuf[0:65, t_ * 16:(t_ + 1) * 16], vsumb[0:65, 0:16],
                                 scr[0:65, t_ * 16:(t_ + 1) * 16])
                v.engine_nop().then_inc(SEM("p6bv"), 1)
            ph.add("vector", fvs6)

            bank_red = {0: 0, 1: 0, 2: 0, 3: 0}
            for t_ in range(NT):
                for h in range(H):
                    ch, ro = h >> 1, 64 * (h & 1)
                    red_ticks = []
                    for kc in range(8):
                        pp = kc % 4
                        gate_red = bank_red[pp]
                        _, t_mm = I("p6bt")

                        def frm_t(t, t_=t_, h=h, ch=ch, ro=ro, kc=kc, pp=pp, gate_red=gate_red):
                            if gate_red:
                                t.wait_ge(SEM("p6bv"), gate_red)
                            t.matmul(ps6s[pp][:, 0:512],
                                     qfm[ro:ro + 64, ch * T + t_ * 128: ch * T + (t_ + 1) * 128],
                                     kful[ro:ro + 64, ch * S + kc * 512: ch * S + (kc + 1) * 512],
                                     start=True, stop=True)
                            t.drain().then_inc(SEM("p6bt"), 1)
                        ph.add("tensor", frm_t)
                        _, t_red = I("p6bv")

                        def frm_v(v, kc=kc, pp=pp, t_mm=t_mm):
                            v.wait_ge(SEM("p6bt"), t_mm)
                            v.reduce_max(scr[:, kc:kc + 1], ps6s[pp][:, 0:512],
                                         AX.X).then_inc(SEM("p6bv"), 1)
                        ph.add("vector", frm_v)
                        red_ticks.append(cnts["p6bv"])
                        bank_red[pp] = cnts["p6bv"]
                    _, t_fin = I("p6bv")

                    def frm_f(v, t_=t_, h=h, need=red_ticks[-1]):
                        v.wait_ge(SEM("p6bv"), need)
                        v.reduce_max(mbuf[:, t_ * 16 + h: t_ * 16 + h + 1], scr[:, 0:8],
                                     AX.X).then_inc(SEM("p6bv"), 1)
                    ph.add("vector", frm_f)
            _, t_nm8 = I("p6bv")

            def fnm8(s_, need=cnts["p6bv"] - 1):
                s_.wait_ge(SEM("p6bv"), need)
                s_.activation(nm8[:], mbuf[:], AF.Copy, bias=0.0,
                              scale=-0.125).then_inc(SEM("p6bv"), 1)
            ph.add("scalar", fnm8)
            ph.emit(); CKPT()
            es_ps6b.close()

            es_ps6c = ES()
            bmr = es_ps6c.enter_context(nc.sbuf_tensor("bmr", [128, 2 * BW], BF16))
            ps6b = [es_ps6c.enter_context(nc.psum_tensor("ps6b%d" % i, [128, BW], F32))
                    for i in range(2)]
            psc6 = [es_ps6c.enter_context(nc.psum_tensor("psc6%d" % i, [65, 128], F32))
                    for i in range(2)]
            pzb6 = es_ps6c.enter_context(nc.psum_tensor("pzb6", [64, 128], F32))
            pst6 = es_ps6c.enter_context(nc.psum_tensor("pst6", [128, BW], BF16))

            ph = Ph()
            SEM("c6t"); SEM("c6e"); SEM("c6m"); SEM("c6f"); SEM("c6pt"); SEM("c6tc")
            SEM("c6a"); SEM("c6z"); SEM("c6zb"); SEM("c6ev")
            NIT = NT * H
            tick_t = [0] * NIT; tick_e1 = [0] * NIT; tick_m = [0] * NIT
            tick_f = [0] * NIT; tick_pt = [0] * NIT; tick_tc = [0] * NIT
            tick_a = [0] * NIT; tick_z = [0] * NIT; tick_zb = [0] * NIT
            tick_ev = [0] * NIT; tick_st = [0] * NIT
            tick_bm = [0] * NT
            mask_last = [0] * NT

            def mk_bm(t_):
                gate = mask_last[t_ - 2] if t_ >= 2 else 0
                _, tk = I("qg", 16)

                def f(g, t_=t_, gate=gate):
                    if gate:
                        g.wait_ge(SEM("c6m"), gate)
                    g.dma_start(out=bmr[:, (t_ % 2) * BW:(t_ % 2 + 1) * BW],
                                in_=bmask[t_, :].rearrange("(p q) -> p q", p=128)
                                ).then_inc(qg, 16)
                ph.add("gpsimd", f)
                tick_bm[t_] = tk

            def mk_t(i):
                t_, h = divmod(i, H)
                ch, ro = h >> 1, 64 * (h & 1)
                pp = i % 2
                kb = band_kbase(t_)
                gate = tick_e1[i - 2] if i >= 2 else 0
                _, tk = I("c6t")

                def f(t, t_=t_, ch=ch, ro=ro, pp=pp, kb=kb, gate=gate):
                    if gate:
                        t.wait_ge(SEM("c6e"), gate)
                    t.matmul(ps6b[pp][:, 0:512],
                             qfm[ro:ro + 64, ch * T + t_ * 128: ch * T + (t_ + 1) * 128],
                             kful[ro:ro + 64, ch * S + kb: ch * S + kb + 512],
                             start=True, stop=True)
                    t.matmul(ps6b[pp][:, 512:BW],
                             qfm[ro:ro + 64, ch * T + t_ * 128: ch * T + (t_ + 1) * 128],
                             kful[ro:ro + 64, ch * S + kb + 512: ch * S + kb + BW],
                             start=True, stop=True)
                    t.drain().then_inc(SEM("c6t"), 1)
                ph.add("tensor", f)
                tick_t[i] = tk

            def mk_e1(i):
                t_, h = divmod(i, H)
                pp = i % 2
                gate = tick_f[i - 2] if i >= 2 else 0
                _, tk = I("c6e")

                def f(s_, t_=t_, h=h, pp=pp, gate=gate, tmm=tick_t[i]):
                    s_.wait_ge(SEM("c6t"), tmm)
                    if gate:
                        s_.wait_ge(SEM("c6f"), gate)
                    s_.activation(scb[:, pp * BW:(pp + 1) * BW], ps6b[pp][:, 0:BW],
                                  AF.Exp, bias=nm8[:, t_ * 16 + h: t_ * 16 + h + 1],
                                  scale=0.125).then_inc(SEM("c6e"), 1)
                ph.add("scalar", f)
                tick_e1[i] = tk

            def mk_m(i):
                t_, h = divmod(i, H)
                pp = i % 2
                _, tk = I("c6m")

                def f(v, t_=t_, pp=pp, te=tick_e1[i], tbm=tick_bm[t_]):
                    v.wait_ge(SEM("c6e"), te)
                    v.wait_ge(qg, tbm)
                    v.tensor_tensor(scb[:, pp * BW:(pp + 1) * BW],
                                    scb[:, pp * BW:(pp + 1) * BW],
                                    bmr[:, (t_ % 2) * BW:(t_ % 2 + 1) * BW],
                                    ALU.mult).then_inc(SEM("c6m"), 1)
                ph.add("vector", f)
                tick_m[i] = tk
                mask_last[t_] = tk

            def mk_e2(i):
                pp = i % 2
                gate = tick_pt[i - 2] if i >= 2 else 0
                _, tk = I("c6f")

                def f(s_, pp=pp, tm=tick_m[i], gate=gate, KTR=KTR):
                    s_.wait_ge(SEM("c6m"), tm)
                    if gate:
                        s_.wait_ge(qs if KTR else SEM("c6pt"), gate)
                    s_.activation(scb[:, 1280 + pp * BW: 1280 + (pp + 1) * BW],
                                  scb[:, pp * BW:(pp + 1) * BW],
                                  AF.Exp, bias=consts[:, 3:4],
                                  scale=1.0).then_inc(SEM("c6f"), 1)
                ph.add("scalar", f)
                tick_f[i] = tk

            KTR = int(os.environ.get("KTR", "0"))

            def mk_pt(i):
                if KTR:
                    pp = i % 2
                    gate = tick_a[i - 2] if i >= 2 else 0
                    _, tk = I("qs", 16 * 5)

                    def f(sy, pp=pp, tf=tick_f[i], gate=gate):
                        sy.wait_ge(SEM("c6f"), tf)
                        if gate:
                            sy.wait_ge(SEM("c6a"), gate)
                        for j in range(5):
                            sy.dma_start(
                                out=scb[:, 2560 + pp * BW + j * 128: 2560 + pp * BW + (j + 1) * 128],
                                in_=scb[:, 1280 + pp * BW + j * 128: 1280 + pp * BW + (j + 1) * 128],
                                transpose=True).then_inc(qs, 16)
                    ph.add("sync", f)
                    tick_tc[i] = tk
                    tick_pt[i] = tk
                    return
                pp = i % 2
                gate = tick_tc[i - 1] if i >= 1 else 0
                _, tk = I("c6pt")

                def f(t, pp=pp, tf=tick_f[i], gate=gate):
                    t.wait_ge(SEM("c6f"), tf)
                    if gate:
                        t.wait_ge(SEM("c6tc"), gate)
                    for j in range(5):
                        t.transpose(pst6[:, j * 128:(j + 1) * 128],
                                    scb[:, 1280 + pp * BW + j * 128: 1280 + pp * BW + (j + 1) * 128],
                                    idsb[:])
                    t.drain().then_inc(SEM("c6pt"), 1)
                ph.add("tensor", f)
                tick_pt[i] = tk

            def mk_tc(i):
                if KTR:
                    return
                pp = i % 2
                gate = tick_a[i - 2] if i >= 2 else 0
                _, tk = I("c6tc")

                def f(v, pp=pp, tp=tick_pt[i], gate=gate):
                    v.wait_ge(SEM("c6pt"), tp)
                    if gate:
                        v.wait_ge(SEM("c6a"), gate)
                    v.tensor_copy(scb[:, 2560 + pp * BW: 2560 + (pp + 1) * BW],
                                  pst6[:, 0:BW]).then_inc(SEM("c6tc"), 1)
                ph.add("vector", f)
                tick_tc[i] = tk

            def mk_av(i):
                t_, h = divmod(i, H)
                pp = i % 2
                kb = band_kbase(t_)
                gate = tick_ev[i - 2] if i >= 2 else 0
                _, tk = I("c6a")

                def f(t, h=h, pp=pp, kb=kb, tc_=tick_tc[i], gate=gate, KTR=KTR):
                    if KTR:
                        t.wait_ge(qs, tc_)
                    else:
                        t.wait_ge(SEM("c6tc"), tc_)
                    if gate:
                        t.wait_ge(SEM("c6ev"), gate)
                    for j in range(5):
                        kt = kb // 128 + j
                        t.matmul(psc6[pp][:, 0:128],
                                 vful[:, kt * 1040 + h * 65: kt * 1040 + h * 65 + 65],
                                 scb[:, 2560 + pp * BW + j * 128: 2560 + pp * BW + (j + 1) * 128],
                                 start=(j == 0), stop=(j == 4))
                    t.drain().then_inc(SEM("c6a"), 1)
                ph.add("tensor", f)
                tick_a[i] = tk

            def mk_z(i):
                pp = i % 2
                gate = tick_zb[i - 2] if i >= 2 else 0
                _, tk = I("c6z")

                def f(v, pp=pp, ta=tick_a[i], gate=gate):
                    v.wait_ge(SEM("c6a"), ta)
                    if gate:
                        v.wait_ge(SEM("c6zb"), gate)
                    zs = 32 + pp * 128
                    v.tensor_scalar_add(scr[0:1, zs:zs + 128], psc6[pp][64:65, 0:128],
                                        float(S - BW))
                    v.reciprocal(scr[0:1, zs:zs + 128],
                                 scr[0:1, zs:zs + 128]).then_inc(SEM("c6z"), 1)
                ph.add("vector", f)
                tick_z[i] = tk

            def mk_zt(i):
                pp = i % 2
                gate = tick_ev[i - 1] if i >= 1 else 0
                _, tk = I("c6zb")

                def f(t, pp=pp, tz=tick_z[i], gate=gate):
                    t.wait_ge(SEM("c6z"), tz)
                    if gate:
                        t.wait_ge(SEM("c6ev"), gate)
                    zs = 32 + pp * 128
                    t.matmul(pzb6[:, 0:128], ones1f[0:1, 0:64], scr[0:1, zs:zs + 128],
                             start=True, stop=True)
                    t.drain().then_inc(SEM("c6zb"), 1)
                ph.add("tensor", f)
                tick_zb[i] = tk

            def mk_ev(i):
                t_, h = divmod(i, H)
                pp = i % 2
                gate = tick_st[i - 2] if i >= 2 else 0
                _, tk = I("c6ev")

                def f(v, t_=t_, h=h, pp=pp, tzb=tick_zb[i], gate=gate):
                    v.wait_ge(SEM("c6zb"), tzb)
                    if gate:
                        v.wait_ge(qs, gate)
                    stg = scb[0:64, 3840 + pp * 128: 3840 + (pp + 1) * 128]
                    v.tensor_single_scalar(stg, psc6[pp][0:64, 0:128],
                                           cbuf[0:64, t_ * 16 + h: t_ * 16 + h + 1], ALU.add)
                    v.tensor_tensor(stg, stg, pzb6[0:64, 0:128],
                                    ALU.mult).then_inc(SEM("c6ev"), 1)
                ph.add("vector", f)
                tick_ev[i] = tk

            def mk_st(i):
                t_, h = divmod(i, H)
                pp = i % 2
                _, tk = I("qs", 16)

                def f(sy, t_=t_, h=h, pp=pp, tev=tick_ev[i]):
                    sy.wait_ge(SEM("c6ev"), tev)
                    sy.dma_start(out=d_ctx[h * 64:(h + 1) * 64, t_ * 128:(t_ + 1) * 128],
                                 in_=scb[0:64, 3840 + pp * 128: 3840 + (pp + 1) * 128]
                                 ).then_inc(qs, 16)
                ph.add("sync", f)
                tick_st[i] = tk

            KST = int(os.environ.get("KST", "2"))
            if KST == 2:
                for i in range(NIT):
                    t_, h = divmod(i, H)
                    if h == 0:
                        mk_bm(t_)
                    mk_t(i)
                    mk_e1(i)
                    mk_m(i)
                    if i >= 1:
                        j = i - 1
                        mk_e2(j); mk_pt(j); mk_tc(j); mk_av(j); mk_z(j); mk_zt(j)
                        mk_ev(j); mk_st(j)
                j = NIT - 1
                mk_e2(j); mk_pt(j); mk_tc(j); mk_av(j); mk_z(j); mk_zt(j)
                mk_ev(j); mk_st(j)
            elif KST:
                for i in range(NIT):
                    t_, h = divmod(i, H)
                    if h == 0:
                        mk_bm(t_)
                    mk_t(i); mk_e1(i); mk_m(i)
                    mk_e2(i); mk_pt(i); mk_tc(i); mk_av(i); mk_z(i); mk_zt(i)
                    mk_ev(i); mk_st(i)
            else:
                for i in range(NIT):
                    t_, h = divmod(i, H)
                    if h == 0:
                        mk_bm(t_)
                    mk_t(i)
                    mk_e1(i)
                    mk_m(i)
                    if i >= 1:
                        j = i - 1
                        mk_e2(j); mk_pt(j); mk_tc(j); mk_av(j); mk_z(j); mk_zt(j)
                    if i >= 2:
                        k = i - 2
                        mk_ev(k); mk_st(k)
                j = NIT - 1
                mk_e2(j); mk_pt(j); mk_tc(j); mk_av(j); mk_z(j); mk_zt(j)
                for k in (NIT - 2, NIT - 1):
                    mk_ev(k); mk_st(k)
            ph.emit(); CKPT()
            es_ps6c.close()
            es_kv.close()
            es_qfm.close()

            es_xf2 = ES()
            xf2 = es_xf2.enter_context(nc.sbuf_tensor("xf2", [128, NT * D], F32))
            es_p6d = ES()
            wo6 = es_p6d.enter_context(nc.sbuf_tensor("wo6", [128, 8 * 1024], BF16))
            ctxl6 = es_p6d.enter_context(nc.sbuf_tensor("ctxl6", [128, 2 * 1024], BF16))
            es_ps6d = ES()
            ps6d = [es_ps6d.enter_context(nc.psum_tensor("ps6d%d" % i, [128, 1024], F32))
                    for i in range(2)]

            def resid_x2(g, ti, dst):
                g.dma_start(out=dst, in_=d_x2[:, ti * D:(ti + 1) * D]).then_inc(qg, 16)

            ph = phase_proj_ln(Wo, zrow, d_ctx, resid_x2, None, "p6d",
                               wo6, ctxl6, ps6d, None, xf2, do_tr=False)
            ph.emit(); CKPT()
            es_ps6d.close()
            es_p6d.close()

            es_p7 = ES()
            alloc_quant(es_p7, "x")
            es_xqf = ES()
            xqf = es_xqf.enter_context(nc.sbuf_tensor("xqf", [128, 8 * T], BF16, side="right"))
            es_wgt = ES()
            wgt6 = es_wgt.enter_context(nc.sbuf_tensor("wgt6", [128, 2 * 2 * 8 * 128], BF16))
            es_ps7 = ES()
            PS7 = [es_ps7.enter_context(nc.psum_tensor("ps7_%d" % i, [128, 512], F32))
                   for i in range(4)]

            ph = Ph()
            SEM("x7")
            xq_tr = {0: 0, 1: 0}
            for ti in range(NT):
                pp = ti % 2
                (s3t3, _) = sched_quant(
                    ph,
                    (lambda ti=ti: xf2.ap()[:, ti * D:(ti + 1) * D]),
                    (lambda pp=pp: scb.ap()[:, pp * 1024:(pp + 1) * 1024]),
                    D, 128, None, "xq",
                    pre_wait=((qs, xq_tr[pp]) if xq_tr[pp] else None))
                _, t_tr = I("qs", 16 * 8)

                def fxtr(sy, ti=ti, pp=pp, s3t3=s3t3):
                    sy.wait_ge(*s3t3)
                    for fc in range(8):
                        sy.dma_start(
                            out=xqf[:, fc * T + ti * 128: fc * T + (ti + 1) * 128],
                            in_=scb[:, pp * 1024 + fc * 128: pp * 1024 + (fc + 1) * 128],
                            transpose=True).then_inc(qs, 16)
                ph.add("sync", fxtr)
                xq_tr[pp] = t_tr
            t_xq = cnts["qs"]

            SEM("g7t"); SEM("g7s"); SEM("g7v")
            hst_dma = {0: 0, 1: 0}
            wl_mm_done = {0: 0, 1: 0}
            for ncol in range(NC22):
                wp = ncol % 2
                gate_mm = wl_mm_done[wp]
                _, t_wl = I("qs", 16 * 16)

                def fwl(g_, ncol=ncol, wp=wp, gate_mm=gate_mm, t_ccw=cnts["ccw"]):
                    if ncol == 0:
                        g_.wait_ge(SEM("ccw"), t_ccw)
                        g_.wait_ge(qs, t_xq)
                    if gate_mm:
                        g_.wait_ge(SEM("g7t"), gate_mm)
                    for hc in range(8):
                        g_.dma_start(out=wgt6[:, wp * 2048 + hc * 128: wp * 2048 + (hc + 1) * 128],
                                     in_=d_wgo[ncol * 128:(ncol + 1) * 128,
                                               hc * 128:(hc + 1) * 128],
                                     transpose=True).then_inc(qs, 16)
                    for hc in range(8):
                        g_.dma_start(out=wgt6[:, wp * 2048 + 1024 + hc * 128: wp * 2048 + 1024 + (hc + 1) * 128],
                                     in_=d_wuo[ncol * 128:(ncol + 1) * 128,
                                               hc * 128:(hc + 1) * 128],
                                     transpose=True).then_inc(qs, 16)
                ph.add("sync", fwl)
                for tc in range(4):
                    n = ncol * 4 + tc
                    gate_ev = cnts["g7v"] - 1 if n >= 2 else 0
                    _, t_mm = I("g7t")

                    def fgu(t, ncol=ncol, tc=tc, t_wl=t_wl, gate_ev=gate_ev, pp=tc % 2,
                            wp=ncol % 2):
                        t.wait_ge(qs, t_wl)
                        if gate_ev:
                            t.wait_ge(SEM("g7v"), gate_ev)
                        for hc in range(8):
                            t.matmul(PS7[2 * pp][:, 0:512],
                                     wgt6[:, wp * 2048 + hc * 128: wp * 2048 + (hc + 1) * 128],
                                     xqf[:, hc * T + tc * 512:(hc) * T + (tc + 1) * 512],
                                     start=(hc == 0), stop=(hc == 7))
                        for hc in range(8):
                            t.matmul(PS7[2 * pp + 1][:, 0:512],
                                     wgt6[:, wp * 2048 + 1024 + hc * 128: wp * 2048 + 1024 + (hc + 1) * 128],
                                     xqf[:, hc * T + tc * 512:(hc) * T + (tc + 1) * 512],
                                     start=(hc == 0), stop=(hc == 7))
                        t.drain().then_inc(SEM("g7t"), 1)
                    ph.add("tensor", fgu)
                    wl_mm_done[ncol % 2] = cnts["g7t"]
                    _, t_sil = I("g7s")

                    def fsil(s_, t_mm=t_mm, pp=tc % 2):
                        s_.wait_ge(SEM("g7t"), t_mm)
                        s_.activation(scb[:, 2048 + pp * 512: 2048 + (pp + 1) * 512],
                                      PS7[2 * pp][:, 0:512], AF.Silu,
                                      bias=consts[:, 3:4], scale=1.0).then_inc(SEM("g7s"), 1)
                    ph.add("scalar", fsil)
                    _, t_hv = I("g7v")

                    def fhv(v, t_sil=t_sil, pp=tc % 2, dg=hst_dma[tc % 2]):
                        v.wait_ge(SEM("g7s"), t_sil)
                        if dg:
                            v.wait_ge(qs, dg)
                        v.tensor_tensor(scb[:, 3072 + pp * 512: 3072 + (pp + 1) * 512],
                                        scb[:, 2048 + pp * 512: 2048 + (pp + 1) * 512],
                                        PS7[2 * pp + 1][:, 0:512],
                                        ALU.mult).then_inc(SEM("g7v"), 1)
                    ph.add("vector", fhv)
                    _, t_hd = I("qs", 16)

                    def fhd(sy, ncol=ncol, tc=tc, t_hv=t_hv, pp=tc % 2):
                        sy.wait_ge(SEM("g7v"), t_hv)
                        sy.dma_start(out=d_h[:, ncol * T + tc * 512: ncol * T + (tc + 1) * 512],
                                     in_=scb[:, 3072 + pp * 512: 3072 + (pp + 1) * 512]
                                     ).then_inc(qs, 16)
                    ph.add("sync", fhd)
                    hst_dma[tc % 2] = t_hd
            ph.emit(); CKPT()
            es_ps7.close()
            es_xqf.close()
            es_wgt.close()

            es_wd = ES()
            wdT = es_wd.enter_context(nc.sbuf_tensor("wdT", [128, NC22 * 1024], BF16))
            hqf = es_wd.enter_context(nc.sbuf_tensor("hqf", [128, 2 * NC22 * 128], BF16))
            hbuf = es_wd.enter_context(nc.sbuf_tensor("hbuf", [128, 4 * DI], BF16))
            es_ps7d = ES()
            ps7d = [es_ps7d.enter_context(nc.psum_tensor("ps7d%d" % i, [128, 1024], F32))
                    for i in range(2)]
            pst7 = es_ps7d.enter_context(nc.psum_tensor("pst7", [128, DI], BF16))

            ph = Ph()
            if DBG:
                _, _t = I("qg", 16 * 4)

                def fdbg(g):
                    g.dma_start(out=o_x1, in_=d_x1).then_inc(qg, 16)
                    g.dma_start(out=o_x2, in_=d_x2).then_inc(qg, 16)
                    g.dma_start(out=o_ctx, in_=d_ctx).then_inc(qg, 16)
                    g.dma_start(out=o_x3[:, 0:NT * D], in_=xf2[:, 0:NT * D]).then_inc(qg, 16)
                ph.add("gpsimd", fdbg)
            qx = SEM("qx")
            _, t_wdl_s = I("qs", 16 * 11 * 8)

            def fwdl_s(sy):
                for ic in range(11):
                    for nch in range(8):
                        sy.dma_start(out=wdT[:, ic * 1024 + nch * 128: ic * 1024 + (nch + 1) * 128],
                                     in_=d_wdo[nch * 128:(nch + 1) * 128,
                                               ic * 128:(ic + 1) * 128],
                                     transpose=True).then_inc(qs, 16)
            ph.add("sync", fwdl_s)
            _, t_wdl_x = I("qx", 16 * 11 * 8)

            def fwdl_x(s_):
                for ic in range(11, NC22):
                    for nch in range(8):
                        s_.dma_start(out=wdT[:, ic * 1024 + nch * 128: ic * 1024 + (nch + 1) * 128],
                                     in_=d_wdo[nch * 128:(nch + 1) * 128,
                                               ic * 128:(ic + 1) * 128],
                                     transpose=True).then_inc(qx, 16)
            ph.add("scalar", fwdl_x)

            SEM("d7t"); SEM("hq1"); SEM("c7pt"); SEM("c7tc")
            ln_a7 = {0: 0, 1: 0}
            tk_v1 = [0] * NT
            tk_v3 = [(None, 0)] * NT
            tk_qt = [0] * NT
            tk_hc = [0] * NT
            tk_dn = [0] * NT
            out_ticks = []

            def mk_hl(ti):
                pp = ti % 2
                gate = tk_v1[ti - 2] if ti >= 2 else 0
                _, tk = I("qs", 16 * NC22)

                def f(sy, ti=ti, pp=pp, gate=gate):
                    if gate:
                        sy.wait_ge(SEM("hq1"), gate)
                    for ic in range(NC22):
                        sy.dma_start(out=hbuf[:, pp * 2 * DI + ic * 128: pp * 2 * DI + (ic + 1) * 128],
                                     in_=d_h[:, ic * T + ti * 128: ic * T + (ti + 1) * 128],
                                     transpose=True).then_inc(qs, 16)
                ph.add("sync", f)
                return tk

            def mk_qt(ti):
                pp = ti % 2
                gate = tk_hc[ti - 1] if ti >= 1 else 0
                _, tk = I("c7pt")

                def f(t, ti=ti, pp=pp, gate=gate, s3t3=tk_v3[ti]):
                    t.wait_ge(*s3t3)
                    if gate:
                        t.wait_ge(SEM("c7tc"), gate)
                    for ic in range(NC22):
                        t.transpose(pst7[:, ic * 128:(ic + 1) * 128],
                                    hbuf[:, pp * 2 * DI + DI + ic * 128: pp * 2 * DI + DI + (ic + 1) * 128],
                                    idsb[:])
                    t.drain().then_inc(SEM("c7pt"), 1)
                ph.add("tensor", f)
                tk_qt[ti] = tk

            def mk_hc(ti):
                pp = ti % 2
                gate = tk_dn[ti - 2] if ti >= 2 else 0
                _, tk = I("c7tc")

                def f(v, pp=pp, tq=tk_qt[ti], gate=gate):
                    v.wait_ge(SEM("c7pt"), tq)
                    if gate:
                        v.wait_ge(SEM("d7t"), gate)
                    v.tensor_copy(hqf[:, pp * NC22 * 128:(pp + 1) * NC22 * 128],
                                  pst7[:, 0:DI]).then_inc(SEM("c7tc"), 1)
                ph.add("vector", f)
                tk_hc[ti] = tk

            def mk_dn(ti):
                pp = ti % 2
                gate_ps = ln_a7[pp]
                _, tk = I("d7t")

                def f(t, ti=ti, pp=pp, thc=tk_hc[ti], gate_ps=gate_ps):
                    t.wait_ge(SEM("c7tc"), thc)
                    if ti == 0:
                        t.wait_ge(qs, t_wdl_s)
                        t.wait_ge(qx, t_wdl_x)
                    if gate_ps:
                        t.wait_ge(SEM("p7a"), gate_ps)
                    for ic in range(NC22):
                        for nh in range(2):
                            t.matmul(ps7d[pp][:, nh * 512:(nh + 1) * 512],
                                     hqf[:, pp * NC22 * 128 + ic * 128: pp * NC22 * 128 + (ic + 1) * 128],
                                     wdT[:, ic * 1024 + nh * 512: ic * 1024 + (nh + 1) * 512],
                                     start=(ic == 0), stop=(ic == NC22 - 1))
                    t.drain().then_inc(SEM("d7t"), 1)
                ph.add("tensor", f)
                tk_dn[ti] = tk

            def mk_ln_out(ti):
                pp = ti % 2
                sc_, tc_ = sched_ln(
                    ph,
                    (lambda pp=pp: ps7d[pp][:, 0:1024]),
                    (lambda ti=ti: xf2[:, ti * D:(ti + 1) * D]),
                    xf2[:, ti * D:(ti + 1) * D],
                    None,
                    None,
                    "p7",
                    extra_wait=(lambda tdn=tk_dn[ti]: (SEM("d7t"), tdn)))
                ln_a7[pp] = cnts["p7a"]
                _, t_out = I("qs", 16)

                def fout(sy, ti=ti, tc_=tc_):
                    sy.wait_ge(SEM("p7c"), tc_)
                    sy.dma_start(out=out_t[ti * 128:(ti + 1) * 128, :],
                                 in_=xf2[:, ti * D:(ti + 1) * D]).then_inc(qs, 16)
                ph.add("sync", fout)
                out_ticks.append(cnts["qs"])

            for ti in range(NT):
                pp = ti % 2
                t_hl = mk_hl(ti)
                qs_snapshot = cnts["qs"]
                pw = ((SEM("c7pt"), tk_qt[ti - 2]) if ti >= 2 else None)
                (s3t3, s1t1) = sched_quant(
                    ph,
                    (lambda ti=ti, pp=pp: hbuf.ap()[:, pp * 2 * DI: pp * 2 * DI + DI]),
                    (lambda ti=ti, pp=pp: hbuf.ap()[:, pp * 2 * DI + DI: pp * 2 * DI + 2 * DI]),
                    DI, 128, (qs, qs_snapshot), "hq", pre_wait=pw)
                tk_v1[ti] = s1t1[1]
                tk_v3[ti] = s3t3
                if ti >= 1:
                    mk_dn(ti - 1)
                    mk_ln_out(ti - 1)
                mk_qt(ti)
                mk_hc(ti)
            mk_dn(NT - 1)
            mk_ln_out(NT - 1)
            ph.emit(); CKPT()
            es_ps7d.close()
            es_wd.close()
            es_p7.close()
            es_xf2.close()

        except StopBuild:
            for e in reversed(es_list):
                try:
                    e.close()
                except Exception:
                    pass
        if phct[0] < 99 and KPH < 99:
            ph = Ph()

            def fzro(g):
                g.memset(scr[:, 0:1024], 0.0).then_inc(SEM("ge"), 1)
            tz = I("ge")[1]
            ph.add("gpsimd", fzro)

            def fdump(sy, tz=tz):
                sy.wait_ge(SEM("ge"), tz)
                for ti in range(NT):
                    sy.dma_start(out=out_t[ti * 128:(ti + 1) * 128, :],
                                 in_=scr[:, 0:1024]).then_inc(qs, 16)
            I("qs", 16 * NT)
            ph.add("sync", fdump)
            ph.emit()
    return nc


_NC_CACHE = {}


def _masks_for_core(par):
    cm = np.zeros((NBLK * 4, 128, 256), dtype=np.float32)
    for bi in range(NBLK):
        g = 2 * bi + par
        for j in range(4):
            kt = 4 * bi + j
            ii = kt * 128 + np.arange(128)[:, None]
            jj = g * 256 + np.arange(256)[None, :]
            cm[bi * 4 + j] = (ii <= jj).astype(np.float32)
    bm = np.zeros((NT, 128, BW), dtype=np.float32)
    for t_ in range(NT):
        gq = 4 * (t_ >> 1) + 2 * par + (t_ & 1)
        kb = band_kbase(t_)
        ii = gq * 128 + np.arange(128)[:, None]
        jj = kb + np.arange(BW)[None, :]
        d = jj - ii
        bm[t_] = ((d >= -128) & (d <= 127) & (jj < S)).astype(np.float32)
    bf = ml_dtypes.bfloat16
    return (cm.reshape(NBLK * 4, 128 * 256).astype(bf),
            bm.reshape(NT, 128 * BW).astype(bf))


def kernel(**inputs):
    ins = {k: np.asarray(v) for k, v in inputs.items()}
    x0 = ins["L0_attn_x"]
    if "nc" not in _NC_CACHE:
        _NC_CACHE["nc"] = build()
    nc = _NC_CACHE["nc"]
    in_maps = []
    for c in range(8):
        b, par = c // 2, c % 2
        own_rows = np.concatenate(
            [np.arange((2 * i + par) * 256, (2 * i + par) * 256 + 256) for i in range(8)])
        cm, bm = _masks_for_core(par)
        m = {
            "x0_own": np.ascontiguousarray(x0[b][own_rows]),
            "x0_full": np.ascontiguousarray(x0[b]),
            "w_qkv": ins["L0_attn_w_qkv"], "b_qkv": ins["L0_attn_b_qkv"],
            "w_out": ins["L0_attn_w_out"], "b_out": ins["L0_attn_b_out"],
            "cn_w1": ins["L0_ffn_cn_w1"], "cn_b1": ins["L0_ffn_cn_b1"],
            "cn_w2": ins["L0_ffn_cn_w2"], "cn_b2": ins["L0_ffn_cn_b2"],
            "cn_w3": ins["L0_ffn_cn_w3"], "cn_b3": ins["L0_ffn_cn_b3"],
            "cn_w4": ins["L0_ffn_cn_w4"], "cn_b4": ins["L0_ffn_cn_b4"],
            "gc_w1": ins["L0_ffn_gc_w1"], "gc_b1": ins["L0_ffn_gc_b1"],
            "gc_w2": ins["L0_ffn_gc_w2"], "gc_b2": ins["L0_ffn_gc_b2"],
            "gc_w3": ins["L0_ffn_gc_w3"], "gc_b3": ins["L0_ffn_gc_b3"],
            "ar": ins["L0_ffn_adaptive_range"],
            "Wq": ins["L1_attn_Wq"], "Wk": ins["L1_attn_Wk"],
            "Wv": ins["L1_attn_Wv"], "Wo": ins["L1_attn_Wo"],
            "wg_sl": np.ascontiguousarray(ins["L1_ffn_w_gate"][c * 352:(c + 1) * 352]),
            "wu_sl": np.ascontiguousarray(ins["L1_ffn_w_up"][c * 352:(c + 1) * 352]),
            "wd_sl": np.ascontiguousarray(ins["L1_ffn_w_down"][c * 128:(c + 1) * 128]),
            "cmask": cm, "bmask": bm,
        "identb": np.eye(128, dtype=ml_dtypes.bfloat16),
        }
        in_maps.append(m)
    res = run_bass_kernel_spmd(nc, in_maps, core_ids=list(range(8)))
    hook = globals().get("_DBG_HOOK")
    if hook is not None:
        hook(res)
    out = np.zeros((B, S, D), dtype=np.float32)
    for c in range(8):
        b, par = c // 2, c % 2
        o = res.results[c]["out"]
        for i in range(8):
            g = 2 * i + par
            out[b, g * 256:(g + 1) * 256] = o[i * 256:(i + 1) * 256]
    return out


# version 10
# speedup vs baseline: 1.3050x; 1.0647x over previous
import numpy as np
import ml_dtypes
from contextlib import ExitStack

import os
import concourse.bass as bass
import concourse.mybir as mybir
from concourse.bass_utils import run_bass_kernel_spmd

F32 = mybir.dt.float32
BF16 = mybir.dt.bfloat16
U8 = mybir.dt.uint8
AF = mybir.ActivationFunctionType
ALU = mybir.AluOpType
AX = mybir.AxisListType

B, S, D = 4, 4096, 1024
H, HD = 16, 64
T, NT, NBLK, NKT = 2048, 16, 8, 32
BW = 640
DI = 2816
NC22 = DI // 128
VC = float(2 ** 22 + 1)
MG = float(3 * 2 ** 22)
ENGS = ("tensor", "vector", "scalar", "gpsimd", "sync")


def band_kbase(t):
    gq0 = 4 * (t >> 1) + (t & 1)
    return min(max(gq0 * 128 - 128, 0), S - BW)


def build():
    nc = bass.Bass("TRN2", target_bir_lowering=False)

    def inp(name, shape, dt=F32):
        return nc.dram_tensor(name, shape, dt, kind="ExternalInput").ap()

    x0_own = inp("x0_own", [T, D])
    x0_full = inp("x0_full", [S, D])
    w_qkv = inp("w_qkv", [D, 3 * D]); b_qkv = inp("b_qkv", [3 * D])
    w_out = inp("w_out", [D, D]); b_out = inp("b_out", [D])
    cn_w1 = inp("cn_w1", [D, D]); cn_b1 = inp("cn_b1", [D])
    cn_w2 = inp("cn_w2", [D, 512]); cn_b2 = inp("cn_b2", [512])
    cn_w3 = inp("cn_w3", [512, 256]); cn_b3 = inp("cn_b3", [256])
    cn_w4 = inp("cn_w4", [256, 1]); cn_b4 = inp("cn_b4", [1])
    gc_w1 = inp("gc_w1", [D, 512]); gc_b1 = inp("gc_b1", [512])
    gc_w2 = inp("gc_w2", [512, 256]); gc_b2 = inp("gc_b2", [256])
    gc_w3 = inp("gc_w3", [256, 1]); gc_b3 = inp("gc_b3", [1])
    ar_in = inp("ar", [1])
    Wq = inp("Wq", [D, D]); Wk = inp("Wk", [D, D])
    Wv = inp("Wv", [D, D]); Wo = inp("Wo", [D, D])
    wg_sl = inp("wg_sl", [352, D]); wu_sl = inp("wu_sl", [352, D])
    wd_sl = inp("wd_sl", [128, DI])
    cmask = inp("cmask", [NBLK * 4, 128 * 256], BF16)
    identb = inp("identb", [128, 128], BF16)
    bmask = inp("bmask", [NT, 128 * BW], BF16)

    out_t = nc.dram_tensor("out", [T, D], F32, kind="ExternalOutput").ap()
    DBG = int(os.environ.get("DBG", "0"))
    if DBG:
        o_x1 = nc.dram_tensor("o_x1", [128, NT * D], F32, kind="ExternalOutput").ap()
        o_x2 = nc.dram_tensor("o_x2", [128, NT * D], F32, kind="ExternalOutput").ap()
        o_x3 = nc.dram_tensor("o_x3", [128, NT * D], F32, kind="ExternalOutput").ap()
        o_ctx = nc.dram_tensor("o_ctx", [D, T], BF16, kind="ExternalOutput").ap()
        o_p5 = nc.dram_tensor("o_p5", [128, 64], F32, kind="ExternalOutput").ap()
        o_p5b = nc.dram_tensor("o_p5b", [128, 8], BF16, kind="ExternalOutput").ap()
        o_p5c = nc.dram_tensor("o_p5c", [128, 24], BF16, kind="ExternalOutput").ap()

    def dram(name, shape, dt=BF16):
        return nc.dram_tensor(name, shape, dt).ap()

    d_v = dram("d_v", [128, NKT * 1040])
    d_x1 = dram("d_x1", [128, NT * D], F32)
    d_x2 = dram("d_x2", [128, NT * D], F32)
    d_ctx = dram("d_ctx", [D, T])
    d_h = dram("d_h", [128, NC22 * T])
    EXK = 8 * T; EXV = 16 * 1040; EXW = EXK + EXV + 8
    d_exi = dram("d_exi", [128, EXW]); d_exo = dram("d_exo", [256, EXW])
    d_gci = dram("d_gci", [128, 8], F32); d_gco = dram("d_gco", [128, 8], F32)
    d_wgi = dram("d_wgi", [352, D]); d_wgo = dram("d_wgo", [DI, D])
    d_wui = dram("d_wui", [352, D]); d_wuo = dram("d_wuo", [DI, D])
    d_wdi = dram("d_wdi", [128, DI]); d_wdo = dram("d_wdo", [D, DI])

    pairs = [[0, 1], [2, 3], [4, 5], [6, 7]]
    allg = [list(range(8))]

    sems, cnts = {}, {}
    stk = ExitStack()
    with stk:
        def SEM(n):
            if n not in sems:
                sems[n] = stk.enter_context(nc.semaphore(n))
                cnts[n] = 0
            return sems[n]

        def I(n, k=1):
            s = SEM(n)
            cnts[n] += k
            return s, cnts[n]

        def C(n):
            SEM(n)
            return cnts[n]

        sb = lambda n, s, d=F32: stk.enter_context(nc.sbuf_tensor(n, s, d))

        consts = sb("consts", [128, 8])
        ones1b = sb("ones1b", [1, 128], BF16)
        ones1f = sb("ones1f", [1, 128], F32)
        onescb = sb("onescb", [128, 1], BF16)
        bq_sb = sb("bq_sb", [128, 16])
        bv_row = sb("bv_row", [1, D], BF16)
        bo_row = sb("bo_row", [1, D], BF16)
        cnb1 = sb("cnb1", [128, 8]); cnb2 = sb("cnb2", [128, 4]); cnb3 = sb("cnb3", [128, 2])
        gcb1 = sb("gcb1", [128, 4]); gcb2 = sb("gcb2", [128, 2])
        b4r = sb("b4r", [128, 1]); gb3r = sb("gb3r", [128, 1]); arr = sb("arr", [128, 1])
        mbuf = sb("mbuf", [128, NT * H])
        nm8 = sb("nm8", [128, NT * H])
        vtsb = sb("vtsb", [128, NKT * H])
        vsumb = sb("vsumb", [128, H])
        cbuf = sb("cbuf", [128, NT * H])
        tfc = sb("tfc", [128, NT]); cplb = sb("cplb", [128, NT]); ncplb = sb("ncplb", [128, NT])
        gfr = sb("gfr", [128, 1]); a6r = sb("a6r", [128, 1]); bvr = sb("bvr", [128, 1])
        lnst = sb("lnst", [128, 12])
        gsc = sb("gsc", [128, 32])
        scr = sb("scr", [128, 2048])
        scb = sb("scb", [128, 4096], BF16)
        zrow = sb("zrow", [1, D], BF16)
        idsb = sb("idsb", [128, 128], BF16)
        QT = {}

        def alloc_quant(es, tagn):
            QT["xn"] = es.enter_context(nc.sbuf_tensor("qxn" + tagn, [128, DI], F32))
            QT["A"] = es.enter_context(nc.sbuf_tensor("qA" + tagn, [128, DI], F32))
            QT["B"] = es.enter_context(nc.sbuf_tensor("qB" + tagn, [128, DI], F32))
            QT["am"] = es.enter_context(nc.sbuf_tensor("qam" + tagn, [128, DI // 16], F32))
            QT["sc"] = es.enter_context(nc.sbuf_tensor("qsc" + tagn, [128, DI // 16], F32))
            QT["rc"] = es.enter_context(nc.sbuf_tensor("qrc" + tagn, [128, DI // 16], F32))
            QT["mk"] = es.enter_context(nc.sbuf_tensor("qmk" + tagn, [128, DI], U8))

        qs = SEM("qs"); qg = SEM("qg"); BAR = SEM("bar")
        nbar = [0]

        class Ph:
            def __init__(self):
                self.ops = {e: [] for e in ENGS}

            def add(self, e, fn):
                self.ops[e].append(fn)

            def emit(self):
                base = nbar[0] * len(ENGS)
                qsv, qgv = cnts["qs"], cnts["qg"]
                with nc.Block() as blk:
                    for ename in ENGS:
                        def mk(ename=ename):
                            def body(eng):
                                eng.wait_ge(BAR, base)
                                for fn in self.ops[ename]:
                                    fn(eng)
                                if ename == "sync":
                                    eng.wait_ge(qs, qsv)
                                if ename == "gpsimd":
                                    eng.wait_ge(qg, qgv)
                                eng.drain().then_inc(BAR, 1)
                            return body
                        getattr(blk, ename)(mk())
                cnts["bar"] += len(ENGS)
                nbar[0] += 1

        KPH = int(os.environ.get("KPH", "99"))
        SEC2 = int(os.environ.get("SEC2", "9"))
        SKIPW = int(os.environ.get("SKIPW", "0"))
        KMO = int(os.environ.get("KMO", "8"))
        KEV = int(os.environ.get("KEV", "1"))
        phct = [0]
        es_list = []

        def ES():
            e = ExitStack()
            es_list.append(e)
            return e

        class StopBuild(Exception):
            pass

        def CKPT():
            phct[0] += 1
            if phct[0] >= KPH:
                raise StopBuild()

        def sched_quant(ph, src_fn, dst_fn, n, rows, gate, tag, pre_wait=None):
            r = slice(0, rows)
            nb = n // 16
            s1, t1 = I(tag + "1")

            def fv1(v):
                src = src_fn()
                qxn, qA, qB = QT["xn"], QT["A"], QT["B"]
                qam, qsc, qrc = QT["am"], QT["sc"], QT["rc"]
                if gate is not None:
                    v.wait_ge(*gate)
                v.tensor_reduce(qam[r, 0:nb], src.rearrange("p (g e) -> p g e", e=16),
                                AX.X, ALU.max, apply_absolute_value=True)
                v.drain()
                v.tensor_scalar(qsc[r, 0:nb], qam[r, 0:nb], 1.0 / 6.0, 1e-12, ALU.mult, ALU.max)
                v.reciprocal(qrc[r, 0:nb], qsc[r, 0:nb])
                v.tensor_tensor(qxn.ap()[r, 0:n].rearrange("p (g e) -> p g e", e=16),
                                src.rearrange("p (g e) -> p g e", e=16),
                                qrc.ap()[r, 0:nb].broadcast_to([rows, nb, 16]),
                                ALU.mult)
                v.scalar_tensor_tensor(qB[r, 0:n], qxn[r, 0:n], VC, qxn[r, 0:n], ALU.mult, ALU.subtract)
                v.scalar_tensor_tensor(qA[r, 0:n], qxn[r, 0:n], VC, qB[r, 0:n],
                                       ALU.mult, ALU.subtract).then_inc(s1, 1)
            ph.add("vector", fv1)
            s2, t2 = I(tag + "2")

            def fs(s_):
                qxn, qB = QT["xn"], QT["B"]
                s_.wait_ge(s1, t1)
                s_.activation(qB[r, 0:n], qxn[r, 0:n], AF.Copy, bias=MG, scale=2.0)
                s_.activation(qB[r, 0:n], qB[r, 0:n], AF.Copy, bias=-MG / 2.0,
                              scale=0.5).then_inc(s2, 1)
            ph.add("scalar", fs)
            s3, t3 = I(tag + "3")

            def fv2(v):
                dst = dst_fn()
                qxn, qA, qB, qsc, msku = QT["xn"], QT["A"], QT["B"], QT["sc"], QT["mk"]
                v.wait_ge(s2, t2)
                if pre_wait is not None:
                    v.wait_ge(*pre_wait)
                v.scalar_tensor_tensor(qxn[r, 0:n], qA[r, 0:n], 1.0, qA[r, 0:n], ALU.mult, ALU.mult)
                v.tensor_single_scalar(msku[r, 0:n], qxn[r, 0:n], 1.0, ALU.is_ge)
                v.select(qxn[r, 0:n], msku[r, 0:n], qA[r, 0:n], qB[r, 0:n])
                v.tensor_tensor(dst.rearrange("p (g e) -> p g e", e=16),
                                qxn.ap()[r, 0:n].rearrange("p (g e) -> p g e", e=16),
                                qsc.ap()[r, 0:nb].broadcast_to([rows, nb, 16]),
                                ALU.mult).then_inc(s3, 1)
            ph.add("vector", fv2)
            return (s3, t3), (s1, t1)

        def sched_ln(ph, psum_fn, resid_fn, out32, outbf, spill, tag, extra_wait=None,
                     outbf_wait=None):
            sa, ta = I(tag + "a")

            def fva(v):
                ps = psum_fn(); res = resid_fn()
                if extra_wait is not None:
                    v.wait_ge(*extra_wait())
                v.scalar_tensor_tensor(out32, ps, 1.0, res, ALU.mult, ALU.add,
                                       accum_out=lnst[:, 0:1])
                v.scalar_tensor_tensor(scr[:, 0:1024], out32, 1.0, out32,
                                       ALU.mult, ALU.mult, accum_out=lnst[:, 1:2])
                v.tensor_scalar_mul(lnst[:, 2:3], lnst[:, 0:1], -1.0 / 1024.0)
                v.tensor_scalar_mul(lnst[:, 3:4], lnst[:, 1:2], 1.0 / 1024.0)
                v.scalar_tensor_tensor(lnst[:, 3:4], lnst[:, 2:3], lnst[:, 2:3],
                                       lnst[:, 3:4], ALU.mult, ALU.subtract).then_inc(sa, 1)
            ph.add("vector", fva)
            sbv, tb = I(tag + "b")

            def fsc(s_):
                s_.wait_ge(sa, ta)
                s_.activation(lnst[:, 5:6], lnst[:, 3:4], AF.Sqrt,
                              bias=consts[:, 1:2], scale=-1.0).then_inc(sbv, 1)
            ph.add("scalar", fsc)
            sc_, tc = I(tag + "c")

            def fvc(v):
                v.wait_ge(sbv, tb)
                if outbf_wait is not None and outbf_wait[1]:
                    v.wait_ge(*outbf_wait)
                v.reciprocal(lnst[:, 6:7], lnst[:, 5:6])
                v.tensor_scalar(out32, out32, lnst[:, 2:3], lnst[:, 6:7], ALU.add, ALU.mult)
                if outbf is not None:
                    v.tensor_copy(outbf, out32).then_inc(sc_, 1)
                else:
                    v.engine_nop().then_inc(sc_, 1)
            ph.add("vector", fvc)
            if spill is not None:
                _, tq = I("qs", 16)

                def fsp(sy):
                    sy.wait_ge(sc_, tc)
                    sy.dma_start(out=spill, in_=out32).then_inc(qs, 16)
                ph.add("sync", fsp)
            return sc_, tc

        try:
            es_p1 = ES()
            wsl = es_p1.enter_context(nc.sbuf_tensor("wsl", [128, DI], F32))
            wsq = es_p1.enter_context(nc.sbuf_tensor("wsq", [128, DI], BF16))
            alloc_quant(es_p1, "w")

            ph = Ph()

            def f_pre(g):
                ncd = nc.allow_non_contiguous_dma(reason="tiny bias/const loads")
                ncd.__enter__()
                g.memset(consts[:, 0:1], -4.0)
                g.memset(consts[:, 1:2], 1e-5)
                g.memset(consts[:, 2:3], -1.0)
                g.memset(consts[:, 3:4], 0.0)
                g.memset(consts[:, 4:5], 1.0)
                g.memset(ones1b[:], 1.0)
                g.memset(ones1f[:], 1.0)
                g.memset(onescb[:], 1.0)
                g.memset(zrow[:], 0.0)
                for src, dst in [(b_qkv[0:1024], bq_sb[:, 0:8]), (b_qkv[1024:2048], bq_sb[:, 8:16]),
                                 (cn_b1, cnb1[:]), (cn_b2, cnb2[:]), (cn_b3, cnb3[:]),
                                 (gc_b1, gcb1[:]), (gc_b2, gcb2[:])]:
                    g.dma_start(out=dst, in_=src.rearrange("(c p) -> p c", p=128)).then_inc(qg, 16)
                g.dma_start(out=bv_row[:], in_=b_qkv[2048:3072].rearrange("(a f) -> a f", a=1)).then_inc(qg, 16)
                g.dma_start(out=bo_row[:], in_=b_out.rearrange("(a f) -> a f", a=1)).then_inc(qg, 16)
                g.dma_start(out=idsb[:], in_=identb).then_inc(qg, 16)
                for src, dst in [(cn_b4, b4r), (gc_b3, gb3r), (ar_in, arr)]:
                    g.dma_start(out=dst[:],
                                in_=src.broadcast_to([1, 128]).rearrange("a p -> p a")).then_inc(qg, 16)
                g.drain().then_inc(SEM("pre"), 1)
                ncd.__exit__(None, None, None)
            I("qg", 16 * 13)
            I("pre")
            ph.add("gpsimd", f_pre)

            wslices = [] if SKIPW else [
                       (wg_sl, 0, 128, d_wgi), (wg_sl, 128, 128, d_wgi), (wg_sl, 256, 96, d_wgi),
                       (wu_sl, 0, 128, d_wui), (wu_sl, 128, 128, d_wui), (wu_sl, 256, 96, d_wui),
                       (wd_sl, 0, 128, d_wdi)]
            last_store_tick = 0
            prev_v1 = None
            prev_store = None
            for j, (src, r0, rr, dsti) in enumerate(wslices):
                n = DI if j == 6 else D
                _, tqg = I("qg", 16)

                def fload(g, src=src, r0=r0, rr=rr, n=n, pv=prev_v1):
                    if pv is not None:
                        g.wait_ge(*pv)
                    g.dma_start(out=wsl[0:rr, 0:n], in_=src[r0:r0 + rr, :]).then_inc(qg, 16)
                ph.add("gpsimd", fload)
                pw = (qg, prev_store) if prev_store is not None else None
                (s3t3, s1t1) = sched_quant(ph, (lambda rr=rr, n=n: wsl.ap()[0:rr, 0:n]),
                                           (lambda rr=rr, n=n: wsq.ap()[0:rr, 0:n]),
                                           n, rr, (qg, tqg), "wq", pre_wait=pw)
                prev_v1 = s1t1
                _, last_store_tick = I("qg", 16)
                prev_store = last_store_tick

                def fstore(g, r0=r0, rr=rr, n=n, dsti=dsti, s3t3=s3t3):
                    g.wait_ge(*s3t3)
                    g.dma_start(out=dsti[r0:r0 + rr, :], in_=wsq[0:rr, 0:n]).then_inc(qg, 16)
                ph.add("gpsimd", fstore)

            sccw, _ = I("ccw", 3)

            def fcoll(g, gate=last_store_tick):
                if SKIPW:
                    g.drain().then_inc(sccw, 3)
                    return
                g.wait_ge(qg, gate)
                for di, do in [(d_wgi, d_wgo), (d_wui, d_wuo), (d_wdi, d_wdo)]:
                    g.collective_compute("AllGather", ALU.bypass, replica_groups=allg,
                                         ins=[di], outs=[do]).then_inc(sccw, 1)
            ph.add("gpsimd", fcoll)
            ph.emit(); CKPT()
            es_p1.close()

            es_fm1 = ES()
            fmA = es_fm1.enter_context(nc.sbuf_tensor("fmA", [128, NT * D], BF16))
            es_kfm = ES()
            kfm = es_kfm.enter_context(nc.sbuf_tensor("kfm", [128, 8 * S], BF16))
            es_p2 = ES()
            wqs = es_p2.enter_context(nc.sbuf_tensor("wqs", [128, 8 * 1024], BF16))
            vstg = es_p2.enter_context(nc.sbuf_tensor("vstg", [128, 2 * 1040], BF16))
            es_x0h = ES()
            x0h = es_x0h.enter_context(nc.sbuf_tensor("x0h", [128, 8 * T], BF16, side="right"))
            es_ps2 = ES()
            PSQ = [es_ps2.enter_context(nc.psum_tensor("ps2_%d" % i, [128, 512], F32))
                   for i in range(8)]

            ph = Ph()
            SEM("p2t"); SEM("p2v"); SEM("p2vt"); SEM("p2vv"); SEM("ge")
            _, t_vm = I("ge")

            def fv_init(g):
                g.memset(vstg[:, 0:1040], 1.0)
                g.memset(vstg[:, 1040:2080], 1.0).then_inc(SEM("ge"), 1)
            ph.add("gpsimd", fv_init)

            bank_k = {i: 0 for i in range(8)}
            bank_v = {i: 0 for i in range(8)}
            qs0_v = cnts["qs"]
            nvst = 0
            for sh in range(2):
                prev_reads = cnts["p2vt"]
                qs0_h = cnts["qs"]
                for i in range(NT):
                    qs_gate = qs0_h + 16 * 8 * (i - 3) if i >= 4 else None
                    _, tqg = I("qg", 16)

                    def flh(g, i=i, sh=sh, qs_gate=qs_gate, prev_reads=prev_reads):
                        if i == 0 and prev_reads:
                            g.wait_ge(SEM("p2vt"), prev_reads)
                        if qs_gate is not None:
                            g.wait_ge(qs, qs_gate)
                        g.dma_start(out=scb[:, (i % 4) * D:(i % 4 + 1) * D],
                                    in_=x0_full[sh * T + i * 128: sh * T + (i + 1) * 128, :]
                                    ).then_inc(qg, 16)
                    ph.add("gpsimd", flh)
                    _, _tq = I("qs", 16 * 8)

                    def fth(sy, i=i, gate=tqg, prev_reads=prev_reads):
                        sy.wait_ge(qg, gate)
                        if i == 0 and prev_reads:
                            sy.wait_ge(SEM("p2vt"), prev_reads)
                        for fc in range(8):
                            sy.dma_start(
                                out=x0h[:, fc * T + i * 128: fc * T + (i + 1) * 128],
                                in_=scb[:, (i % 4) * D + fc * 128:(i % 4) * D + (fc + 1) * 128],
                                transpose=True).then_inc(qs, 16)
                    ph.add("sync", fth)
                t_x0h = cnts["qs"]

                if SEC2 < 2:
                    continue
                gate_prev_w = cnts["p2vt"]
                _, gate_wk = I("qg", 16 * 8)

                NOCAST = int(os.environ.get("NOCAST", "0"))
                SERK = int(os.environ.get("SERK", "0"))

                def flk(g, gate_prev_w=gate_prev_w, NOCAST=NOCAST, SERK=SERK, t_x0h=t_x0h):
                    if SERK:
                        g.wait_ge(qs, t_x0h)
                    if gate_prev_w:
                        g.wait_ge(SEM("p2vt"), gate_prev_w)
                    for ki in range(8):
                        if NOCAST:
                            g.dma_start(out=wqs[:, ki * 1024:(ki + 1) * 1024],
                                        in_=d_v[:, ki * 1024:(ki + 1) * 1024]).then_inc(qg, 16)
                        else:
                            g.dma_start(out=wqs[:, ki * 1024:(ki + 1) * 1024],
                                        in_=w_qkv[ki * 128:(ki + 1) * 128, 1024:2048]).then_inc(qg, 16)
                ph.add("gpsimd", flk)

                for mo in range(KMO):
                    grp = mo % 2
                    gates = [bank_k[4 * grp + tc] for tc in range(4)]
                    _, t_mm = I("p2t")

                    def fk_t(t, mo=mo, sh=sh, grp=grp, gates=gates, gate_wk=gate_wk, t_x0h=t_x0h):
                        if mo == 0:
                            t.wait_ge(qg, gate_wk)
                            t.wait_ge(qs, t_x0h)
                        gmax = max(gates)
                        if gmax:
                            t.wait_ge(SEM("p2v"), gmax)
                        KMM = int(os.environ.get("KMM", "0"))
                        if KMM == 3:
                            pass
                        elif KMM == 1:
                            t.matmul(PSQ[0][:, 0:512], wqs[:, 0:128], x0h[:, 0:512],
                                     start=True, stop=True)
                        else:
                            for tc in range(4):
                                for ki in range(8):
                                    t.matmul(PSQ[4 * grp + tc][:, 0:512],
                                             wqs[:, ki * 1024 + mo * 128: ki * 1024 + (mo + 1) * 128],
                                             x0h[:, ki * T + tc * 512: ki * T + (tc + 1) * 512],
                                             start=(ki == 0), stop=(ki == 7))
                        t.drain().then_inc(SEM("p2t"), 1)
                    ph.add("tensor", fk_t)

                    def fk_v(v, mo=mo, sh=sh, grp=grp, t_mm=t_mm):
                        v.wait_ge(SEM("p2t"), t_mm)
                        if int(os.environ.get("KEV", "1")) == 4:
                            v.wait_ge(qs, 4096)
                        KEV = int(os.environ.get("KEV", "1"))
                        for tc in range(4):
                            if KEV == 6:
                                ic = v.tensor_single_scalar(
                                    kfm[:, tc * 512:(tc + 1) * 512],
                                    PSQ[0][:, 0:512], 0.25, ALU.add)
                            elif KEV == 7:
                                ic = v.tensor_single_scalar(
                                    kfm[:, tc * 512:(tc + 1) * 512],
                                    PSQ[0][:, 0:512], consts[:, 1:2], ALU.add)
                            elif KEV == 5:
                                ic = v.tensor_single_scalar(
                                    fmA[:, tc * 512:(tc + 1) * 512],
                                    PSQ[0][:, 0:512], bq_sb[:, 8 + mo:9 + mo], ALU.add)
                            elif KEV == 3:
                                ic = v.tensor_single_scalar(
                                    kfm[:, tc * 512:(tc + 1) * 512],
                                    PSQ[0][:, 0:512], bq_sb[:, 8 + mo:9 + mo], ALU.add)
                            elif KEV in (1, 4):
                                ic = v.tensor_single_scalar(
                                    kfm[:, mo * S + sh * T + tc * 512: mo * S + sh * T + (tc + 1) * 512],
                                    PSQ[4 * grp + tc][:, 0:512], bq_sb[:, 8 + mo:9 + mo], ALU.add)
                            else:
                                ic = v.engine_nop()
                            ic.then_inc(SEM("p2v"), 1)
                    for tc in range(4):
                        bank_k[4 * grp + tc] = cnts["p2v"] + tc + 1
                    I("p2v", 4)
                    ph.add("vector", fk_v)
                t_k_mm = cnts["p2t"]

                if SEC2 < 3:
                    continue
                _, gate_wv = I("qg", 16 * 8)

                def flv(g, t_k_mm=t_k_mm):
                    g.wait_ge(SEM("p2t"), t_k_mm)
                    for ki in range(8):
                        g.dma_start(out=wqs[:, ki * 1024:(ki + 1) * 1024],
                                    in_=w_qkv[ki * 128:(ki + 1) * 128, 2048:3072]).then_inc(qg, 16)
                ph.add("gpsimd", flv)

                for j in range(NT):
                    kt = sh * NT + j
                    gates = [bank_v[(2 * kt + half) % 4 + 4] for half in range(2)]
                    _, t_mm = I("p2vt")

                    def fvt(t, j=j, kt=kt, sh=sh, gates=gates, gate_wv=gate_wv, t_vm=t_vm):
                        if j == 0:
                            t.wait_ge(qg, gate_wv)
                            t.wait_ge(SEM("ge"), t_vm)
                        gmax = max(gates)
                        if gmax:
                            t.wait_ge(SEM("p2vv"), gmax)
                        for half in range(2):
                            ps = PSQ[(2 * kt + half) % 4 + 4]
                            for ki in range(8):
                                t.matmul(ps[:, 0:512],
                                         x0h[:, ki * T + j * 128: ki * T + (j + 1) * 128],
                                         wqs[:, ki * 1024 + half * 512: ki * 1024 + (half + 1) * 512],
                                         start=(ki == 0), stop=False)
                            t.matmul(ps[:, 0:512], ones1b[:],
                                     bv_row[:, half * 512:(half + 1) * 512], start=False, stop=True)
                        t.drain().then_inc(SEM("p2vt"), 1)
                    ph.add("tensor", fvt)
                    st_gate = qs0_v + 16 * (nvst - 1) if nvst >= 2 else None

                    def fvv(v, kt=kt, t_mm=t_mm, st_gate=st_gate):
                        v.wait_ge(SEM("p2vt"), t_mm)
                        if st_gate is not None:
                            v.wait_ge(qs, st_gate)
                        for half in range(2):
                            ps = PSQ[(2 * kt + half) % 4 + 4]
                            base = (kt % 2) * 1040 + half * 520
                            ic = v.tensor_copy(
                                vstg.ap()[:, base:base + 520].rearrange("p (h c) -> p h c", c=65)[:, :, 0:64],
                                ps.ap()[:, 0:512].rearrange("p (h c) -> p h c", c=64))
                            ic.then_inc(SEM("p2vv"), 1)
                    for half in range(2):
                        bank_v[(2 * kt + half) % 4 + 4] = cnts["p2vv"] + half + 1
                    I("p2vv", 2)
                    ph.add("vector", fvv)
                    _, _tq = I("qs", 16)
                    nvst += 1

                    def fvs(sy, kt=kt, need=cnts["p2vv"]):
                        sy.wait_ge(SEM("p2vv"), need)
                        sy.dma_start(out=d_v[:, kt * 1040:(kt + 1) * 1040],
                                     in_=vstg[:, (kt % 2) * 1040:(kt % 2 + 1) * 1040]).then_inc(qs, 16)
                    ph.add("sync", fvs)
            ph.emit(); CKPT()
            es_x0h.close()

            es_ofm = ES()
            ofm = es_ofm.enter_context(nc.sbuf_tensor("ofm", [128, 8 * T], BF16, side="right"))
            ph = Ph()
            qs0_ofm = cnts["qs"]
            for i in range(NT):
                qs_gate = qs0_ofm + 16 * 8 * (i - 3) if i >= 4 else None
                _, tqg = I("qg", 16)

                def fl(g, i=i, qs_gate=qs_gate):
                    if qs_gate is not None:
                        g.wait_ge(qs, qs_gate)
                    g.dma_start(out=scb[:, (i % 4) * D:(i % 4 + 1) * D],
                                in_=x0_own[i * 128:(i + 1) * 128, :]).then_inc(qg, 16)
                ph.add("gpsimd", fl)
                _, _t = I("qs", 16 * 8)

                def ft(sy, i=i, gate=tqg):
                    sy.wait_ge(qg, gate)
                    for fc in range(8):
                        sy.dma_start(
                            out=ofm[:, fc * T + i * 128: fc * T + (i + 1) * 128],
                            in_=scb[:, (i % 4) * D + fc * 128:(i % 4) * D + (fc + 1) * 128],
                            transpose=True).then_inc(qs, 16)
                ph.add("sync", ft)
            t_ofm = cnts["qs"]

            _, gate_wq = I("qg", 16 * 8)

            def flq(g):
                for ki in range(8):
                    g.dma_start(out=wqs[:, ki * 1024:(ki + 1) * 1024],
                                in_=w_qkv[ki * 128:(ki + 1) * 128, 0:1024]).then_inc(qg, 16)
            ph.add("gpsimd", flq)

            bank_q = {i: 0 for i in range(8)}
            for mo in range(8):
                grp = mo % 2
                gates = [bank_q[4 * grp + tc] for tc in range(4)]
                _, t_mm = I("p2t")

                def fq_t(t, mo=mo, grp=grp, gates=gates):
                    if mo == 0:
                        t.wait_ge(qg, gate_wq)
                        t.wait_ge(qs, t_ofm)
                    gmax = max(gates)
                    if gmax:
                        t.wait_ge(SEM("p2v"), gmax)
                    for ki in range(8):
                        for tc in range(4):
                            t.matmul(PSQ[4 * grp + tc][:, 0:512],
                                     wqs[:, ki * 1024 + mo * 128: ki * 1024 + (mo + 1) * 128],
                                     ofm[:, ki * T + tc * 512: ki * T + (tc + 1) * 512],
                                     start=(ki == 0), stop=(ki == 7))
                    t.drain().then_inc(SEM("p2t"), 1)
                ph.add("tensor", fq_t)

                def fq_v(v, mo=mo, grp=grp, t_mm=t_mm):
                    v.wait_ge(SEM("p2t"), t_mm)
                    for tc in range(4):
                        ic = v.tensor_single_scalar(
                            fmA[:, mo * T + tc * 512: mo * T + (tc + 1) * 512],
                            PSQ[4 * grp + tc][:, 0:512], bq_sb[:, mo:mo + 1], ALU.add)
                        ic.then_inc(SEM("p2v"), 1)
                for tc in range(4):
                    bank_q[4 * grp + tc] = cnts["p2v"] + tc + 1
                I("p2v", 4)
                ph.add("vector", fq_v)
            ph.emit(); CKPT()
            es_ps2.close()
            es_ofm.close()
            es_p2.close()

            es_p3 = ES()
            v_sb = es_p3.enter_context(nc.sbuf_tensor("v_sb", [128, NKT * 1040], BF16))
            cmr = es_p3.enter_context(nc.sbuf_tensor("cmr", [128, 2 * 1024], BF16))
            es_ps3 = ES()
            ps_s = [es_ps3.enter_context(nc.psum_tensor("ps3s%d" % i, [128, 1024], F32))
                    for i in range(2)]
            ps_c = [es_ps3.enter_context(nc.psum_tensor("ps3c%d" % i, [65, 256], F32))
                    for i in range(2)]
            ps_zb = es_ps3.enter_context(nc.psum_tensor("ps3zb", [64, 256], F32))

            ph = Ph()
            SEM("p3t"); SEM("p3e"); SEM("p3m"); SEM("p3a"); SEM("p3z"); SEM("p3zb"); SEM("p3ev")
            expbuf_av = {0: 0, 1: 0}
            psc_ev = {0: 0, 1: 0}
            pss_exp = {0: 0, 1: 0}
            mask_used = {0: 0, 1: 0}
            stage_dma = {0: 0, 1: 0}
            scr_zb = 0
            zb_ev = 0
            vload = []
            pending_tail = None
            P3ST = {"scr_zb": 0, "zb_ev": 0}

            def mk_tail3(bi, h, av_tick):
                _, t_z = I("p3z")

                def fz_v(v, h=h, av_tick=av_tick, scr_gate=P3ST["scr_zb"]):
                    v.wait_ge(SEM("p3a"), av_tick)
                    if scr_gate:
                        v.wait_ge(SEM("p3zb"), scr_gate)
                    v.reciprocal(scr[0:1, 0:256], ps_c[h % 2][64:65, 0:256]).then_inc(SEM("p3z"), 1)
                ph.add("vector", fz_v)
                _, t_zb = I("p3zb")

                def fz_t(t, t_z=t_z, zb_gate=P3ST["zb_ev"]):
                    t.wait_ge(SEM("p3z"), t_z)
                    if zb_gate:
                        t.wait_ge(SEM("p3ev"), zb_gate)
                    t.matmul(ps_zb[:, 0:256], ones1f[0:1, 0:64], scr[0:1, 0:256],
                             start=True, stop=True)
                    t.drain().then_inc(SEM("p3zb"), 1)
                ph.add("tensor", fz_t)
                P3ST["scr_zb"] = t_zb
                _, t_ev = I("p3ev")

                def fz_e(v, h=h, t_zb=t_zb, dma_gate=stage_dma[h % 2]):
                    v.wait_ge(SEM("p3zb"), t_zb)
                    if dma_gate:
                        v.wait_ge(qs, dma_gate)
                    stg = scb[0:64, 2048 + (h % 2) * 256: 2048 + (h % 2) * 256 + 256]
                    v.tensor_copy(stg, ps_c[h % 2][0:64, 0:256])
                    v.tensor_tensor(stg, stg, ps_zb[0:64, 0:256],
                                    ALU.mult).then_inc(SEM("p3ev"), 1)
                ph.add("vector", fz_e)
                psc_ev[h % 2] = t_ev
                P3ST["zb_ev"] = t_ev
                _, tq_st = I("qs", 16)

                def fz_s(sy, bi=bi, h=h, t_ev=t_ev):
                    sy.wait_ge(SEM("p3ev"), t_ev)
                    sy.dma_start(out=d_ctx[h * 64:(h + 1) * 64, bi * 256:(bi + 1) * 256],
                                 in_=scb[0:64, 2048 + (h % 2) * 256: 2048 + (h % 2) * 256 + 256]
                                 ).then_inc(qs, 16)
                ph.add("sync", fz_s)
                stage_dma[h % 2] = tq_st

            for bi in range(NBLK):
                _, tvl = I("qg", 16 * 5)
                mu_gate = mask_used[bi % 2]

                def fld(g, bi=bi, mu_gate=mu_gate):
                    if mu_gate:
                        g.wait_ge(SEM("p3m"), mu_gate)
                    for j in range(4):
                        kt = 4 * bi + j
                        g.dma_start(out=v_sb[:, kt * 1040:(kt + 1) * 1040],
                                    in_=d_v[:, kt * 1040:(kt + 1) * 1040]).then_inc(qg, 16)
                    g.dma_start(
                        out=cmr.ap()[:, (bi % 2) * 1024:(bi % 2 + 1) * 1024]
                            .rearrange("p (m q) -> p m q", q=256),
                        in_=cmask[4 * bi:4 * bi + 4, :].rearrange("m (p q) -> p m q", p=128)
                    ).then_inc(qg, 16)
                ph.add("gpsimd", fld)
                vload.append(tvl)

                for h in range(H):
                    ch, ro = h >> 1, 64 * (h & 1)
                    nsc = bi + 1
                    av_tick = 0
                    av_jobs = []

                    def mk_av3(args):
                        (sc, pp, nsc, av_need, gate_psc, bi_, h_) = args
                        _, t_av = I("p3a")

                        def fsc_a(t, bi=bi_, h=h_, sc=sc, pp=pp, nsc=nsc,
                                  av_need=av_need, gate_psc=gate_psc):
                            t.wait_ge(SEM(av_need[0]), av_need[1])
                            if gate_psc:
                                t.wait_ge(SEM("p3ev"), gate_psc)
                            for j in range(4):
                                kt = 4 * sc + j
                                t.matmul(ps_c[h % 2][:, 0:256],
                                         v_sb[:, kt * 1040 + h * 65: kt * 1040 + h * 65 + 65],
                                         scb[:, pp * 1024 + j * 256: pp * 1024 + (j + 1) * 256],
                                         start=(sc == 0 and j == 0),
                                         stop=(sc == nsc - 1 and j == 3))
                            t.drain().then_inc(SEM("p3a"), 1)
                        ph.add("tensor", fsc_a)
                        expbuf_av[pp] = t_av
                        return t_av

                    for sc in range(nsc):
                        pp = sc % 2
                        gate_exp_reuse = pss_exp[pp]
                        _, t_mm = I("p3t")

                        def fsc_t(t, bi=bi, h=h, ch=ch, ro=ro, sc=sc, pp=pp,
                                  gate=gate_exp_reuse, tvl=tvl):
                            if sc == 0 and h == 0:
                                t.wait_ge(qg, tvl)
                            if gate:
                                t.wait_ge(SEM("p3e"), gate)
                            for j in range(4):
                                kt = 4 * sc + j
                                t.matmul(ps_s[pp][:, j * 256:(j + 1) * 256],
                                         kfm[ro:ro + 64, ch * S + kt * 128: ch * S + (kt + 1) * 128],
                                         fmA[ro:ro + 64, ch * T + bi * 256: ch * T + (bi + 1) * 256],
                                         start=True, stop=True)
                            t.drain().then_inc(SEM("p3t"), 1)
                        ph.add("tensor", fsc_t)

                        gate_av_reuse = expbuf_av[pp]
                        _, t_exp = I("p3e")

                        def fsc_s(s_, pp=pp, t_mm=t_mm, gate=gate_av_reuse):
                            s_.wait_ge(SEM("p3t"), t_mm)
                            if gate:
                                s_.wait_ge(SEM("p3a"), gate)
                            s_.activation(scb[:, pp * 1024:(pp + 1) * 1024], ps_s[pp][:, 0:1024],
                                          AF.Exp, bias=consts[:, 0:1],
                                          scale=0.125).then_inc(SEM("p3e"), 1)
                        ph.add("scalar", fsc_s)
                        pss_exp[pp] = t_exp

                        last = (sc == nsc - 1)
                        if last:
                            _, t_msk = I("p3m")

                            def fsc_m(v, bi=bi, pp=pp, t_exp=t_exp, tvl=tvl):
                                v.wait_ge(SEM("p3e"), t_exp)
                                v.wait_ge(qg, tvl)
                                v.tensor_tensor(scb[:, pp * 1024:(pp + 1) * 1024],
                                                scb[:, pp * 1024:(pp + 1) * 1024],
                                                cmr[:, (bi % 2) * 1024:(bi % 2 + 1) * 1024],
                                                ALU.mult).then_inc(SEM("p3m"), 1)
                            ph.add("vector", fsc_m)
                            mask_used[bi % 2] = t_msk
                            av_need = ("p3m", t_msk)
                        else:
                            av_need = ("p3e", t_exp)

                        gate_psc = psc_ev[h % 2] if sc == 0 else 0
                        av_jobs.append((sc, pp, nsc, av_need, gate_psc, bi, h))
                        if len(av_jobs) >= 2:
                            av_tick = mk_av3(av_jobs.pop(0))
                    while av_jobs:
                        av_tick = mk_av3(av_jobs.pop(0))

                    if pending_tail is not None:
                        mk_tail3(*pending_tail)
                    pending_tail = (bi, h, av_tick)
            if pending_tail is not None:
                mk_tail3(*pending_tail)
            ph.emit(); CKPT()
            es_ps3.close()
            es_p3.close()
            es_kfm.close()

            es_xf = ES()
            xf = es_xf.enter_context(nc.sbuf_tensor("xf", [128, NT * D], F32))
            es_p4 = ES()
            wo_sb = es_p4.enter_context(nc.sbuf_tensor("wo_sb", [128, 8 * 1024], BF16))
            ctxl = es_p4.enter_context(nc.sbuf_tensor("ctxl", [128, 2 * 1024], BF16))
            es_ps4 = ES()
            ps4 = [es_ps4.enter_context(nc.psum_tensor("ps4_%d" % i, [128, 1024], F32))
                   for i in range(2)]

            def phase_proj_ln(wsrc, brow, dctx_src, resid_loader, spill_dst, tag,
                              wo_sb, ctxl, ps4, fm_dst, xf_t, do_tr=True, extra_v_fn=None):
                ph = Ph()
                _, gate_w = I("qg", 16 * 8)

                def flw(g):
                    for ki in range(8):
                        g.dma_start(out=wo_sb[:, ki * 1024:(ki + 1) * 1024],
                                    in_=wsrc[ki * 128:(ki + 1) * 128, :]).then_inc(qg, 16)
                ph.add("gpsimd", flw)
                SEM(tag + "t")
                ln_a = {0: 0, 1: 0}
                ln_c = {0: 0, 1: 0}
                tr_done = {0: 0, 1: 0}
                resid_rd = 0
                for ti in range(NT):
                    pp = ti % 2
                    gate_ctx = cnts[tag + "t"] - 1 if ti >= 2 else 0
                    _, t_ld = I("qg", 32)

                    def fld(g, ti=ti, pp=pp, gate_ctx=gate_ctx, rr=resid_rd):
                        if gate_ctx:
                            g.wait_ge(SEM(tag + "t"), gate_ctx)
                        g.dma_start(
                            out=ctxl.ap()[:, pp * 1024:(pp + 1) * 1024].rearrange("p (k c) -> p k c", c=128),
                            in_=dctx_src.rearrange("(k p) t -> p k t", p=128)[:, :, ti * 128:(ti + 1) * 128]
                        ).then_inc(qg, 16)
                        if rr:
                            g.wait_ge(SEM(tag + "a"), rr)
                        resid_loader(g, ti, scr[:, 1024:2048])
                    ph.add("gpsimd", fld)

                    gate_psum = ln_a[pp]
                    _, t_mm = I(tag + "t")

                    def fmm(t, ti=ti, pp=pp, t_ld=t_ld, gate_w=gate_w, gate_psum=gate_psum):
                        t.wait_ge(qg, t_ld)
                        if ti == 0:
                            t.wait_ge(qg, gate_w)
                        if gate_psum:
                            t.wait_ge(SEM(tag + "a"), gate_psum)
                        for ki in range(8):
                            for nh in range(2):
                                t.matmul(ps4[pp][:, nh * 512:(nh + 1) * 512],
                                         ctxl[:, pp * 1024 + ki * 128: pp * 1024 + (ki + 1) * 128],
                                         wo_sb[:, ki * 1024 + nh * 512: ki * 1024 + (nh + 1) * 512],
                                         start=(ki == 0), stop=False)
                        for nh in range(2):
                            t.matmul(ps4[pp][:, nh * 512:(nh + 1) * 512], ones1b[:],
                                     brow[:, nh * 512:(nh + 1) * 512],
                                     start=False, stop=True)
                        t.drain().then_inc(SEM(tag + "t"), 1)
                    ph.add("tensor", fmm)

                    obf = scb.ap()[:, 2048 + pp * 1024: 2048 + (pp + 1) * 1024]
                    gate_tr = tr_done[pp]
                    sc_, tc = sched_ln(
                        ph,
                        (lambda pp=pp: ps4[pp][:, 0:1024]),
                        (lambda: scr[:, 1024:2048]),
                        xf_t[:, ti * D:(ti + 1) * D],
                        obf,
                        spill_dst[:, ti * D:(ti + 1) * D] if spill_dst is not None else None,
                        tag,
                        extra_wait=(lambda t_mm=t_mm: (SEM(tag + "t"), t_mm)),
                        outbf_wait=(qs, gate_tr))
                    ln_a[pp] = cnts[tag + "a"]
                    ln_c[pp] = tc
                    resid_rd = cnts[tag + "a"]
                    t_tr = I("qs", 16 * 8)[1] if do_tr else 0

                    def ftr(sy, ti=ti, pp=pp, tc=tc, obf=obf):
                        sy.wait_ge(SEM(tag + "c"), tc)
                        for fc in range(8):
                            sy.dma_start(
                                out=fm_dst[:, fc * T + ti * 128: fc * T + (ti + 1) * 128],
                                in_=obf[:, fc * 128:(fc + 1) * 128],
                                transpose=True).then_inc(qs, 16)
                    if do_tr:
                        ph.add("sync", ftr)
                        tr_done[pp] = t_tr
                    if extra_v_fn is not None:
                        extra_v_fn(ph, ti, tc)
                return ph

            def resid_x0(g, ti, dst):
                g.dma_start(out=dst, in_=x0_own[ti * 128:(ti + 1) * 128, :]).then_inc(qg, 16)

            ph = phase_proj_ln(w_out, bo_row, d_ctx, resid_x0, d_x1, "p4",
                               wo_sb, ctxl, ps4, fmA, xf)
            ph.emit(); CKPT()
            es_ps4.close()
            es_p4.close()

            es_p5 = ES()
            wcn1 = es_p5.enter_context(nc.sbuf_tensor("wcn1", [128, 8 * 1024], BF16))
            wcn2 = es_p5.enter_context(nc.sbuf_tensor("wcn2", [128, 8 * 512], BF16))
            wcn3 = es_p5.enter_context(nc.sbuf_tensor("wcn3", [128, 4 * 256], BF16))
            wc4b = es_p5.enter_context(nc.sbuf_tensor("wc4b", [128, 2], BF16))
            wgc1 = es_p5.enter_context(nc.sbuf_tensor("wgc1", [128, 8 * 512], BF16))
            wgc2 = es_p5.enter_context(nc.sbuf_tensor("wgc2", [128, 4 * 256], BF16))
            wgc3 = es_p5.enter_context(nc.sbuf_tensor("wgc3", [128, 2], BF16))
            gcbf = es_p5.enter_context(nc.sbuf_tensor("gcbf", [128, 8], BF16))
            h1b = es_p5.enter_context(nc.sbuf_tensor("h1b", [128, 8 * 512], BF16))
            h2b = es_p5.enter_context(nc.sbuf_tensor("h2b", [128, 4 * 512], BF16))
            h3b = es_p5.enter_context(nc.sbuf_tensor("h3b", [128, 2 * 512], BF16))
            es_ps5 = ES()
            ps5 = [es_ps5.enter_context(nc.psum_tensor("ps5_%d" % i, [128, 512], F32))
                   for i in range(4)]
            pt4 = es_ps5.enter_context(nc.psum_tensor("pt4", [128, 16], F32))
            psg = es_ps5.enter_context(nc.psum_tensor("psg", [128, 8], F32))
            ps1x = es_ps5.enter_context(nc.psum_tensor("ps1x", [1, 8], F32))

            ph = Ph()
            _, t_gcr = I("p5gc")

            def fgc_v(v):
                for ki in range(8):
                    v.reduce_sum(gsc[:, ki:ki + 1], fmA[:, ki * T:(ki + 1) * T], AX.X)
                v.drain().then_inc(SEM("p5gc"), 1)
            ph.add("vector", fgc_v)
            _, t_gci = I("qg", 16)

            def fgc_g(g, t_gcr=t_gcr):
                g.wait_ge(SEM("p5gc"), t_gcr)
                g.dma_start(out=d_gci, in_=gsc[:, 0:8]).then_inc(qg, 16)
            ph.add("gpsimd", fgc_g)
            _, t_cc = I("ccg")

            def fgc_c(g, t_gci=t_gci):
                g.wait_ge(qg, t_gci)
                g.collective_compute("AllReduce", ALU.add, replica_groups=pairs,
                                     ins=[d_gci], outs=[d_gco]).then_inc(SEM("ccg"), 1)
            ph.add("gpsimd", fgc_c)
            _, t_gco = I("qg", 16)

            def fgc_g2(g, t_cc=t_cc):
                g.wait_ge(SEM("ccg"), t_cc)
                g.dma_start(out=gsc[:, 8:16], in_=d_gco).then_inc(qg, 16)
            ph.add("gpsimd", fgc_g2)

            _, t_wcn = I("qg", 16 * (8 + 8 + 4 + 1 + 8 + 4 + 1))

            def fwcn(g):
                ncd = nc.allow_non_contiguous_dma(reason="tiny cn_w4/gc_w3 loads")
                ncd.__enter__()
                for ki in range(8):
                    g.dma_start(out=wcn1[:, ki * 1024:(ki + 1) * 1024],
                                in_=cn_w1[ki * 128:(ki + 1) * 128, :]).then_inc(qg, 16)
                for ki in range(8):
                    g.dma_start(out=wcn2[:, ki * 512:(ki + 1) * 512],
                                in_=cn_w2[ki * 128:(ki + 1) * 128, :]).then_inc(qg, 16)
                for ki in range(4):
                    g.dma_start(out=wcn3[:, ki * 256:(ki + 1) * 256],
                                in_=cn_w3[ki * 128:(ki + 1) * 128, :]).then_inc(qg, 16)
                g.dma_start(out=wc4b[:], in_=cn_w4.rearrange("(c p) o -> p (c o)", p=128)).then_inc(qg, 16)
                for ki in range(8):
                    g.dma_start(out=wgc1[:, ki * 512:(ki + 1) * 512],
                                in_=gc_w1[ki * 128:(ki + 1) * 128, :]).then_inc(qg, 16)
                for ki in range(4):
                    g.dma_start(out=wgc2[:, ki * 256:(ki + 1) * 256],
                                in_=gc_w2[ki * 128:(ki + 1) * 128, :]).then_inc(qg, 16)
                g.dma_start(out=wgc3[:], in_=gc_w3.rearrange("(c p) o -> p (c o)", p=128)).then_inc(qg, 16)
                ncd.__exit__(None, None, None)
            ph.add("gpsimd", fwcn)

            _, t_gc2 = I("p5gf")

            def fgf_v(v, t_gco=t_gco):
                v.wait_ge(qg, t_gco)
                v.tensor_scalar_mul(gsc[:, 16:24], gsc[:, 8:16], 1.0 / 4096.0)
                v.tensor_scalar_mul(gcbf[:], gsc[:, 8:16], 1.0 / 4096.0)
                v.drain().then_inc(SEM("p5gf"), 1)
            ph.add("vector", fgf_v)
            _, t_gf1 = I("p5t")

            def fgf_t(t, t_gc2=t_gc2, t_wcn=t_wcn):
                t.wait_ge(SEM("p5gf"), t_gc2)
                t.wait_ge(qg, t_wcn)
                for mo in range(4):
                    for ki in range(8):
                        t.matmul(psg[:, mo:mo + 1], wgc1[:, ki * 512 + mo * 128: ki * 512 + (mo + 1) * 128],
                                 gcbf[:, ki:ki + 1], start=(ki == 0), stop=(ki == 7))
                t.drain().then_inc(SEM("p5t"), 1)
            ph.add("tensor", fgf_t)
            _, t_gf1s = I("p5s")

            def fgf_s(s_, t_gf1=t_gf1):
                s_.wait_ge(SEM("p5t"), t_gf1)
                for mo in range(4):
                    ic = s_.activation(scb[:, mo:mo + 1], psg[:, mo:mo + 1], AF.Gelu,
                                       bias=gcb1[:, mo:mo + 1], scale=1.0)
                ic.then_inc(SEM("p5s"), 1)
            ph.add("scalar", fgf_s)
            _, t_gf2 = I("p5t")

            def fgf_t2(t, t_gf1s=t_gf1s):
                t.wait_ge(SEM("p5s"), t_gf1s)
                for mo in range(2):
                    for ki in range(4):
                        t.matmul(psg[:, 4 + mo:5 + mo], wgc2[:, ki * 256 + mo * 128: ki * 256 + (mo + 1) * 128],
                                 scb[:, ki:ki + 1], start=(ki == 0), stop=(ki == 3))
                t.drain().then_inc(SEM("p5t"), 1)
            ph.add("tensor", fgf_t2)
            _, t_gf2s = I("p5s")

            def fgf_s2(s_, t_gf2=t_gf2):
                s_.wait_ge(SEM("p5t"), t_gf2)
                for mo in range(2):
                    ic = s_.activation(scb[:, 4 + mo:5 + mo], psg[:, 4 + mo:5 + mo], AF.Gelu,
                                       bias=gcb2[:, mo:mo + 1], scale=1.0)
                ic.then_inc(SEM("p5s"), 1)
            ph.add("scalar", fgf_s2)
            _, t_gf3 = I("p5t")

            def fgf_t3(t, t_gf2s=t_gf2s):
                t.wait_ge(SEM("p5s"), t_gf2s)
                for ki in range(2):
                    t.matmul(ps1x[0:1, 0:1], wgc3[:, ki:ki + 1], scb[:, 4 + ki:5 + ki],
                             start=(ki == 0), stop=(ki == 1))
                t.drain().then_inc(SEM("p5t"), 1)
            ph.add("tensor", fgf_t3)
            _, t_gf3s = I("p5s")

            def fgf_s3(s_, t_gf3=t_gf3):
                s_.wait_ge(SEM("p5t"), t_gf3)
                s_.activation(scr[0:1, 0:1], ps1x[0:1, 0:1], AF.Sigmoid,
                              bias=gb3r[0:1, 0:1], scale=1.0).then_inc(SEM("p5s"), 1)
            ph.add("scalar", fgf_s3)
            _, t_gfb = I("p5t")

            def fgf_t4(t, t_gf3s=t_gf3s):
                t.wait_ge(SEM("p5s"), t_gf3s)
                t.matmul(psg[:, 6:7], ones1f[0:1, :], scr[0:1, 0:1],
                         start=True, stop=True)
                t.drain().then_inc(SEM("p5t"), 1)
            ph.add("tensor", fgf_t4)
            _, t_gfv = I("p5v")

            def fgf_v2(v, t_gfb=t_gfb):
                v.wait_ge(SEM("p5t"), t_gfb)
                v.tensor_scalar_mul(gfr[:], psg[:, 6:7], 1.0)
                v.tensor_scalar_mul(a6r[:], arr[:], 0.6)
                v.tensor_scalar_mul(gsc[:, 30:31], arr[:], -0.6)
                v.tensor_tensor(bvr[:], arr[:], gfr[:], ALU.mult)
                v.tensor_scalar_mul(gsc[:, 24:25], consts[:, 4:5], 1.0)
                v.tensor_scalar_mul(gsc[:, 25:26], consts[:, 4:5], 1.0)
                v.tensor_scalar_mul(bvr[:], bvr[:], 1.4)
                v.tensor_scalar_mul(gsc[:, 24:25], consts[:, 4:5], 1.0)
                v.tensor_scalar_mul(gsc[:, 25:26], consts[:, 4:5], 1.0)
                v.scalar_tensor_tensor(bvr[:], arr[:], -1.0, bvr[:], ALU.mult, ALU.add)
                v.tensor_scalar_mul(gsc[:, 24:25], consts[:, 4:5], 1.0)
                v.tensor_scalar_mul(gsc[:, 25:26], consts[:, 4:5], 1.0)
                v.tensor_scalar_add(bvr[:], bvr[:], 0.1)
                v.tensor_scalar_mul(gsc[:, 24:25], consts[:, 4:5], 1.0)
                v.tensor_scalar_mul(gsc[:, 25:26], consts[:, 4:5], 1.0)
                v.tensor_scalar_mul(gsc[:, 31:32], bvr[:], -1.0)
                v.drain().then_inc(SEM("p5v"), 1)
            ph.add("vector", fgf_v2)

            bank_ev = {0: 0, 1: 0, 2: 0, 3: 0}
            refresh_tr = {0: 0, 1: 0}
            refresh_q = 0
            dbg_prev_dump = [0]
            for cyc in range(3):
                for tc in range(4):
                    for mo in range(8):
                        gate_ps = bank_ev[mo % 4]
                        _, t_mm = I("p5t")

                        def f1t(t, mo=mo, tc=tc, cyc=cyc, gate_ps=gate_ps,
                                t_wcn=t_wcn, rq=refresh_q):
                            if cyc == 0 and tc == 0 and mo == 0:
                                t.wait_ge(qg, t_wcn)
                            if rq and tc == 0 and mo == 0:
                                t.wait_ge(qs, rq)
                            if gate_ps:
                                t.wait_ge(SEM("p5s"), gate_ps)
                            for ki in range(8):
                                t.matmul(ps5[mo % 4][:, 0:512],
                                         wcn1[:, ki * 1024 + mo * 128: ki * 1024 + (mo + 1) * 128],
                                         fmA[:, ki * T + tc * 512: ki * T + (tc + 1) * 512],
                                         start=(ki == 0), stop=(ki == 7))
                            t.drain().then_inc(SEM("p5t"), 1)
                        ph.add("tensor", f1t)
                        _, t_ev = I("p5s")

                        def f1s(s_, mo=mo, t_mm=t_mm):
                            s_.wait_ge(SEM("p5t"), t_mm)
                            s_.activation(h1b[:, mo * 512:(mo + 1) * 512], ps5[mo % 4][:, 0:512],
                                          AF.Gelu, bias=cnb1[:, mo:mo + 1],
                                          scale=1.0).then_inc(SEM("p5s"), 1)
                        ph.add("scalar", f1s)
                        bank_ev[mo % 4] = cnts["p5s"]
                    h1done = cnts["p5s"]
                    for mo in range(4):
                        gate_ps = bank_ev[mo % 4]
                        _, t_mm = I("p5t")

                        def f2t(t, mo=mo, h1done=h1done, gate_ps=gate_ps):
                            t.wait_ge(SEM("p5s"), max(h1done, gate_ps))
                            for ki in range(8):
                                t.matmul(ps5[mo % 4][:, 0:512],
                                         wcn2[:, ki * 512 + mo * 128: ki * 512 + (mo + 1) * 128],
                                         h1b[:, ki * 512:(ki + 1) * 512],
                                         start=(ki == 0), stop=(ki == 7))
                            t.drain().then_inc(SEM("p5t"), 1)
                        ph.add("tensor", f2t)
                        _, t_ev = I("p5s")

                        def f2s(s_, mo=mo, t_mm=t_mm):
                            s_.wait_ge(SEM("p5t"), t_mm)
                            s_.activation(h2b[:, mo * 512:(mo + 1) * 512], ps5[mo % 4][:, 0:512],
                                          AF.Gelu, bias=cnb2[:, mo:mo + 1],
                                          scale=1.0).then_inc(SEM("p5s"), 1)
                        ph.add("scalar", f2s)
                        bank_ev[mo % 4] = cnts["p5s"]
                    h2done = cnts["p5s"]
                    for mo in range(2):
                        gate_ps = bank_ev[mo]
                        _, t_mm = I("p5t")

                        def f3t(t, mo=mo, h2done=h2done, gate_ps=gate_ps):
                            t.wait_ge(SEM("p5s"), max(h2done, gate_ps))
                            for ki in range(4):
                                t.matmul(ps5[mo][:, 0:512],
                                         wcn3[:, ki * 256 + mo * 128: ki * 256 + (mo + 1) * 128],
                                         h2b[:, ki * 512:(ki + 1) * 512],
                                         start=(ki == 0), stop=(ki == 3))
                            t.drain().then_inc(SEM("p5t"), 1)
                        ph.add("tensor", f3t)
                        _, t_ev = I("p5s")

                        def f3s(s_, mo=mo, t_mm=t_mm):
                            s_.wait_ge(SEM("p5t"), t_mm)
                            s_.activation(h3b[:, mo * 512:(mo + 1) * 512], ps5[mo][:, 0:512],
                                          AF.Gelu, bias=cnb3[:, mo:mo + 1],
                                          scale=1.0).then_inc(SEM("p5s"), 1)
                        ph.add("scalar", f3s)
                        bank_ev[mo] = cnts["p5s"]
                    h3done = cnts["p5s"]
                    _, t_mm = I("p5t")

                    def f4t(t, tc=tc, h3done=h3done, cyc=cyc, t_gfv=t_gfv):
                        t.wait_ge(SEM("p5s"), h3done)
                        if cyc == 0 and tc == 0:
                            t.wait_ge(SEM("p5v"), t_gfv)
                        for tj in range(4):
                            ti = tc * 4 + tj
                            for kc in range(2):
                                t.matmul(pt4[:, ti:ti + 1],
                                         h3b[:, kc * 512 + tj * 128: kc * 512 + (tj + 1) * 128],
                                         wc4b[:, kc:kc + 1],
                                         start=(kc == 0), stop=(kc == 1))
                        t.drain().then_inc(SEM("p5t"), 1)
                    ph.add("tensor", f4t)
                t4done = cnts["p5t"]
                _, t_sg = I("p5s")

                def fsg(s_, t4done=t4done, dwait=(dbg_prev_dump[0] if DBG else 0)):
                    s_.wait_ge(SEM("p5t"), t4done)
                    if dwait:
                        s_.wait_ge(qg, dwait)
                    s_.activation(tfc[:], pt4[:], AF.Sigmoid, bias=b4r[:, 0:1],
                                  scale=1.0).then_inc(SEM("p5s"), 1)
                ph.add("scalar", fsg)
                _, t_cp = I("p5v")

                def fcp(v, t_sg=t_sg, t_gfv=t_gfv):
                    v.wait_ge(SEM("p5s"), t_sg)
                    v.wait_ge(SEM("p5v"), t_gfv)
                    v.tensor_scalar(cplb[:], tfc[:], a6r[:, 0:1], bvr[:, 0:1], ALU.mult, ALU.add)
                    v.tensor_scalar(ncplb[:], tfc[:], gsc[:, 30:31], gsc[:, 31:32],
                                    ALU.mult, ALU.add).then_inc(SEM("p5v"), 1)
                ph.add("vector", fcp)
                if DBG and cyc == 0:
                    _, t_d5 = I("qg", 16 * 15)

                    def fd5(g, t_cp=t_cp):
                        ncd = nc.allow_non_contiguous_dma(reason="debug dump")
                        ncd.__enter__()
                        g.wait_ge(SEM("p5v"), t_cp)
                        g.dma_start(out=o_p5[:, 0:1], in_=gfr[:]).then_inc(qg, 16)
                        g.dma_start(out=o_p5[:, 1:2], in_=a6r[:]).then_inc(qg, 16)
                        g.dma_start(out=o_p5[:, 2:3], in_=bvr[:]).then_inc(qg, 16)
                        g.dma_start(out=o_p5[:, 16:32], in_=tfc[:]).then_inc(qg, 16)
                        g.dma_start(out=o_p5[:, 32:48], in_=cplb[:]).then_inc(qg, 16)
                        g.dma_start(out=o_p5[:, 3:11], in_=gsc[:, 8:16]).then_inc(qg, 16)
                        g.dma_start(out=o_p5[:, 48:56], in_=gsc[:, 16:24]).then_inc(qg, 16)
                        g.dma_start(out=o_p5b[:, 0:8], in_=scb[:, 0:8]).then_inc(qg, 16)
                        g.dma_start(out=o_p5[:, 11:12], in_=gfr[:]).then_inc(qg, 16)
                        g.dma_start(out=o_p5[:, 56:64], in_=gsc[:, 24:32]).then_inc(qg, 16)
                        g.dma_start(out=o_p5c[:, 0:8], in_=wgc1[:, 0:8]).then_inc(qg, 16)
                        g.dma_start(out=o_p5c[:, 8:16], in_=wgc1[:, 512:520]).then_inc(qg, 16)
                        g.dma_start(out=o_p5c[:, 16:24], in_=gcbf[:, 0:8]).then_inc(qg, 16)
                        g.dma_start(out=o_p5[:, 12:16], in_=scr[:, 512:516]).then_inc(qg, 16)
                        g.dma_start(out=o_p5[:, 60:64], in_=scr[:, 528:532]).then_inc(qg, 16)
                        ncd.__exit__(None, None, None)
                    ph.add("gpsimd", fd5)
                    dbg_prev_dump[0] = t_d5

                for ti in range(NT):
                    _, t_tw = I("p5v")

                    def ftw(v, ti=ti, t_cp=t_cp):
                        v.wait_ge(SEM("p5v"), t_cp)
                        q0 = xf[:, ti * D + 0:ti * D + 256]
                        q1 = xf[:, ti * D + 256:ti * D + 512]
                        q2 = xf[:, ti * D + 512:ti * D + 768]
                        q3 = xf[:, ti * D + 768:ti * D + 1024]
                        cp = cplb[:, ti:ti + 1]
                        ncp = ncplb[:, ti:ti + 1]
                        v.tensor_scalar_mul(scr[:, 0:256], q0, 1.0)
                        v.scalar_tensor_tensor(q0, q2, cp, q0, ALU.mult, ALU.add)
                        v.scalar_tensor_tensor(q2, scr[:, 0:256], ncp, q2, ALU.mult, ALU.add)
                        v.tensor_scalar_mul(scr[:, 0:256], q1, 1.0)
                        v.scalar_tensor_tensor(q1, q3, ncp, q1, ALU.mult, ALU.add)
                        ic = v.scalar_tensor_tensor(q3, scr[:, 0:256], cp, q3, ALU.mult, ALU.add)
                        ic.then_inc(SEM("p5v"), 1)
                    ph.add("vector", ftw)
                    if cyc < 2:
                        h1_read_done = cnts["p5t"]
                        _, t_cast = I("p5v")

                        def fcst(v, ti=ti, t_tw=t_tw, trg=refresh_tr[ti % 2]):
                            if trg:
                                v.wait_ge(qs, trg)
                            v.tensor_copy(scb[:, 2048 + (ti % 2) * 1024: 2048 + (ti % 2 + 1) * 1024],
                                          xf[:, ti * D:(ti + 1) * D]).then_inc(SEM("p5v"), 1)
                        ph.add("vector", fcst)
                        _, t_tr = I("qs", 16 * 8)

                        def ftr5(sy, ti=ti, t_cast=t_cast, h1_read_done=h1_read_done):
                            sy.wait_ge(SEM("p5v"), t_cast)
                            sy.wait_ge(SEM("p5t"), h1_read_done)
                            for fc in range(8):
                                sy.dma_start(
                                    out=fmA[:, fc * T + ti * 128: fc * T + (ti + 1) * 128],
                                    in_=scb[:, 2048 + (ti % 2) * 1024 + fc * 128:
                                            2048 + (ti % 2) * 1024 + (fc + 1) * 128],
                                    transpose=True).then_inc(qs, 16)
                        ph.add("sync", ftr5)
                        refresh_tr[ti % 2] = cnts["qs"]
                        refresh_q = cnts["qs"]
            ph.emit(); CKPT()

            ph = Ph()
            ln_a = 0
            tr_done2 = {0: 0, 1: 0}
            for ti in range(NT):
                pp = ti % 2
                _, t_ld = I("qg", 16)

                def fldx1(g, ti=ti, rr=ln_a):
                    if rr:
                        g.wait_ge(SEM("p5a"), rr)
                    g.dma_start(out=scr[:, 1024:2048],
                                in_=d_x1[:, ti * D:(ti + 1) * D]).then_inc(qg, 16)
                ph.add("gpsimd", fldx1)
                obf = scb.ap()[:, 2048 + pp * 1024: 2048 + (pp + 1) * 1024]
                sc_, tc = sched_ln(
                    ph,
                    (lambda ti=ti: xf[:, ti * D:(ti + 1) * D]),
                    (lambda: scr[:, 1024:2048]),
                    xf[:, ti * D:(ti + 1) * D],
                    obf,
                    d_x2[:, ti * D:(ti + 1) * D],
                    "p5",
                    extra_wait=(lambda t_ld=t_ld: (qg, t_ld)),
                    outbf_wait=(qs, tr_done2[pp]))
                ln_a = cnts["p5a"]
                _, t_tr = I("qs", 16 * 8)

                def ftr2(sy, ti=ti, pp=pp, tc=tc, obf=obf):
                    sy.wait_ge(SEM("p5c"), tc)
                    for fc in range(8):
                        sy.dma_start(
                            out=fmA[:, fc * T + ti * 128: fc * T + (ti + 1) * 128],
                            in_=obf[:, fc * 128:(fc + 1) * 128],
                            transpose=True).then_inc(qs, 16)
                ph.add("sync", ftr2)
                tr_done2[pp] = t_tr
            ph.emit(); CKPT()
            es_ps5.close()
            es_p5.close()

            es_qfm = ES()
            qfm = es_qfm.enter_context(nc.sbuf_tensor("qfm", [128, 8 * T], BF16, side="right"))
            es_p6a = ES()
            wL1 = es_p6a.enter_context(nc.sbuf_tensor("wL1", [128, 8 * 1024], BF16))
            vst6 = es_p6a.enter_context(nc.sbuf_tensor("vst6", [128, 2 * 1040], BF16))
            kst6 = es_p6a.enter_context(nc.sbuf_tensor("kst6", [128, 2 * 2048], BF16))
            es_ps6a = ES()
            PS6 = [es_ps6a.enter_context(nc.psum_tensor("ps6a%d" % i, [128, 512], F32))
                   for i in range(8)]

            ph = Ph()
            SEM("p6t"); SEM("p6v"); SEM("p6q")

            def wload6(src, gate_need):
                def fl(g, src=src, gate_need=gate_need):
                    if gate_need:
                        g.wait_ge(SEM("p6t"), gate_need)
                    for ki in range(8):
                        g.dma_start(out=wL1[:, ki * 1024:(ki + 1) * 1024],
                                    in_=src[ki * 128:(ki + 1) * 128, :]).then_inc(qg, 16)
                ph.add("gpsimd", fl)
                return I("qg", 16 * 8)[1]

            gate_wq6 = wload6(Wq, 0)
            bank_ev6 = {i: 0 for i in range(8)}
            for mo in range(8):
                grp = mo % 2
                gates = [bank_ev6[4 * grp + tc] for tc in range(4)]
                _, t_mm = I("p6t")

                def fq6(t, mo=mo, grp=grp, gates=gates, gate_w=gate_wq6):
                    if mo == 0:
                        t.wait_ge(qg, gate_w)
                    gmax = max(gates)
                    if gmax:
                        t.wait_ge(SEM("p6v"), gmax)
                    for ki in range(8):
                        for tc in range(4):
                            t.matmul(PS6[4 * grp + tc][:, 0:512],
                                     wL1[:, ki * 1024 + mo * 128: ki * 1024 + (mo + 1) * 128],
                                     fmA[:, ki * T + tc * 512: ki * T + (tc + 1) * 512],
                                     start=(ki == 0), stop=(ki == 7))
                    t.drain().then_inc(SEM("p6t"), 1)
                ph.add("tensor", fq6)

                def fq6v(v, mo=mo, grp=grp, t_mm=t_mm):
                    v.wait_ge(SEM("p6t"), t_mm)
                    for tc in range(4):
                        ic = v.tensor_copy(qfm[:, mo * T + tc * 512: mo * T + (tc + 1) * 512],
                                           PS6[4 * grp + tc][:, 0:512])
                        ic.then_inc(SEM("p6v"), 1)
                for tc in range(4):
                    bank_ev6[4 * grp + tc] = cnts["p6v"] + tc + 1
                I("p6v", 4)
                ph.add("vector", fq6v)
            t_q6t = cnts["p6t"]

            gate_wk6 = wload6(Wk, t_q6t)
            kst_dma = {0: 0, 1: 0}
            for mo in range(8):
                grp = mo % 2
                gates = [bank_ev6[4 * grp + tc] for tc in range(4)]
                _, t_mm = I("p6t")

                def fk6(t, mo=mo, grp=grp, gates=gates, gate_w=gate_wk6):
                    if mo == 0:
                        t.wait_ge(qg, gate_w)
                    gmax = max(gates)
                    if gmax:
                        t.wait_ge(SEM("p6v"), gmax)
                    for ki in range(8):
                        for tc in range(4):
                            t.matmul(PS6[4 * grp + tc][:, 0:512],
                                     wL1[:, ki * 1024 + mo * 128: ki * 1024 + (mo + 1) * 128],
                                     fmA[:, ki * T + tc * 512: ki * T + (tc + 1) * 512],
                                     start=(ki == 0), stop=(ki == 7))
                    t.drain().then_inc(SEM("p6t"), 1)
                ph.add("tensor", fk6)

                def fk6v(v, mo=mo, grp=grp, t_mm=t_mm, dgate=kst_dma[mo % 2]):
                    v.wait_ge(SEM("p6t"), t_mm)
                    if dgate:
                        v.wait_ge(qs, dgate)
                    for tc in range(4):
                        ic = v.tensor_copy(kst6[:, (mo % 2) * 2048 + tc * 512: (mo % 2) * 2048 + (tc + 1) * 512],
                                           PS6[4 * grp + tc][:, 0:512])
                        ic.then_inc(SEM("p6v"), 1)
                for tc in range(4):
                    bank_ev6[4 * grp + tc] = cnts["p6v"] + tc + 1
                t_kv = I("p6v", 4)[1]
                ph.add("vector", fk6v)
                _, t_kd = I("qs", 16)

                def fk6s(sy, mo=mo, t_kv=t_kv):
                    sy.wait_ge(SEM("p6v"), t_kv)
                    sy.dma_start(out=d_exi[:, mo * T:(mo + 1) * T],
                                 in_=kst6[:, (mo % 2) * 2048:(mo % 2 + 1) * 2048]).then_inc(qs, 16)
                ph.add("sync", fk6s)
                kst_dma[mo % 2] = t_kd
            t_k6t = cnts["p6t"]

            gate_wv6 = wload6(Wv, t_k6t)
            _, t_vm6 = I("ge")

            def fvm6(g):
                g.memset(vst6[:, 0:1040], 1.0)
                g.memset(vst6[:, 1040:2080], 1.0).then_inc(SEM("ge"), 1)
            ph.add("gpsimd", fvm6)
            vst_dma = {0: 0, 1: 0}
            for j in range(16):
                gates = [bank_ev6[(2 * j + half) % 4] for half in range(2)]
                _, t_mm = I("p6t")

                def fv6(t, j=j, gates=gates, gate_w=gate_wv6, t_vm6=t_vm6):
                    if j == 0:
                        t.wait_ge(qg, gate_w)
                        t.wait_ge(SEM("ge"), t_vm6)
                    gmax = max(gates)
                    if gmax:
                        t.wait_ge(SEM("p6v"), gmax)
                    for half in range(2):
                        ps = PS6[(2 * j + half) % 4]
                        for ki in range(8):
                            t.matmul(ps[:, 0:512],
                                     fmA[:, ki * T + j * 128: ki * T + (j + 1) * 128],
                                     wL1[:, ki * 1024 + half * 512: ki * 1024 + (half + 1) * 512],
                                     start=(ki == 0), stop=(ki == 7))
                    t.drain().then_inc(SEM("p6t"), 1)
                ph.add("tensor", fv6)

                def fv6v(v, j=j, t_mm=t_mm, dgate=vst_dma[j % 2]):
                    v.wait_ge(SEM("p6t"), t_mm)
                    if dgate:
                        v.wait_ge(qs, dgate)
                    for half in range(2):
                        ps = PS6[(2 * j + half) % 4]
                        base = (j % 2) * 1040 + half * 520
                        ic = v.tensor_copy(
                            vst6.ap()[:, base:base + 520].rearrange("p (h c) -> p h c", c=65)[:, :, 0:64],
                            ps.ap()[:, 0:512].rearrange("p (h c) -> p h c", c=64))
                        ic.then_inc(SEM("p6v"), 1)
                for half in range(2):
                    bank_ev6[(2 * j + half) % 4] = cnts["p6v"] + half + 1
                t_vv = I("p6v", 2)[1]
                ph.add("vector", fv6v)
                _, t_vd = I("qs", 16)

                def fv6s(sy, j=j, t_vv=t_vv):
                    sy.wait_ge(SEM("p6v"), t_vv)
                    sy.dma_start(out=d_exi[:, EXK + j * 1040: EXK + (j + 1) * 1040],
                                 in_=vst6[:, (j % 2) * 1040:(j % 2 + 1) * 1040]).then_inc(qs, 16)
                ph.add("sync", fv6s)
                vst_dma[j % 2] = t_vd
            t_all_st = cnts["qs"]
            _, t_pad = I("qg", 16)

            def fpad(g):
                g.memset(scb[:, 0:8], 0.0)
                g.dma_start(out=d_exi[:, EXK + EXV:EXW], in_=scb[:, 0:8]).then_inc(qg, 16)
            ph.add("gpsimd", fpad)
            _, t_cc6 = I("cc6")

            def fex(g, t_all_st=t_all_st, t_pad=t_pad):
                g.wait_ge(qs, t_all_st)
                g.wait_ge(qg, t_pad)
                g.collective_compute("AllGather", ALU.bypass, replica_groups=pairs,
                                     ins=[d_exi], outs=[d_exo]).then_inc(SEM("cc6"), 1)
            ph.add("gpsimd", fex)
            ph.emit(); CKPT()
            es_ps6a.close()
            es_p6a.close()
            es_xf.close()
            es_fm1.close()

            es_kv = ES()
            kful = es_kv.enter_context(nc.sbuf_tensor("kful", [128, 8 * S], BF16))
            vful = es_kv.enter_context(nc.sbuf_tensor("vful", [128, NKT * 1040], BF16))
            es_ps6b = ES()
            ps6s = [es_ps6b.enter_context(nc.psum_tensor("ps6s%d" % i, [128, 512], F32))
                    for i in range(4)]
            psv6 = es_ps6b.enter_context(nc.psum_tensor("psv6", [65, 16], F32))

            ph = Ph()
            _, t_asm = I("qg", 16 * (128 + 32))

            def fasm(g, t_cc6=t_cc6):
                g.wait_ge(SEM("cc6"), t_cc6)
                for ki in range(8):
                    for gb in range(16):
                        o = gb & 1
                        i_own = gb >> 1
                        g.dma_start(
                            out=kful[:, ki * S + gb * 256: ki * S + (gb + 1) * 256],
                            in_=d_exo[o * 128:(o + 1) * 128,
                                      ki * T + i_own * 256: ki * T + (i_own + 1) * 256]
                        ).then_inc(qg, 16)
                for kt in range(NKT):
                    o = (kt >> 1) & 1
                    j = ((kt >> 2) << 1) | (kt & 1)
                    g.dma_start(
                        out=vful[:, kt * 1040:(kt + 1) * 1040],
                        in_=d_exo[o * 128:(o + 1) * 128, EXK + j * 1040: EXK + (j + 1) * 1040]
                    ).then_inc(qg, 16)
            ph.add("gpsimd", fasm)

            SEM("p6bt"); SEM("p6bv")
            for kt in range(NKT):
                gate_ev = cnts["p6bv"] - 1 if kt >= 2 else 0
                _, t_mm = I("p6bt")

                def fvt6(t, kt=kt, t_asm=t_asm, gate_ev=gate_ev):
                    if kt == 0:
                        t.wait_ge(qg, t_asm)
                    if gate_ev:
                        t.wait_ge(SEM("p6bv"), gate_ev)
                    for h in range(H):
                        t.matmul(psv6[:, h:h + 1],
                                 vful[:, kt * 1040 + h * 65: kt * 1040 + h * 65 + 65],
                                 onescb[:], start=True, stop=True)
                    t.drain().then_inc(SEM("p6bt"), 1)
                ph.add("tensor", fvt6)
                _, t_ev = I("p6bv")

                def fvt6v(v, kt=kt, t_mm=t_mm):
                    v.wait_ge(SEM("p6bt"), t_mm)
                    ic = v.tensor_copy(
                        vtsb.ap()[0:65, :].rearrange("p (h k) -> p h k", k=NKT)[:, :, kt:kt + 1],
                        psv6.ap()[0:65, :].broadcast_to([65, 16, 1]))
                    ic.then_inc(SEM("p6bv"), 1)
                ph.add("vector", fvt6v)
            _, t_vs = I("p6bv")

            def fvs6(v, need=cnts["p6bv"] - 1):
                v.wait_ge(SEM("p6bv"), need)
                v.tensor_reduce(vsumb[0:65, 0:16],
                                vtsb.ap()[0:65, :].rearrange("p (h k) -> p h k", k=NKT),
                                AX.X, ALU.add)
                for t_ in range(NT):
                    kb = band_kbase(t_) // 128
                    v.tensor_reduce(scr[0:65, t_ * 16:(t_ + 1) * 16],
                                    vtsb.ap()[0:65, :].rearrange("p (h k) -> p h k", k=NKT)[:, :, kb:kb + 5],
                                    AX.X, ALU.add)
                for t_ in range(NT):
                    v.tensor_sub(cbuf[0:65, t_ * 16:(t_ + 1) * 16], vsumb[0:65, 0:16],
                                 scr[0:65, t_ * 16:(t_ + 1) * 16])
                v.engine_nop().then_inc(SEM("p6bv"), 1)
            ph.add("vector", fvs6)

            bank_red = {0: 0, 1: 0, 2: 0, 3: 0}
            for t_ in range(NT):
                for h in range(H):
                    ch, ro = h >> 1, 64 * (h & 1)
                    red_ticks = []
                    for kc in range(8):
                        pp = kc % 4
                        gate_red = bank_red[pp]
                        _, t_mm = I("p6bt")

                        def frm_t(t, t_=t_, h=h, ch=ch, ro=ro, kc=kc, pp=pp, gate_red=gate_red):
                            if gate_red:
                                t.wait_ge(SEM("p6bv"), gate_red)
                            t.matmul(ps6s[pp][:, 0:512],
                                     qfm[ro:ro + 64, ch * T + t_ * 128: ch * T + (t_ + 1) * 128],
                                     kful[ro:ro + 64, ch * S + kc * 512: ch * S + (kc + 1) * 512],
                                     start=True, stop=True)
                            t.drain().then_inc(SEM("p6bt"), 1)
                        ph.add("tensor", frm_t)
                        _, t_red = I("p6bv")

                        def frm_v(v, kc=kc, pp=pp, t_mm=t_mm):
                            v.wait_ge(SEM("p6bt"), t_mm)
                            v.reduce_max(scr[:, kc:kc + 1], ps6s[pp][:, 0:512],
                                         AX.X).then_inc(SEM("p6bv"), 1)
                        ph.add("vector", frm_v)
                        red_ticks.append(cnts["p6bv"])
                        bank_red[pp] = cnts["p6bv"]
                    _, t_fin = I("p6bv")

                    def frm_f(v, t_=t_, h=h, need=red_ticks[-1]):
                        v.wait_ge(SEM("p6bv"), need)
                        v.reduce_max(mbuf[:, t_ * 16 + h: t_ * 16 + h + 1], scr[:, 0:8],
                                     AX.X).then_inc(SEM("p6bv"), 1)
                    ph.add("vector", frm_f)
            _, t_nm8 = I("p6bv")

            def fnm8(s_, need=cnts["p6bv"] - 1):
                s_.wait_ge(SEM("p6bv"), need)
                s_.activation(nm8[:], mbuf[:], AF.Copy, bias=0.0,
                              scale=-0.125).then_inc(SEM("p6bv"), 1)
            ph.add("scalar", fnm8)
            ph.emit(); CKPT()
            es_ps6b.close()

            es_ps6c = ES()
            bmr = es_ps6c.enter_context(nc.sbuf_tensor("bmr", [128, 2 * BW], BF16))
            ps6b = [es_ps6c.enter_context(nc.psum_tensor("ps6b%d" % i, [128, BW], F32))
                    for i in range(2)]
            psc6 = [es_ps6c.enter_context(nc.psum_tensor("psc6%d" % i, [65, 128], F32))
                    for i in range(2)]
            pzb6 = es_ps6c.enter_context(nc.psum_tensor("pzb6", [64, 128], F32))
            pst6 = es_ps6c.enter_context(nc.psum_tensor("pst6", [128, BW], BF16))

            ph = Ph()
            SEM("c6t"); SEM("c6e"); SEM("c6m"); SEM("c6f"); SEM("c6pt"); SEM("c6tc")
            SEM("c6a"); SEM("c6z"); SEM("c6zb"); SEM("c6ev")
            NIT = NT * H
            tick_t = [0] * NIT; tick_e1 = [0] * NIT; tick_m = [0] * NIT
            tick_f = [0] * NIT; tick_pt = [0] * NIT; tick_tc = [0] * NIT
            tick_a = [0] * NIT; tick_z = [0] * NIT; tick_zb = [0] * NIT
            tick_ev = [0] * NIT; tick_st = [0] * NIT
            tick_bm = [0] * NT
            mask_last = [0] * NT

            def mk_bm(t_):
                gate = mask_last[t_ - 2] if t_ >= 2 else 0
                _, tk = I("qg", 16)

                def f(g, t_=t_, gate=gate):
                    if gate:
                        g.wait_ge(SEM("c6m"), gate)
                    g.dma_start(out=bmr[:, (t_ % 2) * BW:(t_ % 2 + 1) * BW],
                                in_=bmask[t_, :].rearrange("(p q) -> p q", p=128)
                                ).then_inc(qg, 16)
                ph.add("gpsimd", f)
                tick_bm[t_] = tk

            def mk_t(i):
                t_, h = divmod(i, H)
                ch, ro = h >> 1, 64 * (h & 1)
                pp = i % 2
                kb = band_kbase(t_)
                gate = tick_e1[i - 2] if i >= 2 else 0
                _, tk = I("c6t")

                def f(t, t_=t_, ch=ch, ro=ro, pp=pp, kb=kb, gate=gate):
                    if gate:
                        t.wait_ge(SEM("c6e"), gate)
                    t.matmul(ps6b[pp][:, 0:512],
                             qfm[ro:ro + 64, ch * T + t_ * 128: ch * T + (t_ + 1) * 128],
                             kful[ro:ro + 64, ch * S + kb: ch * S + kb + 512],
                             start=True, stop=True)
                    t.matmul(ps6b[pp][:, 512:BW],
                             qfm[ro:ro + 64, ch * T + t_ * 128: ch * T + (t_ + 1) * 128],
                             kful[ro:ro + 64, ch * S + kb + 512: ch * S + kb + BW],
                             start=True, stop=True)
                    t.drain().then_inc(SEM("c6t"), 1)
                ph.add("tensor", f)
                tick_t[i] = tk

            def mk_e1(i):
                t_, h = divmod(i, H)
                pp = i % 2
                gate = tick_f[i - 2] if i >= 2 else 0
                _, tk = I("c6e")

                def f(s_, t_=t_, h=h, pp=pp, gate=gate, tmm=tick_t[i]):
                    s_.wait_ge(SEM("c6t"), tmm)
                    if gate:
                        s_.wait_ge(SEM("c6f"), gate)
                    s_.activation(scb[:, pp * BW:(pp + 1) * BW], ps6b[pp][:, 0:BW],
                                  AF.Exp, bias=nm8[:, t_ * 16 + h: t_ * 16 + h + 1],
                                  scale=0.125).then_inc(SEM("c6e"), 1)
                ph.add("scalar", f)
                tick_e1[i] = tk

            def mk_m(i):
                t_, h = divmod(i, H)
                pp = i % 2
                _, tk = I("c6m")

                def f(v, t_=t_, pp=pp, te=tick_e1[i], tbm=tick_bm[t_]):
                    v.wait_ge(SEM("c6e"), te)
                    v.wait_ge(qg, tbm)
                    v.tensor_tensor(scb[:, pp * BW:(pp + 1) * BW],
                                    scb[:, pp * BW:(pp + 1) * BW],
                                    bmr[:, (t_ % 2) * BW:(t_ % 2 + 1) * BW],
                                    ALU.mult).then_inc(SEM("c6m"), 1)
                ph.add("vector", f)
                tick_m[i] = tk
                mask_last[t_] = tk

            def mk_e2(i):
                pp = i % 2
                gate = tick_pt[i - 2] if i >= 2 else 0
                _, tk = I("c6f")

                def f(s_, pp=pp, tm=tick_m[i], gate=gate, KTR=KTR):
                    s_.wait_ge(SEM("c6m"), tm)
                    if gate:
                        s_.wait_ge(qs if KTR else SEM("c6pt"), gate)
                    s_.activation(scb[:, 1280 + pp * BW: 1280 + (pp + 1) * BW],
                                  scb[:, pp * BW:(pp + 1) * BW],
                                  AF.Exp, bias=consts[:, 3:4],
                                  scale=1.0).then_inc(SEM("c6f"), 1)
                ph.add("scalar", f)
                tick_f[i] = tk

            KTR = int(os.environ.get("KTR", "0"))

            def mk_pt(i):
                if KTR:
                    pp = i % 2
                    gate = tick_a[i - 2] if i >= 2 else 0
                    _, tk = I("qs", 16 * 5)

                    def f(sy, pp=pp, tf=tick_f[i], gate=gate):
                        sy.wait_ge(SEM("c6f"), tf)
                        if gate:
                            sy.wait_ge(SEM("c6a"), gate)
                        for j in range(5):
                            sy.dma_start(
                                out=scb[:, 2560 + pp * BW + j * 128: 2560 + pp * BW + (j + 1) * 128],
                                in_=scb[:, 1280 + pp * BW + j * 128: 1280 + pp * BW + (j + 1) * 128],
                                transpose=True).then_inc(qs, 16)
                    ph.add("sync", f)
                    tick_tc[i] = tk
                    tick_pt[i] = tk
                    return
                pp = i % 2
                gate = tick_tc[i - 1] if i >= 1 else 0
                _, tk = I("c6pt")

                def f(t, pp=pp, tf=tick_f[i], gate=gate):
                    t.wait_ge(SEM("c6f"), tf)
                    if gate:
                        t.wait_ge(SEM("c6tc"), gate)
                    for j in range(5):
                        t.transpose(pst6[:, j * 128:(j + 1) * 128],
                                    scb[:, 1280 + pp * BW + j * 128: 1280 + pp * BW + (j + 1) * 128],
                                    idsb[:])
                    t.drain().then_inc(SEM("c6pt"), 1)
                ph.add("tensor", f)
                tick_pt[i] = tk

            def mk_tc(i):
                if KTR:
                    return
                pp = i % 2
                gate = tick_a[i - 2] if i >= 2 else 0
                _, tk = I("c6tc")

                def f(v, pp=pp, tp=tick_pt[i], gate=gate):
                    v.wait_ge(SEM("c6pt"), tp)
                    if gate:
                        v.wait_ge(SEM("c6a"), gate)
                    v.tensor_copy(scb[:, 2560 + pp * BW: 2560 + (pp + 1) * BW],
                                  pst6[:, 0:BW]).then_inc(SEM("c6tc"), 1)
                ph.add("vector", f)
                tick_tc[i] = tk

            def mk_av(i):
                t_, h = divmod(i, H)
                pp = i % 2
                kb = band_kbase(t_)
                gate = tick_ev[i - 2] if i >= 2 else 0
                _, tk = I("c6a")

                def f(t, h=h, pp=pp, kb=kb, tc_=tick_tc[i], gate=gate, KTR=KTR):
                    if KTR:
                        t.wait_ge(qs, tc_)
                    else:
                        t.wait_ge(SEM("c6tc"), tc_)
                    if gate:
                        t.wait_ge(SEM("c6ev"), gate)
                    for j in range(5):
                        kt = kb // 128 + j
                        t.matmul(psc6[pp][:, 0:128],
                                 vful[:, kt * 1040 + h * 65: kt * 1040 + h * 65 + 65],
                                 scb[:, 2560 + pp * BW + j * 128: 2560 + pp * BW + (j + 1) * 128],
                                 start=(j == 0), stop=(j == 4))
                    t.drain().then_inc(SEM("c6a"), 1)
                ph.add("tensor", f)
                tick_a[i] = tk

            def mk_z(i):
                pp = i % 2
                gate = tick_zb[i - 2] if i >= 2 else 0
                _, tk = I("c6z")

                def f(v, pp=pp, ta=tick_a[i], gate=gate):
                    v.wait_ge(SEM("c6a"), ta)
                    if gate:
                        v.wait_ge(SEM("c6zb"), gate)
                    zs = 32 + pp * 128
                    v.tensor_scalar_add(scr[0:1, zs:zs + 128], psc6[pp][64:65, 0:128],
                                        float(S - BW))
                    v.reciprocal(scr[0:1, zs:zs + 128],
                                 scr[0:1, zs:zs + 128]).then_inc(SEM("c6z"), 1)
                ph.add("vector", f)
                tick_z[i] = tk

            def mk_zt(i):
                pp = i % 2
                gate = tick_ev[i - 1] if i >= 1 else 0
                _, tk = I("c6zb")

                def f(t, pp=pp, tz=tick_z[i], gate=gate):
                    t.wait_ge(SEM("c6z"), tz)
                    if gate:
                        t.wait_ge(SEM("c6ev"), gate)
                    zs = 32 + pp * 128
                    t.matmul(pzb6[:, 0:128], ones1f[0:1, 0:64], scr[0:1, zs:zs + 128],
                             start=True, stop=True)
                    t.drain().then_inc(SEM("c6zb"), 1)
                ph.add("tensor", f)
                tick_zb[i] = tk

            def mk_ev(i):
                t_, h = divmod(i, H)
                pp = i % 2
                gate = tick_st[i - 2] if i >= 2 else 0
                _, tk = I("c6ev")

                def f(v, t_=t_, h=h, pp=pp, tzb=tick_zb[i], gate=gate):
                    v.wait_ge(SEM("c6zb"), tzb)
                    if gate:
                        v.wait_ge(qs, gate)
                    stg = scb[0:64, 3840 + pp * 128: 3840 + (pp + 1) * 128]
                    v.tensor_single_scalar(stg, psc6[pp][0:64, 0:128],
                                           cbuf[0:64, t_ * 16 + h: t_ * 16 + h + 1], ALU.add)
                    v.tensor_tensor(stg, stg, pzb6[0:64, 0:128],
                                    ALU.mult).then_inc(SEM("c6ev"), 1)
                ph.add("vector", f)
                tick_ev[i] = tk

            def mk_st(i):
                t_, h = divmod(i, H)
                pp = i % 2
                _, tk = I("qs", 16)

                def f(sy, t_=t_, h=h, pp=pp, tev=tick_ev[i]):
                    sy.wait_ge(SEM("c6ev"), tev)
                    sy.dma_start(out=d_ctx[h * 64:(h + 1) * 64, t_ * 128:(t_ + 1) * 128],
                                 in_=scb[0:64, 3840 + pp * 128: 3840 + (pp + 1) * 128]
                                 ).then_inc(qs, 16)
                ph.add("sync", f)
                tick_st[i] = tk

            KST = int(os.environ.get("KST", "2"))
            if KST == 2:
                for i in range(NIT):
                    t_, h = divmod(i, H)
                    if h == 0:
                        mk_bm(t_)
                    mk_t(i)
                    mk_e1(i)
                    mk_m(i)
                    if i >= 1:
                        j = i - 1
                        mk_e2(j); mk_pt(j); mk_tc(j); mk_av(j); mk_z(j); mk_zt(j)
                        mk_ev(j); mk_st(j)
                j = NIT - 1
                mk_e2(j); mk_pt(j); mk_tc(j); mk_av(j); mk_z(j); mk_zt(j)
                mk_ev(j); mk_st(j)
            elif KST:
                for i in range(NIT):
                    t_, h = divmod(i, H)
                    if h == 0:
                        mk_bm(t_)
                    mk_t(i); mk_e1(i); mk_m(i)
                    mk_e2(i); mk_pt(i); mk_tc(i); mk_av(i); mk_z(i); mk_zt(i)
                    mk_ev(i); mk_st(i)
            else:
                for i in range(NIT):
                    t_, h = divmod(i, H)
                    if h == 0:
                        mk_bm(t_)
                    mk_t(i)
                    mk_e1(i)
                    mk_m(i)
                    if i >= 1:
                        j = i - 1
                        mk_e2(j); mk_pt(j); mk_tc(j); mk_av(j); mk_z(j); mk_zt(j)
                    if i >= 2:
                        k = i - 2
                        mk_ev(k); mk_st(k)
                j = NIT - 1
                mk_e2(j); mk_pt(j); mk_tc(j); mk_av(j); mk_z(j); mk_zt(j)
                for k in (NIT - 2, NIT - 1):
                    mk_ev(k); mk_st(k)
            ph.emit(); CKPT()
            es_ps6c.close()
            es_kv.close()
            es_qfm.close()

            es_xf2 = ES()
            xf2 = es_xf2.enter_context(nc.sbuf_tensor("xf2", [128, NT * D], F32))
            es_p6d = ES()
            wo6 = es_p6d.enter_context(nc.sbuf_tensor("wo6", [128, 8 * 1024], BF16))
            ctxl6 = es_p6d.enter_context(nc.sbuf_tensor("ctxl6", [128, 2 * 1024], BF16))
            es_ps6d = ES()
            ps6d = [es_ps6d.enter_context(nc.psum_tensor("ps6d%d" % i, [128, 1024], F32))
                    for i in range(2)]

            def resid_x2(g, ti, dst):
                g.dma_start(out=dst, in_=d_x2[:, ti * D:(ti + 1) * D]).then_inc(qg, 16)

            ph = phase_proj_ln(Wo, zrow, d_ctx, resid_x2, None, "p6d",
                               wo6, ctxl6, ps6d, None, xf2, do_tr=False)
            ph.emit(); CKPT()
            es_ps6d.close()
            es_p6d.close()

            es_p7 = ES()
            alloc_quant(es_p7, "x")
            es_xqf = ES()
            xqf = es_xqf.enter_context(nc.sbuf_tensor("xqf", [128, 8 * T], BF16, side="right"))
            es_wgt = ES()
            wgt6 = es_wgt.enter_context(nc.sbuf_tensor("wgt6", [128, 2 * 2 * 8 * 128], BF16))
            es_ps7 = ES()
            PS7 = [es_ps7.enter_context(nc.psum_tensor("ps7_%d" % i, [128, 512], F32))
                   for i in range(4)]

            ph = Ph()
            SEM("x7")
            xq_tr = {0: 0, 1: 0}
            for ti in range(NT):
                pp = ti % 2
                (s3t3, _) = sched_quant(
                    ph,
                    (lambda ti=ti: xf2.ap()[:, ti * D:(ti + 1) * D]),
                    (lambda pp=pp: scb.ap()[:, pp * 1024:(pp + 1) * 1024]),
                    D, 128, None, "xq",
                    pre_wait=((qs, xq_tr[pp]) if xq_tr[pp] else None))
                _, t_tr = I("qs", 16 * 8)

                def fxtr(sy, ti=ti, pp=pp, s3t3=s3t3):
                    sy.wait_ge(*s3t3)
                    for fc in range(8):
                        sy.dma_start(
                            out=xqf[:, fc * T + ti * 128: fc * T + (ti + 1) * 128],
                            in_=scb[:, pp * 1024 + fc * 128: pp * 1024 + (fc + 1) * 128],
                            transpose=True).then_inc(qs, 16)
                ph.add("sync", fxtr)
                xq_tr[pp] = t_tr
            t_xq = cnts["qs"]

            SEM("g7t"); SEM("g7s"); SEM("g7v")
            hst_dma = {0: 0, 1: 0}
            wl_mm_done = {0: 0, 1: 0}
            for ncol in range(NC22):
                wp = ncol % 2
                gate_mm = wl_mm_done[wp]
                _, t_wl = I("qs", 16 * 16)

                def fwl(g_, ncol=ncol, wp=wp, gate_mm=gate_mm, t_ccw=cnts["ccw"]):
                    if ncol == 0:
                        g_.wait_ge(SEM("ccw"), t_ccw)
                        g_.wait_ge(qs, t_xq)
                    if gate_mm:
                        g_.wait_ge(SEM("g7t"), gate_mm)
                    for hc in range(8):
                        g_.dma_start(out=wgt6[:, wp * 2048 + hc * 128: wp * 2048 + (hc + 1) * 128],
                                     in_=d_wgo[ncol * 128:(ncol + 1) * 128,
                                               hc * 128:(hc + 1) * 128],
                                     transpose=True).then_inc(qs, 16)
                    for hc in range(8):
                        g_.dma_start(out=wgt6[:, wp * 2048 + 1024 + hc * 128: wp * 2048 + 1024 + (hc + 1) * 128],
                                     in_=d_wuo[ncol * 128:(ncol + 1) * 128,
                                               hc * 128:(hc + 1) * 128],
                                     transpose=True).then_inc(qs, 16)
                ph.add("sync", fwl)
                for tc in range(4):
                    n = ncol * 4 + tc
                    gate_ev = cnts["g7v"] - 1 if n >= 2 else 0
                    _, t_mm = I("g7t")

                    def fgu(t, ncol=ncol, tc=tc, t_wl=t_wl, gate_ev=gate_ev, pp=tc % 2,
                            wp=ncol % 2):
                        t.wait_ge(qs, t_wl)
                        if gate_ev:
                            t.wait_ge(SEM("g7v"), gate_ev)
                        for hc in range(8):
                            t.matmul(PS7[2 * pp][:, 0:512],
                                     wgt6[:, wp * 2048 + hc * 128: wp * 2048 + (hc + 1) * 128],
                                     xqf[:, hc * T + tc * 512:(hc) * T + (tc + 1) * 512],
                                     start=(hc == 0), stop=(hc == 7))
                        for hc in range(8):
                            t.matmul(PS7[2 * pp + 1][:, 0:512],
                                     wgt6[:, wp * 2048 + 1024 + hc * 128: wp * 2048 + 1024 + (hc + 1) * 128],
                                     xqf[:, hc * T + tc * 512:(hc) * T + (tc + 1) * 512],
                                     start=(hc == 0), stop=(hc == 7))
                        t.drain().then_inc(SEM("g7t"), 1)
                    ph.add("tensor", fgu)
                    wl_mm_done[ncol % 2] = cnts["g7t"]
                    _, t_sil = I("g7s")

                    def fsil(s_, t_mm=t_mm, pp=tc % 2):
                        s_.wait_ge(SEM("g7t"), t_mm)
                        s_.activation(scb[:, 2048 + pp * 512: 2048 + (pp + 1) * 512],
                                      PS7[2 * pp][:, 0:512], AF.Silu,
                                      bias=consts[:, 3:4], scale=1.0).then_inc(SEM("g7s"), 1)
                    ph.add("scalar", fsil)
                    _, t_hv = I("g7v")

                    def fhv(v, t_sil=t_sil, pp=tc % 2, dg=hst_dma[tc % 2]):
                        v.wait_ge(SEM("g7s"), t_sil)
                        if dg:
                            v.wait_ge(qs, dg)
                        v.tensor_tensor(scb[:, 3072 + pp * 512: 3072 + (pp + 1) * 512],
                                        scb[:, 2048 + pp * 512: 2048 + (pp + 1) * 512],
                                        PS7[2 * pp + 1][:, 0:512],
                                        ALU.mult).then_inc(SEM("g7v"), 1)
                    ph.add("vector", fhv)
                    _, t_hd = I("qs", 16)

                    def fhd(sy, ncol=ncol, tc=tc, t_hv=t_hv, pp=tc % 2):
                        sy.wait_ge(SEM("g7v"), t_hv)
                        sy.dma_start(out=d_h[:, ncol * T + tc * 512: ncol * T + (tc + 1) * 512],
                                     in_=scb[:, 3072 + pp * 512: 3072 + (pp + 1) * 512]
                                     ).then_inc(qs, 16)
                    ph.add("sync", fhd)
                    hst_dma[tc % 2] = t_hd
            ph.emit(); CKPT()
            es_ps7.close()
            es_xqf.close()
            es_wgt.close()

            es_wd = ES()
            wdT = es_wd.enter_context(nc.sbuf_tensor("wdT", [128, NC22 * 1024], BF16))
            hqf = es_wd.enter_context(nc.sbuf_tensor("hqf", [128, 2 * NC22 * 128], BF16))
            hbuf = es_wd.enter_context(nc.sbuf_tensor("hbuf", [128, 4 * DI], BF16))
            es_ps7d = ES()
            ps7d = [es_ps7d.enter_context(nc.psum_tensor("ps7d%d" % i, [128, 1024], F32))
                    for i in range(2)]
            pst7 = es_ps7d.enter_context(nc.psum_tensor("pst7", [128, DI], BF16))

            ph = Ph()
            if DBG:
                _, _t = I("qg", 16 * 4)

                def fdbg(g):
                    g.dma_start(out=o_x1, in_=d_x1).then_inc(qg, 16)
                    g.dma_start(out=o_x2, in_=d_x2).then_inc(qg, 16)
                    g.dma_start(out=o_ctx, in_=d_ctx).then_inc(qg, 16)
                    g.dma_start(out=o_x3[:, 0:NT * D], in_=xf2[:, 0:NT * D]).then_inc(qg, 16)
                ph.add("gpsimd", fdbg)
            qx = SEM("qx")
            _, t_wdl_s = I("qs", 16 * 11 * 8)

            def fwdl_s(sy):
                for ic in range(11):
                    for nch in range(8):
                        sy.dma_start(out=wdT[:, ic * 1024 + nch * 128: ic * 1024 + (nch + 1) * 128],
                                     in_=d_wdo[nch * 128:(nch + 1) * 128,
                                               ic * 128:(ic + 1) * 128],
                                     transpose=True).then_inc(qs, 16)
            ph.add("sync", fwdl_s)
            _, t_wdl_x = I("qx", 16 * 11 * 8)

            def fwdl_x(s_):
                for ic in range(11, NC22):
                    for nch in range(8):
                        s_.dma_start(out=wdT[:, ic * 1024 + nch * 128: ic * 1024 + (nch + 1) * 128],
                                     in_=d_wdo[nch * 128:(nch + 1) * 128,
                                               ic * 128:(ic + 1) * 128],
                                     transpose=True).then_inc(qx, 16)
            ph.add("scalar", fwdl_x)

            SEM("d7t"); SEM("hq1"); SEM("c7pt"); SEM("c7tc")
            ln_a7 = {0: 0, 1: 0}
            tk_v1 = [0] * NT
            tk_v3 = [(None, 0)] * NT
            tk_qt = [0] * NT
            tk_hc = [0] * NT
            tk_dn = [0] * NT
            out_ticks = []

            def mk_hl(ti):
                pp = ti % 2
                gate = tk_v1[ti - 2] if ti >= 2 else 0
                _, tk = I("qs", 16 * NC22)

                def f(sy, ti=ti, pp=pp, gate=gate):
                    if gate:
                        sy.wait_ge(SEM("hq1"), gate)
                    for ic in range(NC22):
                        sy.dma_start(out=hbuf[:, pp * 2 * DI + ic * 128: pp * 2 * DI + (ic + 1) * 128],
                                     in_=d_h[:, ic * T + ti * 128: ic * T + (ti + 1) * 128],
                                     transpose=True).then_inc(qs, 16)
                ph.add("sync", f)
                return tk

            def mk_qt(ti):
                pp = ti % 2
                gate = tk_hc[ti - 1] if ti >= 1 else 0
                _, tk = I("c7pt")

                def f(t, ti=ti, pp=pp, gate=gate, s3t3=tk_v3[ti]):
                    t.wait_ge(*s3t3)
                    if gate:
                        t.wait_ge(SEM("c7tc"), gate)
                    for ic in range(NC22):
                        t.transpose(pst7[:, ic * 128:(ic + 1) * 128],
                                    hbuf[:, pp * 2 * DI + DI + ic * 128: pp * 2 * DI + DI + (ic + 1) * 128],
                                    idsb[:])
                    t.drain().then_inc(SEM("c7pt"), 1)
                ph.add("tensor", f)
                tk_qt[ti] = tk

            def mk_hc(ti):
                pp = ti % 2
                gate = tk_dn[ti - 2] if ti >= 2 else 0
                _, tk = I("c7tc")

                def f(v, pp=pp, tq=tk_qt[ti], gate=gate):
                    v.wait_ge(SEM("c7pt"), tq)
                    if gate:
                        v.wait_ge(SEM("d7t"), gate)
                    v.tensor_copy(hqf[:, pp * NC22 * 128:(pp + 1) * NC22 * 128],
                                  pst7[:, 0:DI]).then_inc(SEM("c7tc"), 1)
                ph.add("vector", f)
                tk_hc[ti] = tk

            def mk_dn(ti):
                pp = ti % 2
                gate_ps = ln_a7[pp]
                _, tk = I("d7t")

                def f(t, ti=ti, pp=pp, thc=tk_hc[ti], gate_ps=gate_ps):
                    t.wait_ge(SEM("c7tc"), thc)
                    if ti == 0:
                        t.wait_ge(qs, t_wdl_s)
                        t.wait_ge(qx, t_wdl_x)
                    if gate_ps:
                        t.wait_ge(SEM("p7a"), gate_ps)
                    for ic in range(NC22):
                        for nh in range(2):
                            t.matmul(ps7d[pp][:, nh * 512:(nh + 1) * 512],
                                     hqf[:, pp * NC22 * 128 + ic * 128: pp * NC22 * 128 + (ic + 1) * 128],
                                     wdT[:, ic * 1024 + nh * 512: ic * 1024 + (nh + 1) * 512],
                                     start=(ic == 0), stop=(ic == NC22 - 1))
                    t.drain().then_inc(SEM("d7t"), 1)
                ph.add("tensor", f)
                tk_dn[ti] = tk

            def mk_ln_out(ti):
                pp = ti % 2
                sc_, tc_ = sched_ln(
                    ph,
                    (lambda pp=pp: ps7d[pp][:, 0:1024]),
                    (lambda ti=ti: xf2[:, ti * D:(ti + 1) * D]),
                    xf2[:, ti * D:(ti + 1) * D],
                    None,
                    None,
                    "p7",
                    extra_wait=(lambda tdn=tk_dn[ti]: (SEM("d7t"), tdn)))
                ln_a7[pp] = cnts["p7a"]
                _, t_out = I("qs", 16)

                def fout(sy, ti=ti, tc_=tc_):
                    sy.wait_ge(SEM("p7c"), tc_)
                    sy.dma_start(out=out_t[ti * 128:(ti + 1) * 128, :],
                                 in_=xf2[:, ti * D:(ti + 1) * D]).then_inc(qs, 16)
                ph.add("sync", fout)
                out_ticks.append(cnts["qs"])

            for ti in range(NT):
                pp = ti % 2
                t_hl = mk_hl(ti)
                qs_snapshot = cnts["qs"]
                pw = ((SEM("c7pt"), tk_qt[ti - 2]) if ti >= 2 else None)
                (s3t3, s1t1) = sched_quant(
                    ph,
                    (lambda ti=ti, pp=pp: hbuf.ap()[:, pp * 2 * DI: pp * 2 * DI + DI]),
                    (lambda ti=ti, pp=pp: hbuf.ap()[:, pp * 2 * DI + DI: pp * 2 * DI + 2 * DI]),
                    DI, 128, (qs, qs_snapshot), "hq", pre_wait=pw)
                tk_v1[ti] = s1t1[1]
                tk_v3[ti] = s3t3
                if ti >= 1:
                    mk_dn(ti - 1)
                    mk_ln_out(ti - 1)
                mk_qt(ti)
                mk_hc(ti)
            mk_dn(NT - 1)
            mk_ln_out(NT - 1)
            ph.emit(); CKPT()
            es_ps7d.close()
            es_wd.close()
            es_p7.close()
            es_xf2.close()

        except StopBuild:
            for e in reversed(es_list):
                try:
                    e.close()
                except Exception:
                    pass
        if phct[0] < 99 and KPH < 99:
            ph = Ph()

            def fzro(g):
                g.memset(scr[:, 0:1024], 0.0).then_inc(SEM("ge"), 1)
            tz = I("ge")[1]
            ph.add("gpsimd", fzro)

            def fdump(sy, tz=tz):
                sy.wait_ge(SEM("ge"), tz)
                for ti in range(NT):
                    sy.dma_start(out=out_t[ti * 128:(ti + 1) * 128, :],
                                 in_=scr[:, 0:1024]).then_inc(qs, 16)
            I("qs", 16 * NT)
            ph.add("sync", fdump)
            ph.emit()
    return nc


_NC_CACHE = {}


def _masks_for_core(par):
    cm = np.zeros((NBLK * 4, 128, 256), dtype=np.float32)
    for bi in range(NBLK):
        g = 2 * bi + par
        for j in range(4):
            kt = 4 * bi + j
            ii = kt * 128 + np.arange(128)[:, None]
            jj = g * 256 + np.arange(256)[None, :]
            cm[bi * 4 + j] = (ii <= jj).astype(np.float32)
    bm = np.zeros((NT, 128, BW), dtype=np.float32)
    for t_ in range(NT):
        gq = 4 * (t_ >> 1) + 2 * par + (t_ & 1)
        kb = band_kbase(t_)
        ii = gq * 128 + np.arange(128)[:, None]
        jj = kb + np.arange(BW)[None, :]
        d = jj - ii
        bm[t_] = ((d >= -128) & (d <= 127) & (jj < S)).astype(np.float32)
    bf = ml_dtypes.bfloat16
    return (cm.reshape(NBLK * 4, 128 * 256).astype(bf),
            bm.reshape(NT, 128 * BW).astype(bf))


def kernel(**inputs):
    ins = {k: np.asarray(v) for k, v in inputs.items()}
    x0 = ins["L0_attn_x"]
    if "nc" not in _NC_CACHE:
        _NC_CACHE["nc"] = build()
    nc = _NC_CACHE["nc"]
    in_maps = []
    for c in range(8):
        b, par = c // 2, c % 2
        own_rows = np.concatenate(
            [np.arange((2 * i + par) * 256, (2 * i + par) * 256 + 256) for i in range(8)])
        cm, bm = _masks_for_core(par)
        m = {
            "x0_own": np.ascontiguousarray(x0[b][own_rows]),
            "x0_full": np.ascontiguousarray(x0[b]),
            "w_qkv": ins["L0_attn_w_qkv"], "b_qkv": ins["L0_attn_b_qkv"],
            "w_out": ins["L0_attn_w_out"], "b_out": ins["L0_attn_b_out"],
            "cn_w1": ins["L0_ffn_cn_w1"], "cn_b1": ins["L0_ffn_cn_b1"],
            "cn_w2": ins["L0_ffn_cn_w2"], "cn_b2": ins["L0_ffn_cn_b2"],
            "cn_w3": ins["L0_ffn_cn_w3"], "cn_b3": ins["L0_ffn_cn_b3"],
            "cn_w4": ins["L0_ffn_cn_w4"], "cn_b4": ins["L0_ffn_cn_b4"],
            "gc_w1": ins["L0_ffn_gc_w1"], "gc_b1": ins["L0_ffn_gc_b1"],
            "gc_w2": ins["L0_ffn_gc_w2"], "gc_b2": ins["L0_ffn_gc_b2"],
            "gc_w3": ins["L0_ffn_gc_w3"], "gc_b3": ins["L0_ffn_gc_b3"],
            "ar": ins["L0_ffn_adaptive_range"],
            "Wq": ins["L1_attn_Wq"], "Wk": ins["L1_attn_Wk"],
            "Wv": ins["L1_attn_Wv"], "Wo": ins["L1_attn_Wo"],
            "wg_sl": np.ascontiguousarray(ins["L1_ffn_w_gate"][c * 352:(c + 1) * 352]),
            "wu_sl": np.ascontiguousarray(ins["L1_ffn_w_up"][c * 352:(c + 1) * 352]),
            "wd_sl": np.ascontiguousarray(ins["L1_ffn_w_down"][c * 128:(c + 1) * 128]),
            "cmask": cm, "bmask": bm,
        "identb": np.eye(128, dtype=ml_dtypes.bfloat16),
        }
        in_maps.append(m)
    res = run_bass_kernel_spmd(nc, in_maps, core_ids=list(range(8)))
    hook = globals().get("_DBG_HOOK")
    if hook is not None:
        hook(res)
    out = np.zeros((B, S, D), dtype=np.float32)
    for c in range(8):
        b, par = c // 2, c % 2
        o = res.results[c]["out"]
        for i in range(8):
            g = 2 * i + par
            out[b, g * 256:(g + 1) * 256] = o[i * 256:(i + 1) * 256]
    return out
